# version 02
# baseline (speedup 1.0000x reference)
import numpy as np
import ml_dtypes
from contextlib import ExitStack

import concourse.bass as bass
import concourse.mybir as mybir
from concourse.bass_utils import run_bass_kernel_spmd

F32 = mybir.dt.float32
BF16 = mybir.dt.bfloat16
AF = mybir.ActivationFunctionType
ALU = mybir.AluOpType
AX = mybir.AxisListType
BF = ml_dtypes.bfloat16

T = 2048
DM = 2048
HK = 16
DK = 128
VE = 256
CH = 64
NCH = T // CH
E = 8
NT = T // 128
NKD = DM // 128
NVT = 4096 // 128
QD = 4
QT = T // QD


def _tile_w(w, nm):
    d = w.shape[0]
    nkd = d // 128
    return np.ascontiguousarray(
        w.reshape(nkd, 128, nm, 128).transpose(2, 1, 0, 3).reshape(nm, 128, nkd * 128))


def host_prep(inp):
    g = {}
    f32 = np.float32
    g["wq"] = _tile_w(np.asarray(inp["L0_attn_Wq"], f32), 16)
    g["wk"] = _tile_w(np.asarray(inp["L0_attn_Wk"], f32), 16)
    g["wv"] = _tile_w(np.asarray(inp["L0_attn_Wv"], f32), 32)
    for nm, key in (("wa", "L0_attn_Wa"), ("wb", "L0_attn_Wb")):
        w = np.asarray(inp[key], f32)
        g[nm] = np.ascontiguousarray(
            w.reshape(NKD, 128, 16).transpose(1, 0, 2).reshape(128, NKD * 16))
    for nm, wkey, bkey, ntile in (("qc", "L0_attn_qcw", "L0_attn_qcb", 16),
                                  ("kc", "L0_attn_kcw", "L0_attn_kcb", 16),
                                  ("vc", "L0_attn_vcw", "L0_attn_vcb", 32)):
        w = np.asarray(inp[wkey], f32).reshape(ntile, 128, 4).transpose(1, 0, 2)
        g[nm + "w"] = np.ascontiguousarray(w.reshape(128, ntile * 4))
        g[nm + "b"] = np.ascontiguousarray(np.asarray(inp[bkey], f32).reshape(ntile, 128).T)
    gw = np.asarray(inp["L0_ffn_gate_w"], f32).T
    g["gw"] = np.ascontiguousarray(gw.reshape(NKD, 128, 8).transpose(1, 0, 2).reshape(128, NKD * 8))
    se = np.asarray(inp["L0_ffn_se_gate_w"], f32).T
    g["se"] = np.ascontiguousarray(se.reshape(NKD, 128, 1).transpose(1, 0, 2).reshape(128, NKD))
    wg = np.asarray(inp["L0_ffn_w_gate"], f32)
    wu = np.asarray(inp["L0_ffn_w_up"], f32)
    g["wg"] = np.stack([_tile_w(wg[e].T, 8) for e in range(E)])
    g["wu"] = np.stack([_tile_w(wu[e].T, 8) for e in range(E)])
    g["shg"] = _tile_w(np.asarray(inp["L0_ffn_sh_gate"], f32).T, 16)
    g["shu"] = _tile_w(np.asarray(inp["L0_ffn_sh_up"], f32).T, 16)
    wd = np.asarray(inp["L0_ffn_w_down"], f32)
    g["wd"] = np.ascontiguousarray(
        np.stack([wd[e].T for e in range(E)]).reshape(E * 8, 128, DM))
    g["shd"] = np.ascontiguousarray(np.asarray(inp["L0_ffn_sh_down"], f32).T.reshape(16, 128, DM))
    g["wo"] = np.ascontiguousarray(np.asarray(inp["L0_attn_Wo"], f32).reshape(32, 128, DM))

    g["ident16"] = np.eye(128, dtype=BF)
    g["ident32"] = np.eye(128, dtype=f32)
    g["ones_col16"] = np.ones((128, 1), dtype=BF)
    g["ones_row16"] = np.ones((1, 128), dtype=BF)
    i8d = np.zeros((128, 8), dtype=BF)
    i8d[0:8] = np.eye(8)
    i8d[64:72] = np.eye(8)
    g["i8dup"] = i8d
    ju = np.arange(64)[:, None]
    iu = np.arange(64)[None, :]
    strict = -(iu > ju).astype(f32)
    incl = (iu >= ju).astype(f32)
    g["mask_strict"] = np.ascontiguousarray(np.tile(strict, (1, 16)))
    g["mask_incl"] = np.ascontiguousarray(np.tile(incl, (1, 16)))
    i64 = np.eye(64, dtype=BF)
    g["idup"] = np.ascontiguousarray(np.tile(i64, (1, 16)))
    ex = np.zeros((128, 512), dtype=BF)
    for h in range(8):
        ex[h, h * 64:(h + 1) * 64] = 1
        ex[64 + h, h * 64:(h + 1) * 64] = 1
    g["exp64"] = ex
    g["fconst"] = np.tile(np.array([1e-5, -0.5, -1.0, 0.0], f32), (128, 1))
    return g


class Prog:
    def __init__(self, nc, stack):
        self.nc = nc
        self.stack = stack
        self.sem = {}
        self.cnt = {}

    def sem_new(self, name):
        self.sem[name] = self.stack.enter_context(self.nc.semaphore(name))
        self.cnt[name] = 0

    def inc(self, instr, name, amount=1):
        instr.then_inc(self.sem[name], amount)
        self.cnt[name] += amount
        return self.cnt[name]

    def wait(self, eng, name, val=None):
        eng.wait_ge(self.sem[name], self.cnt[name] if val is None else val)


class _Stop(Exception):
    pass


def build(peek=(), last_phase=None):
    peek = set(peek)
    nc = bass.Bass("TRN2", target_bir_lowering=False)
    stack = ExitStack()
    p = Prog(nc, stack)
    sync, te, dv, ac, gp = nc.sync, nc.tensor, nc.vector, nc.scalar, nc.gpsimd
    LD, SP, TE, DV, AC = "ld", "sp", "te", "dv", "ac"

    def din(name, shape, dtype=F32):
        return nc.dram_tensor(name, list(shape), dtype, kind="ExternalInput").ap()

    x_h = din("x", [T, DM])
    wq_h = din("wq", [16, 128, DM]); wk_h = din("wk", [16, 128, DM]); wv_h = din("wv", [32, 128, DM])
    wa_h = din("wa", [128, NKD * 16]); wb_h = din("wb", [128, NKD * 16])
    qcw_h = din("qcw", [128, 64]); qcb_h = din("qcb", [128, 16])
    kcw_h = din("kcw", [128, 64]); kcb_h = din("kcb", [128, 16])
    vcw_h = din("vcw", [128, 128]); vcb_h = din("vcb", [128, 32])
    gw_h = din("gw", [128, NKD * 8]); se_h = din("se", [128, NKD])
    wg_h = din("wg", [E, 8, 128, DM]); wu_h = din("wu", [E, 8, 128, DM])
    shg_h = din("shg", [16, 128, DM]); shu_h = din("shu", [16, 128, DM])
    wd_h = din("wd", [E * 8, 128, DM]); shd_h = din("shd", [16, 128, DM])
    wo_h = din("wo", [32, 128, DM])
    ident16_h = din("ident16", [128, 128], BF16)
    ident32_h = din("ident32", [128, 128], F32)
    ones_col_h = din("ones_col16", [128, 1], BF16)
    ones_row_h = din("ones_row16", [1, 128], BF16)
    i8_h = din("i8dup", [128, 8], BF16)
    mstr_h = din("mask_strict", [64, 1024], F32)
    mincl_h = din("mask_incl", [64, 1024], F32)
    idup_h = din("idup", [64, 1024], BF16)
    exp64_h = din("exp64", [128, 512], BF16)
    fconst_h = din("fconst", [128, 4], F32)
    out_h = nc.dram_tensor("out", [T, DM], F32, kind="ExternalOutput").ap()

    def scr(name, shape, dtype=BF16):
        return nc.dram_tensor(name, list(shape), dtype, kind="Internal").ap()

    qT_sp = scr("qT_sp", [HK * 128, T])
    kT_sp = scr("kT_sp", [HK * 128, T])
    km_sp = scr("km_sp", [HK * T, DK])
    nbk_sp = scr("nbk_sp", [HK * T, DK])
    bv_sp = scr("bv_sp", [HK * T, VE])
    ht_sp = scr("ht_sp", [E * 8 * 128, T])
    s_sp = scr("s_sp", [16 * 128, T])
    ot_sp = scr("ot_sp", [8 * 32 * 128, 256])
    wd_bf = scr("wd_bf", [E * 8 * 128, DM])
    shd_bf = scr("shd_bf", [16 * 128, DM])
    wo_bf = scr("wo_bf", [32 * 128, DM])

    try:
      with stack:
        for s in (LD, SP, TE, DV, AC):
            p.sem_new(s)

        SB = lambda n, sh, dt: stack.enter_context(nc.sbuf_tensor(n, sh, dt))
        AR1 = SB("AR1", [128, 57344], BF16)
        AR2 = SB("AR2", [128, 12288], BF16)
        AR1f = AR1[:].bitcast(F32)
        AR2f = AR2[:].bitcast(F32)

        idn16 = SB("idn16", [128, 128], BF16)
        idn32 = SB("idn32", [128, 128], F32)
        ones_col = SB("ones_col", [128, 1], BF16)
        ones_row = SB("ones_row", [1, 128], BF16)
        i8c = SB("i8c", [128, 8], BF16)
        mstr = SB("mstr", [64, 1024], F32)
        mincl = SB("mincl", [64, 1024], F32)
        idup = SB("idup_c", [64, 1024], BF16)
        exp64 = SB("exp64_c", [128, 512], BF16)
        fcn = SB("fcn", [128, 4], F32)
        convw = SB("convw", [128, 320], F32)
        wab = SB("wab", [128, NKD * 32], BF16)
        gwc = SB("gwc", [128, NKD * 8], BF16)
        sec = SB("sec", [128, NKD], BF16)
        md16lrk = SB("md16lrk", [128, T], BF16)
        md16bt = SB("md16bt", [128, T], BF16)
        tokmd = SB("tokmd", [128, 1280], F32)
        b_tok = tokmd[:, 0:256]
        lrk_tok = tokmd[:, 256:512]
        nbt_tok = tokmd[:, 512:768]
        rk_tok = tokmd[:, 768:1024]
        rq_tok = tokmd[:, 1024:1280]
        lqscal = SB("lqscal", [128, 1024], F32)
        lq_dup = lqscal[:, 0:512]
        scal = lqscal[:, 512:1024]
        sgbc = SB("sgbc", [128, T], BF16)
        combT = SB("combT", [8, T], BF16)
        combbc = SB("combbc", [128, T], BF16)
        WK1 = SB("WK1", [128, DM], BF16)
        WK2 = SB("WK2", [128, DM], BF16)
        WK3 = SB("WK3", [128, DM], BF16)
        colt = SB("colt", [128, 40], F32)
        rowsg = SB("rowsg", [1, T], BF16)
        rowb = rowsg[0:1, 0:1024]

        stgf = [AR2f[:, i * 2048:(i + 1) * 2048] for i in range(2)]
        stg16 = [AR2[:, 8192 + i * 2048:8192 + (i + 1) * 2048] for i in range(2)]
        xin = stgf
        wdn = [AR2[:, i * QT:(i + 1) * QT] for i in range(4)]
        xres = AR2f[:, 1024:1536]
        oute = AR2f[:, 1536:2048]

        XNT = AR1[:, 0:32768]
        R = [AR1f[:, 16384 + i * 2048:16384 + (i + 1) * 2048] for i in range(6)]
        LTv, bTv, TMP2, rkTv, rqTv, TMP5 = R
        sqscr = R[0]
        sgrow = TMP2[0:1, :]

        def a1b(c0, n):
            return AR1[:, c0:c0 + n]

        def a1f(c0, n):
            return AR1f[:, c0 // 2:(c0 + n) // 2]

        qbuf = [a1b(0, 4096), a1b(4096, 4096)]
        kbuf = [a1b(8192, 4096), a1b(12288, 4096)]

        def a1b64(c0, n):
            return AR1[0:64, c0:c0 + n]

        bvb = [a1b64(16384, 4096), a1b64(20480, 4096)]
        kmb = [a1b64(24576, 2048), a1b64(26624, 2048)]
        nbkb = [a1b64(28672, 2048), a1b64(30720, 2048)]
        Us = a1b64(32768, 4096)
        obuf = a1b64(36864, 4096)
        Gts = a1b(40960, 1024)
        ATs = a1b64(41984, 1024)
        Tcb = a1b64(43008, 1024)
        Pcb = a1b64(44032, 1024)
        PTcb = a1b64(45056, 1024)
        lstr = AR1[0:64, 46080:47104]
        lincl = AR1[0:64, 47104:48128]
        bbs = AR1[0:64, 48128:49152]
        SR = a1f(49152, 8192)
        SR16 = AR2[:, 0:4096]
        OT_QT = 256
        OTacc = AR2[:, 4096:12288]
        ACTS = AR1[:, 0:112 * QT]

        PA = stack.enter_context(nc.psum_tensor("PA", [128, 2048], F32))
        PB = stack.enter_context(nc.psum_tensor("PB", [128, 2048], F32))

        def peek_out(tag, ap):
            if tag not in peek:
                return
            t = nc.dram_tensor("peek_" + tag, list(ap.shape), ap.dtype,
                               kind="ExternalOutput").ap()
            p.wait(sync, DV); p.wait(sync, AC); p.wait(sync, TE)
            p.inc(sync.dma_start(t, ap), LD, 16)

        GA = ((0, "A"), (1, "B"))

        def endphase(name):
            if last_phase == name:
                raise _Stop()


        def grows(grp, n=8):
            return slice(grp * 64, grp * 64 + n)

        for dst, src in ((idn16, ident16_h), (idn32, ident32_h), (ones_col, ones_col_h),
                         (ones_row, ones_row_h), (i8c, i8_h), (mstr, mstr_h),
                         (mincl, mincl_h), (idup, idup_h), (exp64, exp64_h),
                         (fcn, fconst_h)):
            p.inc(sync.dma_start(dst[:], src), LD, 16)
        for (c0, c1), src in (((0, 64), qcw_h), ((64, 80), qcb_h), ((80, 144), kcw_h),
                              ((144, 160), kcb_h), ((160, 288), vcw_h), ((288, 320), vcb_h)):
            p.inc(sync.dma_start(convw[:, c0:c1], src), LD, 16)
        p.inc(sync.dma_start(stgf[0][:, 0:256], wa_h), LD, 16)
        p.inc(sync.dma_start(stgf[0][:, 256:512], wb_h), LD, 16)
        p.inc(sync.dma_start(stgf[1][:, 0:128], gw_h), LD, 16)
        p.inc(sync.dma_start(stgf[1][:, 128:144], se_h), LD, 16)
        p.wait(dv, LD)
        p.inc(dv.tensor_copy(wab[:], stgf[0][:, 0:512]), DV)
        p.inc(dv.tensor_copy(gwc[:], stgf[1][:, 0:128]), DV)
        p.inc(dv.tensor_copy(sec[:], stgf[1][:, 128:144]), DV)

        qcw = convw[:, 0:64].rearrange("p (m j) -> p m j", j=4)
        qcb = convw[:, 64:80]
        kcw = convw[:, 80:144].rearrange("p (m j) -> p m j", j=4)
        kcb = convw[:, 144:160]
        vcw = convw[:, 160:288].rearrange("p (m j) -> p m j", j=4)
        vcb = convw[:, 288:320]

        strip = WK3
        PTR = PB[:, 0:512].bitcast(BF16)
        p.wait(sync, DV)
        ld_mark = p.inc(sync.dma_start(xin[0][:], x_h[0:128, :]), LD, 16)
        for tt in range(NT):
            buf = xin[tt % 2]
            my_ld = ld_mark
            if tt + 1 < NT:
                if tt >= 1:
                    p.wait(sync, DV)
                ld_mark = p.inc(
                    sync.dma_start(xin[(tt + 1) % 2][:], x_h[(tt + 1) * 128:(tt + 2) * 128, :]),
                    LD, 16)
            p.wait(dv, LD, my_ld)
            p.inc(dv.reduce_sum(colt[:, 0:1], buf[:], axis=AX.X), DV)
            p.wait(dv, DV)
            p.inc(dv.tensor_scalar_mul(colt[:, 1:2], colt[:, 0:1], 1.0 / DM), DV)
            p.wait(dv, DV)
            p.wait(ac, LD, my_ld)
            p.inc(ac.activation(sqscr[:], buf[:], AF.Square, accum_out=colt[:, 2:3]), AC)
            p.wait(dv, AC)
            p.inc(dv.tensor_scalar_mul(colt[:, 3:4], colt[:, 2:3], 1.0 / DM), DV)
            p.inc(dv.tensor_tensor(colt[:, 4:5], colt[:, 1:2], colt[:, 1:2], op=ALU.mult), DV)
            p.wait(dv, DV)
            p.inc(dv.tensor_tensor(colt[:, 5:6], colt[:, 3:4], colt[:, 4:5], op=ALU.subtract), DV)
            p.wait(ac, DV)
            p.inc(ac.activation(colt[:, 6:7], colt[:, 5:6], AF.Sqrt, bias=fcn[:, 0:1]), AC)
            p.wait(dv, AC)
            p.inc(dv.reciprocal(colt[:, 7:8], colt[:, 6:7]), DV)
            p.wait(dv, DV)
            p.inc(dv.tensor_scalar(strip[:], buf[:], colt[:, 1:2], colt[:, 7:8],
                                   op0=ALU.subtract, op1=ALU.mult), DV)
            p.wait(te, DV)
            for half in range(2):
                for k8 in range(8):
                    kd = half * 8 + k8
                    ins = te.transpose(PTR[:, k8 * 128:(k8 + 1) * 128],
                                       strip[:, kd * 128:(kd + 1) * 128], idn16[:])
                p.inc(ins, TE)
                p.wait(dv, TE)
                dst = XNT.rearrange("p (kd t) -> p kd t", t=T)[
                    :, half * 8:(half + 1) * 8, tt * 128:(tt + 1) * 128]
                p.inc(dv.tensor_copy(dst, PTR.rearrange("p (k c) -> p k c", c=128)), DV)
                if half == 0:
                    p.wait(te, DV)
        peek_out("xnt", XNT[:, 0:4096])

        endphase("P1")
        PAB = PA[0:8, :]
        PABB = PA[64:72, :]
        for which in ("a", "b"):
            for grp, gn in GA:
                pab = PAB if grp == 0 else PABB
                for ng in range(4):
                    for kd in range(NKD):
                        off = (0 if which == "a" else 256) + kd * 16 + grp * 8
                        ins = te.matmul(pab[:, ng * 512:(ng + 1) * 512],
                                        wab[:, off:off + 8],
                                        XNT[:, kd * T + ng * 512:kd * T + (ng + 1) * 512],
                                        start=(kd == 0), stop=(kd == NKD - 1))
                    p.inc(ins, TE)
                p.wait(ac, TE)
                gr = grows(grp)
                if which == "a":
                    p.inc(ac.activation(TMP2[gr, :], pab[:], AF.Sigmoid), AC)
                    p.wait(ac, AC)
                    p.inc(ac.activation(LTv[gr, :], TMP2[gr, :], AF.Ln), AC)
                else:
                    p.inc(ac.activation(bTv[gr, :], pab[:], AF.Sigmoid), AC)
                p.wait(te, AC)
        p.wait(dv, AC)
        for grp, gn in GA:
            lt = LTv[grows(grp), :]
            for c in range(NCH):
                p.inc(dv.tensor_tensor_scan(
                    lt[:, c * CH:(c + 1) * CH], lt[:, c * CH:(c + 1) * CH],
                    lt[:, c * CH:(c + 1) * CH], 0.0, op0=ALU.add, op1=ALU.bypass), DV)
        peek_out("LA", LTv[0:8, 0:T])
        peek_out("bA", bTv[0:8, 0:T])

        endphase("P2")
        PSC = PB[:, 0:16]
        PTRN = PB[0:8, 512:1024]
        for tt in range(NT):
            for kd in range(NKD):
                ins = te.matmul(PSC[:, 0:8], XNT[:, kd * T + tt * 128:kd * T + (tt + 1) * 128],
                                gwc[:, kd * 8:(kd + 1) * 8], start=(kd == 0), stop=(kd == NKD - 1))
            p.inc(ins, TE)
            for kd in range(NKD):
                ins = te.matmul(PSC[:, 8:9], XNT[:, kd * T + tt * 128:kd * T + (tt + 1) * 128],
                                sec[:, kd:kd + 1], start=(kd == 0), stop=(kd == NKD - 1))
            p.inc(ins, TE)
            p.wait(dv, TE)
            c_ = colt
            p.inc(dv.tensor_copy(c_[:, 8:16], PSC[:, 0:8]), DV)
            p.wait(dv, DV)
            p.inc(dv.reduce_max(c_[:, 16:17], c_[:, 8:16], axis=AX.X), DV)
            p.wait(dv, DV)
            p.inc(dv.tensor_scalar(c_[:, 17:25], c_[:, 8:16], c_[:, 16:17], None,
                                   op0=ALU.is_equal), DV)
            p.wait(dv, DV)
            p.inc(dv.scalar_tensor_tensor(c_[:, 25:33], c_[:, 17:25], -1e30, c_[:, 8:16],
                                          op0=ALU.mult, op1=ALU.add), DV)
            p.wait(dv, DV)
            p.inc(dv.reduce_max(c_[:, 33:34], c_[:, 25:33], axis=AX.X), DV)
            p.wait(dv, DV)
            p.inc(dv.tensor_scalar(c_[:, 25:33], c_[:, 25:33], c_[:, 33:34], None,
                                   op0=ALU.is_equal), DV)
            p.inc(dv.tensor_tensor(c_[:, 34:35], c_[:, 16:17], c_[:, 33:34], op=ALU.subtract), DV)
            p.wait(dv, DV)
            p.wait(ac, DV)
            p.inc(ac.activation(c_[:, 35:36], c_[:, 34:35], AF.Sigmoid), AC)
            p.inc(ac.activation(c_[:, 36:37], PSC[:, 8:9], AF.Sigmoid), AC)
            p.wait(dv, AC)
            p.inc(dv.tensor_scalar(c_[:, 37:38], c_[:, 35:36], -1.0, 1.0,
                                   op0=ALU.mult, op1=ALU.add), DV)
            p.inc(dv.tensor_scalar_mul(c_[:, 17:25], c_[:, 17:25], c_[:, 35:36]), DV)
            p.wait(dv, DV)
            p.inc(dv.scalar_tensor_tensor(c_[:, 17:25], c_[:, 25:33], c_[:, 37:38],
                                          c_[:, 17:25], op0=ALU.mult, op1=ALU.add), DV)
            p.wait(te, DV)
            p.inc(te.transpose(PTRN[0:8, 0:128], c_[:, 17:25], idn32[:]), TE)
            p.inc(te.transpose(PTRN[0:1, 128:256], c_[:, 36:37], idn32[:]), TE)
            p.wait(dv, TE)
            p.inc(dv.tensor_copy(combT[:, tt * 128:(tt + 1) * 128], PTRN[0:8, 0:128]), DV)
            p.inc(dv.tensor_copy(sgrow[0:1, tt * 128:(tt + 1) * 128], PTRN[0:1, 128:256]), DV)
            p.wait(te, DV)
        p.inc(dv.tensor_copy(rowsg[0:1, :], sgrow[0:1, :]), DV)
        p.wait(te, DV)
        for ng in range(4):
            ins = te.matmul(PA[:, ng * 512:(ng + 1) * 512], ones_row[0:1, :],
                            rowsg[0:1, ng * 512:(ng + 1) * 512], start=True, stop=True)
        p.inc(ins, TE)
        p.wait(ac, TE)
        p.inc(ac.activation(sgbc[:], PA[:], AF.Copy), AC)
        p.wait(te, AC)
        peek_out("comb", combT[:, 0:T])

        endphase("P3")
        spill_mark = [0]
        t0b = WK1
        sq16 = WK2

        def proj_strip(m, w_h, wcol, bcol, sq_dst):
            buf = stgf[m % 2]; buf16 = stg16[m % 2]
            p.wait(sync, TE)
            wld = p.inc(sync.dma_start(buf[:], w_h[m]), LD, 16)
            p.wait(dv, LD, wld)
            p.wait(dv, SP, spill_mark[0])
            p.inc(dv.tensor_copy(buf16[:], buf[:]), DV)
            p.wait(te, DV)
            for ng in range(4):
                for kd in range(NKD):
                    ins = te.matmul(PA[:, ng * 512:(ng + 1) * 512],
                                    buf16[:, kd * 128:(kd + 1) * 128],
                                    XNT[:, kd * T + ng * 512:kd * T + (ng + 1) * 512],
                                    start=(kd == 0), stop=(kd == NKD - 1))
            p.inc(ins, TE)
            p.wait(dv, TE)
            p.inc(dv.tensor_scalar(t0b[:], PA[:], wcol[:, m, 3:4], bcol[:, m:m + 1],
                                   op0=ALU.mult, op1=ALU.add), DV)
            for j, off in ((2, 1), (1, 2), (0, 3)):
                p.inc(dv.scalar_tensor_tensor(t0b[:, off:T], PA[:, 0:T - off],
                                              wcol[:, m, j:j + 1], t0b[:, off:T],
                                              op0=ALU.mult, op1=ALU.add), DV)
            p.wait(ac, DV)
            p.wait(ac, SP, spill_mark[0])
            p.inc(ac.activation(strip[:], t0b[:], AF.Silu), AC)
            if sq_dst is not None:
                p.wait(dv, AC)
                p.inc(dv.tensor_tensor(sq16[:], strip[:], strip[:], op=ALU.mult), DV)
                p.wait(te, DV)
                for tt in range(NT):
                    p.inc(te.matmul(sq_dst(tt), sq16[:, tt * 128:(tt + 1) * 128],
                                    ones_col[:], start=True, stop=True), TE)

        PSS = PB[:, 512:1024]

        for m in range(HK):
            proj_strip(m, wq_h, qcw, qcb,
                       lambda tt, m=m: PSS[:, tt * 16 + m:tt * 16 + m + 1])
            p.wait(ac, AC)
            spill_mark[0] = p.inc(ac.dma_start(qT_sp[m * 128:(m + 1) * 128, :], strip[:]), SP, 16)
        peek_out("q15", strip[:, 0:T])

        endphase("P4")
        for m in range(HK):
            proj_strip(m, wk_h, kcw, kcb,
                       lambda tt, m=m: PSS[:, 256 + tt * 16 + m:256 + tt * 16 + m + 1])
            p.wait(ac, AC)
            spill_mark[0] = p.inc(ac.dma_start(kT_sp[m * 128:(m + 1) * 128, :], strip[:]), SP, 16)

        endphase("P5")
        ts0 = stgf[0][:, 0:512]
        ts1 = stgf[0][:, 512:1024]
        p.wait(dv, TE)
        p.wait(dv, SP, spill_mark[0])
        p.inc(dv.tensor_scalar_max(ts0, PSS[:], 1e-24), DV)
        p.wait(ac, DV)
        p.inc(ac.activation(ts1, ts0, AF.Ln), AC)
        p.wait(ac, AC)
        p.inc(ac.activation(ts0, ts1, AF.Exp, scale=fcn[:, 1:2]), AC)
        p.wait(dv, AC)
        p.inc(dv.tensor_copy(rq_tok, ts0[:, 0:256]), DV)
        p.inc(dv.tensor_copy(rk_tok, ts0[:, 256:512]), DV)
        PT8 = PB[:, 0:160]
        p.wait(te, DV)
        for tt in range(NT):
            for grp, gn in GA:
                gr = grows(grp)
                p.inc(te.transpose(PT8[0:8, 32:160], rk_tok[:, tt * 16 + grp * 8:
                                                            tt * 16 + grp * 8 + 8],
                                   idn32[:]), TE)
                p.wait(dv, TE)
                p.inc(dv.tensor_copy(rkTv[gr, tt * 128:(tt + 1) * 128], PT8[0:8, 32:160]), DV)
                p.wait(te, DV)
                p.inc(te.transpose(PT8[0:8, 32:160], rq_tok[:, tt * 16 + grp * 8:
                                                            tt * 16 + grp * 8 + 8],
                                   idn32[:]), TE)
                p.wait(dv, TE)
                p.inc(dv.tensor_copy(rqTv[gr, tt * 128:(tt + 1) * 128], PT8[0:8, 32:160]), DV)
                p.wait(te, DV)
        for tt in range(NT):
            for grp, gn in GA:
                gr = grows(grp)
                cs = tt * 16 + grp * 8
                p.inc(te.transpose(PT8[0:128, 32:40], bTv[gr, tt * 128:(tt + 1) * 128],
                                   idn32[gr, grp * 64:grp * 64 + 8]), TE)
                p.wait(dv, TE)
                p.inc(dv.tensor_copy(b_tok[:, cs:cs + 8], PT8[0:128, 32:40]), DV)
                p.wait(te, DV)
        for grp, gn in GA:
            gr = grows(grp)
            lt = LTv[gr, :]
            p.wait(ac, DV); p.wait(ac, TE)
            for c in range(NCH):
                mid = c * CH + CH // 2 - 1
                p.inc(ac.activation(TMP5[gr, c * CH:(c + 1) * CH], lt[:, c * CH:(c + 1) * CH],
                                    AF.Exp, scale=fcn[gr, 2:3], bias=lt[:, mid:mid + 1]), AC)
            p.wait(dv, AC)
            p.inc(dv.tensor_tensor(TMP5[gr, :], TMP5[gr, :], rkTv[gr, :], op=ALU.mult), DV)
            p.wait(dv, DV)
            p.inc(dv.tensor_copy(md16lrk[gr, :], TMP5[gr, :]), DV)
            p.wait(te, DV)
            for tt in range(NT):
                cs = tt * 16 + grp * 8
                p.inc(te.transpose(PT8[0:128, 32:40], TMP5[gr, tt * 128:(tt + 1) * 128],
                                   idn32[gr, grp * 64:grp * 64 + 8]), TE)
                p.wait(dv, TE)
                p.inc(dv.tensor_copy(lrk_tok[:, cs:cs + 8], PT8[0:128, 32:40]), DV)
                p.wait(te, DV)
            for c in range(NCH):
                mid = c * CH + CH // 2 - 1
                p.wait(dv, AC)
                p.inc(dv.tensor_scalar_mul(colt[gr, 38:39], lt[:, mid:mid + 1], -1.0), DV)
                p.wait(dv, DV)
                p.wait(ac, DV)
                p.inc(ac.activation(TMP2[gr, c * CH:(c + 1) * CH], lt[:, c * CH:(c + 1) * CH],
                                    AF.Exp, bias=colt[gr, 38:39]), AC)
            p.wait(dv, AC)
            p.inc(dv.tensor_tensor(TMP5[gr, :], TMP2[gr, :], bTv[gr, :], op=ALU.mult), DV)
            p.wait(dv, DV)
            p.inc(dv.tensor_tensor(TMP5[gr, :], TMP5[gr, :], rkTv[gr, :], op=ALU.mult), DV)
            p.wait(dv, DV)
            p.inc(dv.tensor_copy(md16bt[gr, :], TMP5[gr, :]), DV)
            p.wait(te, DV)
            for tt in range(NT):
                cs = tt * 16 + grp * 8
                p.inc(te.transpose(PT8[0:128, 32:40], TMP5[gr, tt * 128:(tt + 1) * 128],
                                   idn32[gr, grp * 64:grp * 64 + 8]), TE)
                p.wait(dv, TE)
                p.inc(dv.tensor_scalar_mul(nbt_tok[:, cs:cs + 8], PT8[0:128, 32:40], -1.0), DV)
                p.wait(te, DV)
            p.inc(dv.tensor_tensor(TMP2[gr, :], TMP2[gr, :], rqTv[gr, :], op=ALU.mult), DV)
            p.wait(dv, DV)
            p.wait(te, DV)
            for c in range(NCH):
                p.inc(te.transpose(PT8[0:64, 32:40],
                                   TMP2[gr, c * CH:(c + 1) * CH],
                                   idn32[gr, grp * 64:grp * 64 + 8]), TE)
                p.wait(dv, TE)
                dst = lq_dup[0:64, :].rearrange("p (h c) -> p h c", c=32)[
                    :, grp * 8:grp * 8 + 8, c]
                p.inc(dv.tensor_copy(dst, PT8[0:64, 32:40]), DV)
                p.wait(te, DV)
        for grp, gn in GA:
            gr = grows(grp)
            lt = LTv[gr, :]
            lc = lt.rearrange("p (c j) -> p c j", j=CH)[:, :, CH - 1]
            lm_ = lt.rearrange("p (c j) -> p c j", j=CH)[:, :, CH // 2 - 1]
            p.inc(dv.tensor_tensor(colt[gr, 0:32], lc, lm_, op=ALU.subtract), DV)
            p.inc(dv.memset(colt[gr, 32:36], 0.0), DV)
            nxt = lt[:, CH + CH // 2 - 1:T:CH]
            p.inc(dv.tensor_tensor(colt[gr, 0:31], colt[gr, 0:31], nxt, op=ALU.add), DV)
            p.wait(ac, DV)
            p.inc(ac.activation(colt[gr, 0:32], colt[gr, 0:32], AF.Exp), AC)
            p.wait(dv, AC)
            p.inc(dv.tensor_copy(sq16[gr, 0:32], colt[gr, 0:32]), DV)
            p.wait(te, DV)
            for h in range(8):
                p.inc(te.matmul(PB[0:1, grp * 256 + h * 32:grp * 256 + (h + 1) * 32],
                                i8c[gr, h:h + 1], sq16[gr, 0:32], start=True, stop=True), TE)
        p.wait(dv, TE)
        p.inc(dv.tensor_copy(rowb[0:1, 0:512], PB[0:1, 0:512]), DV)
        p.wait(te, DV)
        for ng in range(4):
            ins = te.matmul(PB[:, 512 + ng * 128:512 + (ng + 1) * 128], ones_row[0:1, :],
                            rowb[0:1, ng * 128:(ng + 1) * 128], start=True, stop=True)
        p.inc(ins, TE)
        p.wait(dv, TE)
        p.inc(dv.tensor_copy(scal[:, :], PB[:, 512:1024]), DV)
        peek_out("scal", scal[:, 0:512])
        peek_out("lrkA", md16lrk[0:8, 0:T])
        peek_out("btA", md16bt[0:8, 0:T])
        peek_out("lqdup", lq_dup[:, 0:512])
        peek_out("btok", tokmd[:, 0:1280])

        endphase("P6")
        PTRK = PA[:, 0:1024].bitcast(BF16)
        p.wait(dv, DV)
        for m in range(HK):
            p.wait(sync, DV); p.wait(sync, AC); p.wait(sync, SP)
            wld = p.inc(sync.dma_start(strip[:], kT_sp[m * 128:(m + 1) * 128, :]), LD, 16)
            p.wait(te, LD, wld)
            p.wait(te, DV); p.wait(te, AC)
            for tt in range(NT):
                ins = te.transpose(PTRK[:, tt * 128:(tt + 1) * 128],
                                   strip[:, tt * 128:(tt + 1) * 128], idn16[:])
            p.inc(ins, TE)
            p.wait(dv, TE); p.wait(ac, TE)
            for tt in range(NT):
                src = PTRK[:, tt * 128:(tt + 1) * 128]
                cK = lrk_tok[:, tt * 16 + m:tt * 16 + m + 1]
                cB = nbt_tok[:, tt * 16 + m:tt * 16 + m + 1]
                if tt < 8:
                    p.inc(dv.tensor_scalar_mul(stg16[0][:, tt * 128:(tt + 1) * 128], src, cK), DV)
                    p.inc(dv.tensor_scalar_mul(stg16[1][:, tt * 128:(tt + 1) * 128], src, cB), DV)
                else:
                    p.inc(ac.activation(stg16[0][:, tt * 128:(tt + 1) * 128], src,
                                        AF.Copy, scale=cK), AC)
                    p.inc(ac.activation(stg16[1][:, tt * 128:(tt + 1) * 128], src,
                                        AF.Copy, scale=cB), AC)
            p.wait(ac, DV); p.wait(ac, AC)
            km_dst = km_sp[m * T:(m + 1) * T, :].rearrange("(tt p) k -> p tt k", p=128)
            nbk_dst = nbk_sp[m * T:(m + 1) * T, :].rearrange("(tt p) k -> p tt k", p=128)
            p.wait(ac, AC)
            p.inc(ac.dma_start(km_dst, stg16[0][:].rearrange("p (tt k) -> p tt k", k=128)),
                  SP, 16)
            spill_mark[0] = p.inc(
                ac.dma_start(nbk_dst, stg16[1][:].rearrange("p (tt k) -> p tt k", k=128)),
                SP, 16)
            p.wait(dv, SP); p.wait(ac, SP)

        endphase("P7")
        for m in range(NVT):
            proj_strip(m, wv_h, vcw, vcb, None)
            h, vh = m // 2, m % 2
            p.wait(te, AC)
            for tt in range(NT):
                ins = te.transpose(PTRK[:, tt * 128:(tt + 1) * 128],
                                   strip[:, tt * 128:(tt + 1) * 128], idn16[:])
            p.inc(ins, TE)
            p.wait(dv, TE); p.wait(ac, TE)
            if vh == 0:
                p.wait(dv, SP); p.wait(ac, SP)
            for tt in range(NT):
                src = PTRK[:, tt * 128:(tt + 1) * 128]
                cB = b_tok[:, tt * 16 + h:tt * 16 + h + 1]
                dst = stg16[vh][:, tt * 128:(tt + 1) * 128]
                if tt < 8:
                    p.inc(dv.tensor_scalar_mul(dst, src, cB), DV)
                else:
                    p.inc(ac.activation(dst, src, AF.Copy, scale=cB), AC)
            if vh == 1:
                p.wait(ac, DV); p.wait(ac, AC)
                dst0 = bv_sp[h * T:(h + 1) * T, 0:128].rearrange("(tt p) k -> p tt k", p=128)
                dst1 = bv_sp[h * T:(h + 1) * T, 128:256].rearrange("(tt p) k -> p tt k", p=128)
                p.wait(ac, AC)
                p.inc(ac.dma_start(dst0, stg16[0][:].rearrange("p (tt k) -> p tt k", k=128)),
                      SP, 16)
                spill_mark[0] = p.inc(
                    ac.dma_start(dst1, stg16[1][:].rearrange("p (tt k) -> p tt k", k=128)),
                    SP, 16)
        p.wait(dv, SP); p.wait(ac, SP)
        peek_out("bv15", stg16[1][:, 0:T])

        endphase("P8")
        s1b, x3b, htb = WK1, WK2, WK3
        for e in range(E):
            p.wait(dv, AC); p.wait(dv, TE)
            p.wait(te, DV)
            for ng in range(4):
                ins = te.matmul(PB[0:1, ng * 512:(ng + 1) * 512], i8c[0:8, e:e + 1],
                                combT[0:8, ng * 512:(ng + 1) * 512], start=True, stop=True)
            p.inc(ins, TE)
            p.wait(dv, TE)
            p.inc(dv.tensor_copy(rowsg[0:1, :], PB[0:1, 0:2048]), DV)
            p.wait(te, DV)
            for ng in range(4):
                ins = te.matmul(PA[:, ng * 512:(ng + 1) * 512], ones_row[0:1, :],
                                rowsg[0:1, ng * 512:(ng + 1) * 512], start=True, stop=True)
            p.inc(ins, TE)
            p.wait(dv, TE)
            p.inc(dv.tensor_copy(combbc[:], PA[:]), DV)
            p.wait(te, DV)
            for m in range(8):
                buf = stgf[m % 2]; buf16 = stg16[m % 2]
                p.wait(sync, TE)
                wld = p.inc(sync.dma_start(buf[:], wg_h[e, m]), LD, 16)
                p.wait(dv, LD, wld)
                p.inc(dv.tensor_copy(buf16[:], buf[:]), DV)
                p.wait(te, DV)
                for ng in range(4):
                    for kd in range(NKD):
                        ins = te.matmul(PA[:, ng * 512:(ng + 1) * 512],
                                        buf16[:, kd * 128:(kd + 1) * 128],
                                        XNT[:, kd * T + ng * 512:kd * T + (ng + 1) * 512],
                                        start=(kd == 0), stop=(kd == NKD - 1))
                p.inc(ins, TE)
                p.wait(sync, DV)
                wld = p.inc(sync.dma_start(buf[:], wu_h[e, m]), LD, 16)
                p.wait(dv, LD, wld)
                p.wait(dv, TE)
                p.inc(dv.tensor_copy(buf16[:], buf[:]), DV)
                p.wait(te, DV)
                for ng in range(4):
                    for kd in range(NKD):
                        ins = te.matmul(PB[:, ng * 512:(ng + 1) * 512],
                                        buf16[:, kd * 128:(kd + 1) * 128],
                                        XNT[:, kd * T + ng * 512:kd * T + (ng + 1) * 512],
                                        start=(kd == 0), stop=(kd == NKD - 1))
                p.inc(ins, TE)
                p.wait(ac, TE)
                p.inc(ac.activation(s1b[:], PA[:], AF.Silu), AC)
                p.wait(dv, TE)
                p.inc(dv.tensor_tensor(x3b[:], PB[:], combbc[:], op=ALU.mult), DV)
                p.wait(dv, AC)
                p.wait(dv, SP, spill_mark[0])
                p.inc(dv.tensor_tensor(htb[:], s1b[:], x3b[:], op=ALU.mult), DV)
                p.wait(ac, DV)
                spill_mark[0] = p.inc(
                    ac.dma_start(ht_sp[(e * 8 + m) * 128:(e * 8 + m + 1) * 128, :], htb[:]),
                    SP, 16)
        peek_out("ht63", htb[:, 0:T])

        endphase("P9")
        for m in range(16):
            buf = stgf[m % 2]; buf16 = stg16[m % 2]
            p.wait(sync, TE)
            wld = p.inc(sync.dma_start(buf[:], shg_h[m]), LD, 16)
            p.wait(dv, LD, wld)
            p.wait(dv, SP, spill_mark[0])
            p.inc(dv.tensor_copy(buf16[:], buf[:]), DV)
            p.wait(te, DV)
            for ng in range(4):
                for kd in range(NKD):
                    ins = te.matmul(PA[:, ng * 512:(ng + 1) * 512],
                                    buf16[:, kd * 128:(kd + 1) * 128],
                                    XNT[:, kd * T + ng * 512:kd * T + (ng + 1) * 512],
                                    start=(kd == 0), stop=(kd == NKD - 1))
            p.inc(ins, TE)
            p.wait(sync, DV)
            wld = p.inc(sync.dma_start(buf[:], shu_h[m]), LD, 16)
            p.wait(dv, LD, wld)
            p.wait(dv, TE)
            p.inc(dv.tensor_copy(buf16[:], buf[:]), DV)
            p.wait(te, DV)
            for ng in range(4):
                for kd in range(NKD):
                    ins = te.matmul(PB[:, ng * 512:(ng + 1) * 512],
                                    buf16[:, kd * 128:(kd + 1) * 128],
                                    XNT[:, kd * T + ng * 512:kd * T + (ng + 1) * 512],
                                    start=(kd == 0), stop=(kd == NKD - 1))
            p.inc(ins, TE)
            p.wait(ac, TE)
            p.inc(ac.activation(s1b[:], PA[:], AF.Silu), AC)
            p.wait(dv, TE)
            p.inc(dv.tensor_tensor(x3b[:], PB[:], sgbc[:], op=ALU.mult), DV)
            p.wait(dv, AC)
            p.wait(dv, SP, spill_mark[0])
            p.inc(dv.tensor_tensor(htb[:], s1b[:], x3b[:], op=ALU.mult), DV)
            p.wait(ac, DV)
            p.wait(ac, AC)
            spill_mark[0] = p.inc(ac.dma_start(s_sp[m * 128:(m + 1) * 128, :], htb[:]), SP, 16)

        endphase("P10")
        for src_h, dst_sp, n in ((wd_h, wd_bf, E * 8), (shd_h, shd_bf, 16), (wo_h, wo_bf, 32)):
            for m in range(n):
                buf = stgf[m % 2]; buf16 = stg16[m % 2]
                p.wait(sync, SP)
                wld = p.inc(sync.dma_start(buf[:], src_h[m]), LD, 16)
                if m % 2 == 0:
                    p.wait(dv, LD, wld)
                    p.inc(dv.tensor_copy(buf16[:], buf[:]), DV)
                    p.wait(ac, DV)
                else:
                    p.wait(ac, LD, wld)
                    p.inc(ac.activation(buf16[:], buf[:], AF.Copy), AC)
                    p.wait(ac, AC)
                spill_mark[0] = p.inc(
                    ac.dma_start(dst_sp[m * 128:(m + 1) * 128, :], buf16[:]), SP, 16)
        p.wait(dv, SP); p.wait(ac, SP)

        endphase("P105")
        p.wait(dv, DV)
        p.inc(dv.memset(SR[:, :], 0.0), DV)
        p.inc(dv.memset(SR16[:, :], 0.0), DV)
        PD = PA[0:64, 0:1024]
        PKQ = PA[0:64, 1024:2048]
        PLAM = PB[0:64, 0:1024]
        PROW = PB[0:1, 1024:2048]
        PBB = PB[0:64, 0:1024]
        PTR2 = PB[0:64, 0:512].bitcast(BF16)
        PSQ = PB[0:64, 0:2048]
        PCB = PB[0:64, 0:1024]
        PG = PA[:, 0:1024]
        PUW = [PA[0:64, 0:2048], PB[0:64, 0:2048]]
        POW = [PA[0:64, 0:2048], PB[0:64, 0:2048]]
        PDLT = PA[:, 0:2048]
        POTR = PB[:, 0:512].bitcast(BF16)

        qk_mark = [0, 0]
        ch_mark = [0] * NCH

        def pref_quad(qd_):
            bq = qd_ % 2
            src_q = qT_sp.rearrange("(h p) t -> h p t", p=128)[:, :, qd_ * 256:(qd_ + 1) * 256]
            src_k = kT_sp.rearrange("(h p) t -> h p t", p=128)[:, :, qd_ * 256:(qd_ + 1) * 256]
            p.inc(sync.dma_start(qbuf[bq].rearrange("p (h t) -> p h t", t=256),
                                 src_q.rearrange("h p t -> p h t")), LD, 16)
            qk_mark[bq] = p.inc(
                sync.dma_start(kbuf[bq].rearrange("p (h t) -> p h t", t=256),
                               src_k.rearrange("h p t -> p h t")), LD, 16)

        def pref_chunk(c_):
            bcn = c_ % 2
            src = bv_sp.rearrange("(h t) k -> h t k", t=T)[:, c_ * CH:(c_ + 1) * CH, :]
            p.inc(sync.dma_start(
                bvb[bcn].rearrange("p (h k) -> p h k", k=VE),
                src.rearrange("h t k -> t h k")), LD, 16)
            src = km_sp.rearrange("(h t) k -> h t k", t=T)[:, c_ * CH:(c_ + 1) * CH, :]
            p.inc(sync.dma_start(
                kmb[bcn].rearrange("p (h k) -> p h k", k=DK),
                src.rearrange("h t k -> t h k")), LD, 16)
            src = nbk_sp.rearrange("(h t) k -> h t k", t=T)[:, c_ * CH:(c_ + 1) * CH, :]
            ch_mark[c_] = p.inc(sync.dma_start(
                nbkb[bcn].rearrange("p (h k) -> p h k", k=DK),
                src.rearrange("h t k -> t h k")), LD, 16)

        pref_quad(0)
        pref_chunk(0)
        pref_chunk(1)
        id64 = idn16[0:64, 0:64]

        for c in range(NCH):
            qd_, cq = c // 4, c % 4
            bcn = c % 2
            qb, kb = qbuf[qd_ % 2], kbuf[qd_ % 2]

            def ksl(h, cq=cq, kb=kb):
                return kb[:, h * 256 + cq * 64:h * 256 + cq * 64 + 64]

            def qsl(h, cq=cq, qb=qb):
                return qb[:, h * 256 + cq * 64:h * 256 + cq * 64 + 64]

            if cq == 3 and qd_ + 1 < 8:
                p.wait(sync, TE)
                pref_quad(qd_ + 1)
            p.wait(te, LD, max(qk_mark[qd_ % 2], ch_mark[c]))

            for grp, gn in GA:
                gr = grows(grp)
                p.inc(te.matmul(PLAM[:, grp * 512:(grp + 1) * 512],
                                md16lrk[gr, c * CH:(c + 1) * CH],
                                exp64[gr, :], start=True, stop=True), TE)
                for h in range(8):
                    p.inc(te.matmul(PROW[0:1, grp * 512 + h * 64:grp * 512 + (h + 1) * 64],
                                    i8c[gr, h:h + 1], md16bt[gr, c * CH:(c + 1) * CH],
                                    start=True, stop=True), TE)
            p.wait(dv, TE)
            p.inc(dv.tensor_tensor(lstr[:, :], PLAM, mstr[:], op=ALU.mult), DV)
            p.inc(dv.tensor_tensor(lincl[:, :], PLAM, mincl[:], op=ALU.mult), DV)
            p.inc(dv.tensor_copy(rowb[0:1, 0:1024], PROW), DV)
            p.wait(dv, DV)
            p.wait(te, DV)
            for half in range(2):
                p.inc(te.matmul(PBB[:, half * 512:(half + 1) * 512], ones_row[0:1, 0:64],
                                rowb[0:1, half * 512:(half + 1) * 512],
                                start=True, stop=True), TE)
            p.wait(ac, TE)
            p.inc(ac.activation(bbs[:, :], PBB, AF.Copy), AC)
            p.wait(te, AC)
            for h in range(HK):
                p.inc(te.matmul(PD[:, h * 64:(h + 1) * 64], ksl(h), ksl(h),
                                start=True, stop=True), TE)
                p.inc(te.matmul(PKQ[:, h * 64:(h + 1) * 64], ksl(h), qsl(h),
                                start=True, stop=True), TE)
            p.wait(dv, TE)
            p.inc(dv.tensor_tensor(Pcb[:, :], PD, lstr[:, :], op=ALU.mult), DV)
            p.wait(dv, DV)
            p.inc(dv.tensor_tensor(Pcb[:, :], Pcb[:, :], bbs[:, :], op=ALU.mult), DV)
            p.inc(dv.tensor_tensor(ATs[:, :], PKQ, lincl[:, :], op=ALU.mult), DV)
            if c == 0:
                peek_out("negMT0", Pcb[:, 0:1024])
            p.wait(te, DV)
            for h in range(HK):
                ins = te.transpose(PTR2[:, h * 64:(h + 1) * 64],
                                   Pcb[:, h * 64:(h + 1) * 64], id64)
            p.inc(ins, TE)
            p.wait(dv, TE)
            p.inc(dv.tensor_copy(PTcb[:, :], PTR2), DV)
            p.inc(dv.tensor_tensor(Tcb[:, :], Pcb[:, :], idup[:], op=ALU.add), DV)
            for lvl in range(5):
                p.wait(te, DV)
                for h in range(HK):
                    sl = slice(h * 64, (h + 1) * 64)
                    p.inc(te.matmul(PSQ[:, h * 128:h * 128 + 64], PTcb[:, sl],
                                    Pcb[:, sl], start=True, stop=True), TE)
                    p.inc(te.matmul(PSQ[:, h * 128 + 64:h * 128 + 128], Pcb[:, sl],
                                    PTcb[:, sl], start=True, stop=True), TE)
                p.wait(dv, TE)
                psq = PSQ.rearrange("p (h two j) -> p h two j", two=2, j=64)
                p.inc(dv.tensor_copy(
                    Pcb.rearrange("p (h j) -> p h j", j=64), psq[:, :, 0, :]), DV)
                p.inc(dv.tensor_copy(
                    PTcb.rearrange("p (h j) -> p h j", j=64), psq[:, :, 1, :]), DV)
                p.wait(te, DV)
                for h in range(HK):
                    sl = slice(h * 64, (h + 1) * 64)
                    p.inc(te.matmul(PCB[:, sl], PTcb[:, sl], Tcb[:, sl],
                                    start=True, stop=True), TE)
                p.wait(dv, TE)
                p.wait(dv, DV)
                p.inc(dv.tensor_tensor(Tcb[:, :], Tcb[:, :], PCB, op=ALU.add), DV)
            p.wait(te, DV)
            for h in range(HK):
                p.inc(te.matmul(PG[:, h * 64:(h + 1) * 64],
                                nbkb[bcn][:, h * 128:(h + 1) * 128],
                                Tcb[:, h * 64:(h + 1) * 64], start=True, stop=True), TE)
            p.wait(ac, TE)
            p.inc(ac.activation(Gts[:, :], PG, AF.Copy), AC)
            p.wait(te, AC)
            for wv in range(2):
                for hh in range(8):
                    h = wv * 8 + hh
                    p.inc(te.matmul(PUW[wv][:, hh * 256:(hh + 1) * 256],
                                    Tcb[:, h * 64:(h + 1) * 64],
                                    bvb[bcn][:, h * 256:(h + 1) * 256],
                                    start=True, stop=False), TE)
                    p.inc(te.matmul(PUW[wv][:, hh * 256:(hh + 1) * 256],
                                    Gts[:, h * 64:(h + 1) * 64],
                                    SR16[:, h * 256:(h + 1) * 256],
                                    start=False, stop=True), TE)
                p.wait(dv, TE)
                p.inc(dv.tensor_copy(Us[:, wv * 2048:(wv + 1) * 2048], PUW[wv]), DV)
            for wv in range(2):
                p.wait(te, DV)
                for hh in range(8):
                    h = wv * 8 + hh
                    p.inc(te.matmul(POW[wv][:, hh * 256:(hh + 1) * 256], qsl(h),
                                    SR16[:, h * 256:(h + 1) * 256], start=True, stop=False), TE)
                    p.inc(te.matmul(POW[wv][:, hh * 256:(hh + 1) * 256],
                                    ATs[:, h * 64:(h + 1) * 64],
                                    Us[:, h * 256:(h + 1) * 256],
                                    start=False, stop=True), TE)
                p.wait(dv, TE); p.wait(ac, TE)
                for hh in range(8):
                    h = wv * 8 + hh
                    colap = lq_dup[0:64, h * 32 + c:h * 32 + c + 1]
                    dst = obuf[:, h * 256:(h + 1) * 256]
                    if hh < 4:
                        p.inc(dv.tensor_scalar_mul(dst, POW[wv][:, hh * 256:(hh + 1) * 256],
                                                   colap), DV)
                    else:
                        p.inc(ac.activation(dst, POW[wv][:, hh * 256:(hh + 1) * 256],
                                            AF.Copy, scale=colap), AC)
            tq0 = (c % 4) * 64
            p.wait(te, DV); p.wait(te, AC)
            for rnd in range(2):
                for s in range(16):
                    vt = rnd * 16 + s
                    h, vh = vt // 2, vt % 2
                    ins = te.transpose(POTR[:, s * 64:(s + 1) * 64],
                                       obuf[:, h * 256 + vh * 128:h * 256 + (vh + 1) * 128],
                                       id64)
                p.inc(ins, TE)
                p.wait(dv, TE)
                dst = OTacc.rearrange("p (vt q) -> p vt q", q=OT_QT)[
                    :, rnd * 16:(rnd + 1) * 16, tq0:tq0 + 64]
                p.inc(dv.tensor_copy(
                    dst, POTR.rearrange("p (vt j) -> p vt j", j=64)[:, 0:16, :]), DV)
                p.wait(te, DV)
            for wv in range(2):
                p.wait(te, DV)
                for hh in range(8):
                    h = wv * 8 + hh
                    p.inc(te.matmul(PDLT[:, hh * 256:(hh + 1) * 256],
                                    kmb[bcn][:, h * 128:(h + 1) * 128],
                                    Us[:, h * 256:(h + 1) * 256],
                                    start=True, stop=True), TE)
                p.wait(dv, TE)
                p.inc(dv.tensor_tensor(SR[:, wv * 2048:(wv + 1) * 2048],
                                       SR[:, wv * 2048:(wv + 1) * 2048], PDLT,
                                       op=ALU.add), DV)
            p.wait(dv, DV)
            for h in range(HK):
                scol = scal[:, h * 32 + c:h * 32 + c + 1]
                p.inc(dv.tensor_scalar_mul(SR[:, h * 256:(h + 1) * 256],
                                           SR[:, h * 256:(h + 1) * 256], scol), DV)
            p.wait(dv, DV)
            p.wait(ac, DV)
            p.inc(ac.activation(SR16[:, :], SR[:, :], AF.Copy), AC)
            if c + 2 < NCH:
                p.wait(sync, TE)
                pref_chunk(c + 2)
            if c % 4 == 3:
                qd4 = c // 4
                p.wait(ac, DV)
                p.wait(ac, AC)
                spill_mark[0] = p.inc(ac.dma_start(
                    ot_sp.rearrange("(b vt p) q -> b p vt q", vt=32, p=128)[qd4],
                    OTacc.rearrange("p (vt q) -> p vt q", q=OT_QT)), SP, 16)
                if c + 1 < NCH:
                    p.wait(dv, SP)
            if c == 0:
                peek_out("TT0", Tcb[:, 0:1024])
                peek_out("U0", Us[:, 0:4096])
                peek_out("O0", obuf[:, 0:4096])
                peek_out("AT0", ATs[:, 0:1024])
        p.wait(dv, AC)
        peek_out("SR", SR[:, 0:4096])

        endphase("P11")
        nsrc = 112
        srcs = ([(wo_bf, vt) for vt in range(32)] + [(wd_bf, em) for em in range(64)]
                + [(shd_bf, m) for m in range(16)])
        NWB = 4
        wmark = [0] * NWB

        def wload(gi):
            dq_, si_ = gi // nsrc, gi % nsrc
            src, idx = srcs[si_]
            wmark[gi % NWB] = p.inc(sync.dma_start(
                wdn[gi % NWB][:, :], src[idx * 128:(idx + 1) * 128, dq_ * QT:(dq_ + 1) * QT]),
                LD, 16)

        for qd_ in range(QD):
            p.wait(sync, TE); p.wait(sync, SP)
            for hb in range(2):
                p.inc(sync.dma_start(
                    ACTS.rearrange("p (f t) -> p f t", t=QT)[:, 0:32, hb * 256:(hb + 1) * 256],
                    ot_sp.rearrange("(b vt p) q -> b p vt q", vt=32, p=128)[qd_ * 2 + hb]),
                    LD, 16)
            for h8 in range(4):
                p.inc(sync.dma_start(
                    ACTS.rearrange("p (f t) -> p f t", t=QT)[:, 32 + h8 * 16:32 + (h8 + 1) * 16, :],
                    ht_sp.rearrange("(f p) t -> p f t", p=128)[
                        :, h8 * 16:(h8 + 1) * 16, qd_ * QT:(qd_ + 1) * QT]), LD, 16)
            acts_ld = p.inc(sync.dma_start(
                ACTS.rearrange("p (f t) -> p f t", t=QT)[:, 96:112, :],
                s_sp.rearrange("(f p) t -> p f t", p=128)[:, :, qd_ * QT:(qd_ + 1) * QT]),
                LD, 16)
            for gi in range(NWB - 1):
                wload(gi)
            for dq in range(4):
                PQ = (PA if dq % 2 == 0 else PB)
                p.wait(te, DV)
                for si in range(nsrc):
                    gi = dq * nsrc + si
                    if gi + NWB - 1 < 4 * nsrc:
                        p.wait(sync, TE)
                        wload(gi + NWB - 1)
                    p.wait(te, LD, max(wmark[gi % NWB], acts_ld))
                    for sub in range(4):
                        p.inc(te.matmul(
                            PQ[:, sub * 512:(sub + 1) * 512],
                            ACTS[:, si * QT + sub * 128:si * QT + sub * 128 + 128],
                            wdn[gi % NWB][:, :], start=(si == 0), stop=(si == nsrc - 1)), TE)
                for sub in range(4):
                    tglob = qd_ * QT + sub * 128
                    p.wait(sync, DV)
                    wld = p.inc(sync.dma_start(
                        xres[:, :], x_h[tglob:tglob + 128, dq * QT:(dq + 1) * QT]), LD, 16)
                    p.wait(dv, LD, wld)
                    p.wait(dv, TE)
                    p.inc(dv.tensor_tensor(oute[:, :], PQ[:, sub * 512:(sub + 1) * 512],
                                           xres[:, :], op=ALU.add), DV)
                    p.wait(sync, DV)
                    p.inc(sync.dma_start(
                        out_h[tglob:tglob + 128, dq * QT:(dq + 1) * QT], oute[:, :]), LD, 16)

        p.wait(sync, LD)
        p.wait(sync, SP)
    except _Stop:
        sync.wait_ge(p.sem[LD], p.cnt[LD])
        sync.wait_ge(p.sem[SP], p.cnt[SP])
        stack.close()

    return nc


def kernel(**inputs):
    g = host_prep(inputs)
    x = np.asarray(inputs["L0_attn_x"], np.float32)
    nc = build()
    in_maps = []
    for core in range(8):
        m = dict(g)
        m["x"] = np.ascontiguousarray(x[core])
        in_maps.append(m)
    res = run_bass_kernel_spmd(nc, in_maps, core_ids=list(range(8)))
    out = np.stack([res.results[i]["out"] for i in range(8)])
    return out.astype(np.float32)




# version 05
# speedup vs baseline: 1.0418x; 1.0418x over previous
import numpy as np
import ml_dtypes
from contextlib import ExitStack

import concourse.bass as bass
import concourse.mybir as mybir
from concourse.bass_utils import run_bass_kernel_spmd

F32 = mybir.dt.float32
BF16 = mybir.dt.bfloat16
AF = mybir.ActivationFunctionType
ALU = mybir.AluOpType
AX = mybir.AxisListType
BF = ml_dtypes.bfloat16

T = 2048
DM = 2048
HK = 16
DK = 128
VE = 256
CH = 64
NCH = T // CH
E = 8
NT = T // 128
NKD = DM // 128
NVT = 4096 // 128
QD = 4
QT = T // QD


def _tile_w(w, nm):
    d = w.shape[0]
    nkd = d // 128
    return np.ascontiguousarray(
        w.reshape(nkd, 128, nm, 128).transpose(2, 1, 0, 3).reshape(nm, 128, nkd * 128))


def host_prep(inp):
    g = {}
    f32 = np.float32
    g["wq"] = _tile_w(np.asarray(inp["L0_attn_Wq"], f32), 16)
    g["wk"] = _tile_w(np.asarray(inp["L0_attn_Wk"], f32), 16)
    g["wv"] = _tile_w(np.asarray(inp["L0_attn_Wv"], f32), 32)
    for nm, key in (("wa", "L0_attn_Wa"), ("wb", "L0_attn_Wb")):
        w = np.asarray(inp[key], f32)
        g[nm] = np.ascontiguousarray(
            w.reshape(NKD, 128, 16).transpose(1, 0, 2).reshape(128, NKD * 16))
    for nm, wkey, bkey, ntile in (("qc", "L0_attn_qcw", "L0_attn_qcb", 16),
                                  ("kc", "L0_attn_kcw", "L0_attn_kcb", 16),
                                  ("vc", "L0_attn_vcw", "L0_attn_vcb", 32)):
        w = np.asarray(inp[wkey], f32).reshape(ntile, 128, 4).transpose(1, 0, 2)
        g[nm + "w"] = np.ascontiguousarray(w.reshape(128, ntile * 4))
        g[nm + "b"] = np.ascontiguousarray(np.asarray(inp[bkey], f32).reshape(ntile, 128).T)
    gw = np.asarray(inp["L0_ffn_gate_w"], f32).T
    g["gw"] = np.ascontiguousarray(gw.reshape(NKD, 128, 8).transpose(1, 0, 2).reshape(128, NKD * 8))
    se = np.asarray(inp["L0_ffn_se_gate_w"], f32).T
    g["se"] = np.ascontiguousarray(se.reshape(NKD, 128, 1).transpose(1, 0, 2).reshape(128, NKD))
    wg = np.asarray(inp["L0_ffn_w_gate"], f32)
    wu = np.asarray(inp["L0_ffn_w_up"], f32)
    g["wg"] = np.stack([_tile_w(wg[e].T, 8) for e in range(E)])
    g["wu"] = np.stack([_tile_w(wu[e].T, 8) for e in range(E)])
    g["shg"] = _tile_w(np.asarray(inp["L0_ffn_sh_gate"], f32).T, 16)
    g["shu"] = _tile_w(np.asarray(inp["L0_ffn_sh_up"], f32).T, 16)
    wd = np.asarray(inp["L0_ffn_w_down"], f32)
    g["wd"] = np.ascontiguousarray(
        np.stack([wd[e].T for e in range(E)]).reshape(E * 8, 128, DM))
    g["shd"] = np.ascontiguousarray(np.asarray(inp["L0_ffn_sh_down"], f32).T.reshape(16, 128, DM))
    g["wo"] = np.ascontiguousarray(np.asarray(inp["L0_attn_Wo"], f32).reshape(32, 128, DM))

    g["ident16"] = np.eye(128, dtype=BF)
    g["ident32"] = np.eye(128, dtype=f32)
    g["ones_col16"] = np.ones((128, 1), dtype=BF)
    g["ones_row16"] = np.ones((1, 128), dtype=BF)
    i8d = np.zeros((128, 8), dtype=BF)
    i8d[0:8] = np.eye(8)
    i8d[64:72] = np.eye(8)
    g["i8dup"] = i8d
    ju = np.arange(64)[:, None]
    iu = np.arange(64)[None, :]
    strict = -(iu > ju).astype(f32)
    incl = (iu >= ju).astype(f32)
    g["mask_strict"] = np.ascontiguousarray(np.tile(strict, (1, 16)))
    g["mask_incl"] = np.ascontiguousarray(np.tile(incl, (1, 16)))
    i64 = np.eye(64, dtype=BF)
    g["idup"] = np.ascontiguousarray(np.tile(i64, (1, 16)))
    ex = np.zeros((128, 512), dtype=BF)
    for h in range(8):
        ex[h, h * 64:(h + 1) * 64] = 1
        ex[64 + h, h * 64:(h + 1) * 64] = 1
    g["exp64"] = ex
    g["fconst"] = np.tile(np.array([1e-5, -0.5, -1.0, 0.0], f32), (128, 1))
    return g


class Prog:
    def __init__(self, nc, stack):
        self.nc = nc
        self.stack = stack
        self.sem = {}
        self.cnt = {}

    def sem_new(self, name):
        self.sem[name] = self.stack.enter_context(self.nc.semaphore(name))
        self.cnt[name] = 0

    def inc(self, instr, name, amount=1):
        instr.then_inc(self.sem[name], amount)
        self.cnt[name] += amount
        return self.cnt[name]

    def wait(self, eng, name, val=None):
        eng.wait_ge(self.sem[name], self.cnt[name] if val is None else val)


class _Stop(Exception):
    pass


def build(peek=(), last_phase=None):
    peek = set(peek)
    nc = bass.Bass("TRN2", target_bir_lowering=False)
    stack = ExitStack()
    p = Prog(nc, stack)
    sync, te, dv, ac, gp = nc.sync, nc.tensor, nc.vector, nc.scalar, nc.gpsimd
    LD, SP, TE, DV, AC = "ld", "sp", "te", "dv", "ac"

    def din(name, shape, dtype=F32):
        return nc.dram_tensor(name, list(shape), dtype, kind="ExternalInput").ap()

    x_h = din("x", [T, DM])
    wq_h = din("wq", [16, 128, DM]); wk_h = din("wk", [16, 128, DM]); wv_h = din("wv", [32, 128, DM])
    wa_h = din("wa", [128, NKD * 16]); wb_h = din("wb", [128, NKD * 16])
    qcw_h = din("qcw", [128, 64]); qcb_h = din("qcb", [128, 16])
    kcw_h = din("kcw", [128, 64]); kcb_h = din("kcb", [128, 16])
    vcw_h = din("vcw", [128, 128]); vcb_h = din("vcb", [128, 32])
    gw_h = din("gw", [128, NKD * 8]); se_h = din("se", [128, NKD])
    wg_h = din("wg", [E, 8, 128, DM]); wu_h = din("wu", [E, 8, 128, DM])
    shg_h = din("shg", [16, 128, DM]); shu_h = din("shu", [16, 128, DM])
    wd_h = din("wd", [E * 8, 128, DM]); shd_h = din("shd", [16, 128, DM])
    wo_h = din("wo", [32, 128, DM])
    ident16_h = din("ident16", [128, 128], BF16)
    ident32_h = din("ident32", [128, 128], F32)
    ones_col_h = din("ones_col16", [128, 1], BF16)
    ones_row_h = din("ones_row16", [1, 128], BF16)
    i8_h = din("i8dup", [128, 8], BF16)
    mstr_h = din("mask_strict", [64, 1024], F32)
    mincl_h = din("mask_incl", [64, 1024], F32)
    idup_h = din("idup", [64, 1024], BF16)
    exp64_h = din("exp64", [128, 512], BF16)
    fconst_h = din("fconst", [128, 4], F32)
    out_h = nc.dram_tensor("out", [T, DM], F32, kind="ExternalOutput").ap()

    def scr(name, shape, dtype=BF16):
        return nc.dram_tensor(name, list(shape), dtype, kind="Internal").ap()

    qT_sp = scr("qT_sp", [HK * 128, T])
    kT_sp = scr("kT_sp", [HK * 128, T])
    km_sp = scr("km_sp", [HK * T, DK])
    nbk_sp = scr("nbk_sp", [HK * T, DK])
    bv_sp = scr("bv_sp", [HK * T, VE])
    ht_sp = scr("ht_sp", [E * 8 * 128, T])
    s_sp = scr("s_sp", [16 * 128, T])
    ot_sp = scr("ot_sp", [8 * 32 * 128, 256])
    wd_bf = scr("wd_bf", [E * 8 * 128, DM])
    shd_bf = scr("shd_bf", [16 * 128, DM])
    wo_bf = scr("wo_bf", [32 * 128, DM])

    try:
      with stack:
        for s in (LD, SP, TE, DV, AC):
            p.sem_new(s)

        SB = lambda n, sh, dt: stack.enter_context(nc.sbuf_tensor(n, sh, dt))
        AR1 = SB("AR1", [128, 57344], BF16)
        AR2 = SB("AR2", [128, 12288], BF16)
        AR1f = AR1[:].bitcast(F32)
        AR2f = AR2[:].bitcast(F32)

        idn16 = SB("idn16", [128, 128], BF16)
        idn32 = SB("idn32", [128, 128], F32)
        ones_col = SB("ones_col", [128, 1], BF16)
        ones_row = SB("ones_row", [1, 128], BF16)
        i8c = SB("i8c", [128, 8], BF16)
        mstr = SB("mstr", [64, 1024], F32)
        mincl = SB("mincl", [64, 1024], F32)
        idup = SB("idup_c", [64, 1024], BF16)
        exp64 = SB("exp64_c", [128, 512], BF16)
        fcn = SB("fcn", [128, 4], F32)
        convw = SB("convw", [128, 320], F32)
        wab = SB("wab", [128, NKD * 32], BF16)
        gwc = SB("gwc", [128, NKD * 8], BF16)
        sec = SB("sec", [128, NKD], BF16)
        md16lrk = SB("md16lrk", [128, T], BF16)
        md16bt = SB("md16bt", [128, T], BF16)
        tokmd = SB("tokmd", [128, 1280], F32)
        b_tok = tokmd[:, 0:256]
        lrk_tok = tokmd[:, 256:512]
        nbt_tok = tokmd[:, 512:768]
        rk_tok = tokmd[:, 768:1024]
        rq_tok = tokmd[:, 1024:1280]
        lqscal = SB("lqscal", [128, 1024], F32)
        lq_dup = lqscal[:, 0:512]
        scal = lqscal[:, 512:1024]
        sgbc = SB("sgbc", [128, T], BF16)
        combT = SB("combT", [8, T], BF16)
        combbc = SB("combbc", [128, T], BF16)
        WK1 = SB("WK1", [128, DM], BF16)
        WK2 = SB("WK2", [128, DM], BF16)
        WK3 = SB("WK3", [128, DM], BF16)
        colt = SB("colt", [128, 40], F32)
        rowsg = SB("rowsg", [1, T], BF16)
        rowb = rowsg[0:1, 0:1024]

        stgf = [AR2f[:, i * 2048:(i + 1) * 2048] for i in range(2)]
        stg16 = [AR2[:, 8192 + i * 2048:8192 + (i + 1) * 2048] for i in range(2)]
        xin = stgf
        wdn = [AR2[:, i * QT:(i + 1) * QT] for i in range(4)]
        xres = AR2f[:, 1024:1536]
        oute = AR2f[:, 1536:2048]

        XNT = AR1[:, 0:32768]
        R = [AR1f[:, 16384 + i * 2048:16384 + (i + 1) * 2048] for i in range(6)]
        LTv, bTv, TMP2, rkTv, rqTv, TMP5 = R
        sqscr = R[0]
        sgrow = TMP2[0:1, :]

        def a1b(c0, n):
            return AR1[:, c0:c0 + n]

        def a1f(c0, n):
            return AR1f[:, c0 // 2:(c0 + n) // 2]

        qbuf = [a1b(0, 4096), a1b(4096, 4096)]
        kbuf = [a1b(8192, 4096), a1b(12288, 4096)]

        def a1b64(c0, n):
            return AR1[0:64, c0:c0 + n]

        bvb = [a1b64(16384, 4096), a1b64(20480, 4096)]
        kmb = [a1b64(24576, 2048), a1b64(26624, 2048)]
        nbkb = [a1b64(28672, 2048), a1b64(30720, 2048)]
        Us = a1b64(32768, 4096)
        obuf = a1b64(36864, 4096)
        Gts = a1b(40960, 1024)
        ATs = a1b64(41984, 1024)
        Tcb = a1b64(43008, 1024)
        Pcb = a1b64(44032, 1024)
        PTcb = a1b64(45056, 1024)
        lstr = AR1[0:64, 46080:47104]
        lincl = AR1[0:64, 47104:48128]
        bbs = AR1[0:64, 48128:49152]
        SR = a1f(49152, 8192)
        SR16 = AR2[:, 0:4096]
        OT_QT = 256
        OTacc = AR2[:, 4096:12288]
        ACTS = AR1[:, 0:112 * QT]

        PA = stack.enter_context(nc.psum_tensor("PA", [128, 2048], F32))
        PB = stack.enter_context(nc.psum_tensor("PB", [128, 2048], F32))

        def peek_out(tag, ap):
            if tag not in peek:
                return
            t = nc.dram_tensor("peek_" + tag, list(ap.shape), ap.dtype,
                               kind="ExternalOutput").ap()
            p.wait(sync, DV); p.wait(sync, AC); p.wait(sync, TE)
            p.inc(sync.dma_start(t, ap), LD, 16)

        GA = ((0, "A"), (1, "B"))

        def endphase(name):
            if last_phase == name:
                raise _Stop()


        def grows(grp, n=8):
            return slice(grp * 64, grp * 64 + n)

        for dst, src in ((idn16, ident16_h), (idn32, ident32_h), (ones_col, ones_col_h),
                         (ones_row, ones_row_h), (i8c, i8_h), (mstr, mstr_h),
                         (mincl, mincl_h), (idup, idup_h), (exp64, exp64_h),
                         (fcn, fconst_h)):
            p.inc(sync.dma_start(dst[:], src), LD, 16)
        for (c0, c1), src in (((0, 64), qcw_h), ((64, 80), qcb_h), ((80, 144), kcw_h),
                              ((144, 160), kcb_h), ((160, 288), vcw_h), ((288, 320), vcb_h)):
            p.inc(sync.dma_start(convw[:, c0:c1], src), LD, 16)
        p.inc(sync.dma_start(stgf[0][:, 0:256], wa_h), LD, 16)
        p.inc(sync.dma_start(stgf[0][:, 256:512], wb_h), LD, 16)
        p.inc(sync.dma_start(stgf[1][:, 0:128], gw_h), LD, 16)
        p.inc(sync.dma_start(stgf[1][:, 128:144], se_h), LD, 16)
        p.wait(dv, LD)
        p.inc(dv.tensor_copy(wab[:], stgf[0][:, 0:512]), DV)
        p.inc(dv.tensor_copy(gwc[:], stgf[1][:, 0:128]), DV)
        p.inc(dv.tensor_copy(sec[:], stgf[1][:, 128:144]), DV)

        qcw = convw[:, 0:64].rearrange("p (m j) -> p m j", j=4)
        qcb = convw[:, 64:80]
        kcw = convw[:, 80:144].rearrange("p (m j) -> p m j", j=4)
        kcb = convw[:, 144:160]
        vcw = convw[:, 160:288].rearrange("p (m j) -> p m j", j=4)
        vcb = convw[:, 288:320]

        strip = WK3
        PTR = PB[:, 0:512].bitcast(BF16)
        p.wait(sync, DV)
        ld_mark = p.inc(sync.dma_start(xin[0][:], x_h[0:128, :]), LD, 16)
        for tt in range(NT):
            buf = xin[tt % 2]
            my_ld = ld_mark
            if tt + 1 < NT:
                if tt >= 1:
                    p.wait(sync, DV)
                ld_mark = p.inc(
                    sync.dma_start(xin[(tt + 1) % 2][:], x_h[(tt + 1) * 128:(tt + 2) * 128, :]),
                    LD, 16)
            p.wait(dv, LD, my_ld)
            p.inc(dv.reduce_sum(colt[:, 0:1], buf[:], axis=AX.X), DV)
            p.wait(dv, DV)
            p.inc(dv.tensor_scalar_mul(colt[:, 1:2], colt[:, 0:1], 1.0 / DM), DV)
            p.wait(dv, DV)
            p.wait(ac, LD, my_ld)
            p.inc(ac.activation(sqscr[:], buf[:], AF.Square, accum_out=colt[:, 2:3]), AC)
            p.wait(dv, AC)
            p.inc(dv.tensor_scalar_mul(colt[:, 3:4], colt[:, 2:3], 1.0 / DM), DV)
            p.inc(dv.tensor_tensor(colt[:, 4:5], colt[:, 1:2], colt[:, 1:2], op=ALU.mult), DV)
            p.wait(dv, DV)
            p.inc(dv.tensor_tensor(colt[:, 5:6], colt[:, 3:4], colt[:, 4:5], op=ALU.subtract), DV)
            p.wait(ac, DV)
            p.inc(ac.activation(colt[:, 6:7], colt[:, 5:6], AF.Sqrt, bias=fcn[:, 0:1]), AC)
            p.wait(dv, AC)
            p.inc(dv.reciprocal(colt[:, 7:8], colt[:, 6:7]), DV)
            p.wait(dv, DV)
            p.inc(dv.tensor_scalar(strip[:], buf[:], colt[:, 1:2], colt[:, 7:8],
                                   op0=ALU.subtract, op1=ALU.mult), DV)
            p.wait(te, DV)
            for half in range(2):
                for k8 in range(8):
                    kd = half * 8 + k8
                    ins = te.transpose(PTR[:, k8 * 128:(k8 + 1) * 128],
                                       strip[:, kd * 128:(kd + 1) * 128], idn16[:])
                p.inc(ins, TE)
                p.wait(dv, TE)
                dst = XNT.rearrange("p (kd t) -> p kd t", t=T)[
                    :, half * 8:(half + 1) * 8, tt * 128:(tt + 1) * 128]
                p.inc(dv.tensor_copy(dst, PTR.rearrange("p (k c) -> p k c", c=128)), DV)
                if half == 0:
                    p.wait(te, DV)
        peek_out("xnt", XNT[:, 0:4096])

        endphase("P1")
        PAB = PA[0:8, :]
        PABB = PA[64:72, :]
        for which in ("a", "b"):
            for grp, gn in GA:
                pab = PAB if grp == 0 else PABB
                for ng in range(4):
                    for kd in range(NKD):
                        off = (0 if which == "a" else 256) + kd * 16 + grp * 8
                        ins = te.matmul(pab[:, ng * 512:(ng + 1) * 512],
                                        wab[:, off:off + 8],
                                        XNT[:, kd * T + ng * 512:kd * T + (ng + 1) * 512],
                                        start=(kd == 0), stop=(kd == NKD - 1))
                    p.inc(ins, TE)
                p.wait(ac, TE)
                gr = grows(grp)
                if which == "a":
                    p.inc(ac.activation(TMP2[gr, :], pab[:], AF.Sigmoid), AC)
                    p.wait(ac, AC)
                    p.inc(ac.activation(LTv[gr, :], TMP2[gr, :], AF.Ln), AC)
                else:
                    p.inc(ac.activation(bTv[gr, :], pab[:], AF.Sigmoid), AC)
                p.wait(te, AC)
        p.wait(dv, AC)
        for grp, gn in GA:
            lt = LTv[grows(grp), :]
            for c in range(NCH):
                p.inc(dv.tensor_tensor_scan(
                    lt[:, c * CH:(c + 1) * CH], lt[:, c * CH:(c + 1) * CH],
                    lt[:, c * CH:(c + 1) * CH], 0.0, op0=ALU.add, op1=ALU.bypass), DV)
        peek_out("LA", LTv[0:8, 0:T])
        peek_out("bA", bTv[0:8, 0:T])

        endphase("P2")
        PSC = PB[:, 0:16]
        PTRN = PB[0:8, 512:1024]
        for tt in range(NT):
            for kd in range(NKD):
                ins = te.matmul(PSC[:, 0:8], XNT[:, kd * T + tt * 128:kd * T + (tt + 1) * 128],
                                gwc[:, kd * 8:(kd + 1) * 8], start=(kd == 0), stop=(kd == NKD - 1))
            p.inc(ins, TE)
            for kd in range(NKD):
                ins = te.matmul(PSC[:, 8:9], XNT[:, kd * T + tt * 128:kd * T + (tt + 1) * 128],
                                sec[:, kd:kd + 1], start=(kd == 0), stop=(kd == NKD - 1))
            p.inc(ins, TE)
            p.wait(dv, TE)
            c_ = colt
            p.inc(dv.tensor_copy(c_[:, 8:16], PSC[:, 0:8]), DV)
            p.wait(dv, DV)
            p.inc(dv.reduce_max(c_[:, 16:17], c_[:, 8:16], axis=AX.X), DV)
            p.wait(dv, DV)
            p.inc(dv.tensor_scalar(c_[:, 17:25], c_[:, 8:16], c_[:, 16:17], None,
                                   op0=ALU.is_equal), DV)
            p.wait(dv, DV)
            p.inc(dv.scalar_tensor_tensor(c_[:, 25:33], c_[:, 17:25], -1e30, c_[:, 8:16],
                                          op0=ALU.mult, op1=ALU.add), DV)
            p.wait(dv, DV)
            p.inc(dv.reduce_max(c_[:, 33:34], c_[:, 25:33], axis=AX.X), DV)
            p.wait(dv, DV)
            p.inc(dv.tensor_scalar(c_[:, 25:33], c_[:, 25:33], c_[:, 33:34], None,
                                   op0=ALU.is_equal), DV)
            p.inc(dv.tensor_tensor(c_[:, 34:35], c_[:, 16:17], c_[:, 33:34], op=ALU.subtract), DV)
            p.wait(dv, DV)
            p.wait(ac, DV)
            p.inc(ac.activation(c_[:, 35:36], c_[:, 34:35], AF.Sigmoid), AC)
            p.inc(ac.activation(c_[:, 36:37], PSC[:, 8:9], AF.Sigmoid), AC)
            p.wait(dv, AC)
            p.inc(dv.tensor_scalar(c_[:, 37:38], c_[:, 35:36], -1.0, 1.0,
                                   op0=ALU.mult, op1=ALU.add), DV)
            p.inc(dv.tensor_scalar_mul(c_[:, 17:25], c_[:, 17:25], c_[:, 35:36]), DV)
            p.wait(dv, DV)
            p.inc(dv.scalar_tensor_tensor(c_[:, 17:25], c_[:, 25:33], c_[:, 37:38],
                                          c_[:, 17:25], op0=ALU.mult, op1=ALU.add), DV)
            p.wait(te, DV)
            p.inc(te.transpose(PTRN[0:8, 0:128], c_[:, 17:25], idn32[:]), TE)
            p.inc(te.transpose(PTRN[0:1, 128:256], c_[:, 36:37], idn32[:]), TE)
            p.wait(dv, TE)
            p.inc(dv.tensor_copy(combT[:, tt * 128:(tt + 1) * 128], PTRN[0:8, 0:128]), DV)
            p.inc(dv.tensor_copy(sgrow[0:1, tt * 128:(tt + 1) * 128], PTRN[0:1, 128:256]), DV)
            p.wait(te, DV)
        p.inc(dv.tensor_copy(rowsg[0:1, :], sgrow[0:1, :]), DV)
        p.wait(te, DV)
        for ng in range(4):
            ins = te.matmul(PA[:, ng * 512:(ng + 1) * 512], ones_row[0:1, :],
                            rowsg[0:1, ng * 512:(ng + 1) * 512], start=True, stop=True)
        p.inc(ins, TE)
        p.wait(ac, TE)
        p.inc(ac.activation(sgbc[:], PA[:], AF.Copy), AC)
        p.wait(te, AC)
        peek_out("comb", combT[:, 0:T])

        endphase("P3")
        spill_mark = [0]
        t0b = WK1
        sq16 = WK2
        strip = WK3
        pvt = s1b_alias = None
        pvtb = SB("pvtb", [128, DM], BF16)
        stg16v = [SB(f"stg16v{i}", [128, DM], BF16) for i in range(2)]

        cast_mark = [0, 0]
        mm_mark = [0, 0]
        dma_mark = [0, 0]
        evac_mark = [0]

        def w_load(i, w_h, m):
            p.wait(sync, DV, cast_mark[i])
            dma_mark[i] = p.inc(sync.dma_start(stgf[i][:, :], w_h[m]), LD, 16)

        def w_cast(i):
            p.wait(dv, LD, dma_mark[i])
            p.wait(dv, TE, mm_mark[i])
            cast_mark[i] = p.inc(dv.tensor_copy(stg16[i][:, :], stgf[i][:, :]), DV)

        def proj_mms(i, dst):
            p.wait(te, DV, max(cast_mark[i], evac_mark[0]))
            for ng in range(4):
                for kd in range(NKD):
                    ins = te.matmul(dst[:, ng * 512:(ng + 1) * 512],
                                    stg16[i][:, kd * 128:(kd + 1) * 128],
                                    XNT[:, kd * T + ng * 512:kd * T + (ng + 1) * 512],
                                    start=(kd == 0), stop=(kd == NKD - 1))
            mm_mark[i] = p.inc(ins, TE)

        def conv_silu(m, wcol, bcol, sq_dst):
            p.wait(dv, TE, mm_mark[m % 2])
            evac_mark[0] = p.inc(dv.tensor_copy(pvtb[:, :], PA[:, :]), DV)
            p.wait(dv, AC)
            p.inc(dv.tensor_scalar(t0b[:, :], pvtb[:, :], wcol[:, m, 3:4], bcol[:, m:m + 1],
                                   op0=ALU.mult, op1=ALU.add), DV)
            for j, off in ((2, 1), (1, 2), (0, 3)):
                p.inc(dv.scalar_tensor_tensor(t0b[:, off:T], pvtb[:, 0:T - off],
                                              wcol[:, m, j:j + 1], t0b[:, off:T],
                                              op0=ALU.mult, op1=ALU.add), DV)
            p.wait(ac, DV)
            p.wait(ac, SP, spill_mark[0])
            p.inc(ac.activation(strip[:, :], t0b[:, :], AF.Silu), AC)
            if sq_dst is not None:
                p.wait(dv, AC)
                p.inc(dv.tensor_tensor(sq16[:, :], strip[:, :], strip[:, :], op=ALU.mult), DV)
                p.wait(te, DV)
                for tt in range(NT):
                    p.inc(te.matmul(sq_dst(tt), sq16[:, tt * 128:(tt + 1) * 128],
                                    ones_col[:], start=True, stop=True), TE)

        PSS = PB[:, 512:1024]

        p.wait(sync, DV); p.wait(sync, AC); p.wait(sync, TE)
        w_load(0, wq_h, 0)
        for m in range(HK):
            w_cast(m % 2)
            if m + 1 < HK:
                w_load((m + 1) % 2, wq_h, m + 1)
            proj_mms(m % 2, PA)
            conv_silu(m, qcw, qcb,
                      lambda tt, m=m: PSS[:, tt * 16 + m:tt * 16 + m + 1])
            p.wait(ac, AC)
            spill_mark[0] = p.inc(ac.dma_start(qT_sp[m * 128:(m + 1) * 128, :], strip[:, :]),
                                  SP, 16)
        peek_out("q15", strip[:, 0:T])

        endphase("P4")
        w_load(0, wk_h, 0)
        for m in range(HK):
            w_cast(m % 2)
            if m + 1 < HK:
                w_load((m + 1) % 2, wk_h, m + 1)
            proj_mms(m % 2, PA)
            conv_silu(m, kcw, kcb,
                      lambda tt, m=m: PSS[:, 256 + tt * 16 + m:256 + tt * 16 + m + 1])
            p.wait(ac, AC)
            spill_mark[0] = p.inc(ac.dma_start(kT_sp[m * 128:(m + 1) * 128, :], strip[:, :]),
                                  SP, 16)
        endphase("P5")
        ts0 = stgf[0][:, 0:512]
        ts1 = stgf[0][:, 512:1024]
        p.wait(dv, TE)
        p.wait(dv, SP, spill_mark[0])
        p.inc(dv.tensor_scalar_max(ts0, PSS[:], 1e-24), DV)
        p.wait(ac, DV)
        p.inc(ac.activation(ts1, ts0, AF.Ln), AC)
        p.wait(ac, AC)
        p.inc(ac.activation(ts0, ts1, AF.Exp, scale=fcn[:, 1:2]), AC)
        p.wait(dv, AC)
        p.inc(dv.tensor_copy(rq_tok, ts0[:, 0:256]), DV)
        p.inc(dv.tensor_copy(rk_tok, ts0[:, 256:512]), DV)
        PT8 = PB[:, 0:160]
        p.wait(te, DV)
        for tt in range(NT):
            for grp, gn in GA:
                gr = grows(grp)
                p.inc(te.transpose(PT8[0:8, 32:160], rk_tok[:, tt * 16 + grp * 8:
                                                            tt * 16 + grp * 8 + 8],
                                   idn32[:]), TE)
                p.wait(dv, TE)
                p.inc(dv.tensor_copy(rkTv[gr, tt * 128:(tt + 1) * 128], PT8[0:8, 32:160]), DV)
                p.wait(te, DV)
                p.inc(te.transpose(PT8[0:8, 32:160], rq_tok[:, tt * 16 + grp * 8:
                                                            tt * 16 + grp * 8 + 8],
                                   idn32[:]), TE)
                p.wait(dv, TE)
                p.inc(dv.tensor_copy(rqTv[gr, tt * 128:(tt + 1) * 128], PT8[0:8, 32:160]), DV)
                p.wait(te, DV)
        for tt in range(NT):
            for grp, gn in GA:
                gr = grows(grp)
                cs = tt * 16 + grp * 8
                p.inc(te.transpose(PT8[0:128, 32:40], bTv[gr, tt * 128:(tt + 1) * 128],
                                   idn32[gr, grp * 64:grp * 64 + 8]), TE)
                p.wait(dv, TE)
                p.inc(dv.tensor_copy(b_tok[:, cs:cs + 8], PT8[0:128, 32:40]), DV)
                p.wait(te, DV)
        for grp, gn in GA:
            gr = grows(grp)
            lt = LTv[gr, :]
            p.wait(ac, DV); p.wait(ac, TE)
            for c in range(NCH):
                mid = c * CH + CH // 2 - 1
                p.inc(ac.activation(TMP5[gr, c * CH:(c + 1) * CH], lt[:, c * CH:(c + 1) * CH],
                                    AF.Exp, scale=fcn[gr, 2:3], bias=lt[:, mid:mid + 1]), AC)
            p.wait(dv, AC)
            p.inc(dv.tensor_tensor(TMP5[gr, :], TMP5[gr, :], rkTv[gr, :], op=ALU.mult), DV)
            p.wait(dv, DV)
            p.inc(dv.tensor_copy(md16lrk[gr, :], TMP5[gr, :]), DV)
            p.wait(te, DV)
            for tt in range(NT):
                cs = tt * 16 + grp * 8
                p.inc(te.transpose(PT8[0:128, 32:40], TMP5[gr, tt * 128:(tt + 1) * 128],
                                   idn32[gr, grp * 64:grp * 64 + 8]), TE)
                p.wait(dv, TE)
                p.inc(dv.tensor_copy(lrk_tok[:, cs:cs + 8], PT8[0:128, 32:40]), DV)
                p.wait(te, DV)
            for c in range(NCH):
                mid = c * CH + CH // 2 - 1
                p.wait(dv, AC)
                p.inc(dv.tensor_scalar_mul(colt[gr, 38:39], lt[:, mid:mid + 1], -1.0), DV)
                p.wait(dv, DV)
                p.wait(ac, DV)
                p.inc(ac.activation(TMP2[gr, c * CH:(c + 1) * CH], lt[:, c * CH:(c + 1) * CH],
                                    AF.Exp, bias=colt[gr, 38:39]), AC)
            p.wait(dv, AC)
            p.inc(dv.tensor_tensor(TMP5[gr, :], TMP2[gr, :], bTv[gr, :], op=ALU.mult), DV)
            p.wait(dv, DV)
            p.inc(dv.tensor_tensor(TMP5[gr, :], TMP5[gr, :], rkTv[gr, :], op=ALU.mult), DV)
            p.wait(dv, DV)
            p.inc(dv.tensor_copy(md16bt[gr, :], TMP5[gr, :]), DV)
            p.wait(te, DV)
            for tt in range(NT):
                cs = tt * 16 + grp * 8
                p.inc(te.transpose(PT8[0:128, 32:40], TMP5[gr, tt * 128:(tt + 1) * 128],
                                   idn32[gr, grp * 64:grp * 64 + 8]), TE)
                p.wait(dv, TE)
                p.inc(dv.tensor_scalar_mul(nbt_tok[:, cs:cs + 8], PT8[0:128, 32:40], -1.0), DV)
                p.wait(te, DV)
            p.inc(dv.tensor_tensor(TMP2[gr, :], TMP2[gr, :], rqTv[gr, :], op=ALU.mult), DV)
            p.wait(dv, DV)
            p.wait(te, DV)
            for c in range(NCH):
                p.inc(te.transpose(PT8[0:64, 32:40],
                                   TMP2[gr, c * CH:(c + 1) * CH],
                                   idn32[gr, grp * 64:grp * 64 + 8]), TE)
                p.wait(dv, TE)
                dst = lq_dup[0:64, :].rearrange("p (h c) -> p h c", c=32)[
                    :, grp * 8:grp * 8 + 8, c]
                p.inc(dv.tensor_copy(dst, PT8[0:64, 32:40]), DV)
                p.wait(te, DV)
        for grp, gn in GA:
            gr = grows(grp)
            lt = LTv[gr, :]
            lc = lt.rearrange("p (c j) -> p c j", j=CH)[:, :, CH - 1]
            lm_ = lt.rearrange("p (c j) -> p c j", j=CH)[:, :, CH // 2 - 1]
            p.inc(dv.tensor_tensor(colt[gr, 0:32], lc, lm_, op=ALU.subtract), DV)
            p.inc(dv.memset(colt[gr, 32:36], 0.0), DV)
            nxt = lt[:, CH + CH // 2 - 1:T:CH]
            p.inc(dv.tensor_tensor(colt[gr, 0:31], colt[gr, 0:31], nxt, op=ALU.add), DV)
            p.wait(ac, DV)
            p.inc(ac.activation(colt[gr, 0:32], colt[gr, 0:32], AF.Exp), AC)
            p.wait(dv, AC)
            p.inc(dv.tensor_copy(sq16[gr, 0:32], colt[gr, 0:32]), DV)
            p.wait(te, DV)
            for h in range(8):
                p.inc(te.matmul(PB[0:1, grp * 256 + h * 32:grp * 256 + (h + 1) * 32],
                                i8c[gr, h:h + 1], sq16[gr, 0:32], start=True, stop=True), TE)
        p.wait(dv, TE)
        p.inc(dv.tensor_copy(rowb[0:1, 0:512], PB[0:1, 0:512]), DV)
        p.wait(te, DV)
        for ng in range(4):
            ins = te.matmul(PB[:, 512 + ng * 128:512 + (ng + 1) * 128], ones_row[0:1, :],
                            rowb[0:1, ng * 128:(ng + 1) * 128], start=True, stop=True)
        p.inc(ins, TE)
        p.wait(dv, TE)
        p.inc(dv.tensor_copy(scal[:, :], PB[:, 512:1024]), DV)
        peek_out("scal", scal[:, 0:512])
        peek_out("lrkA", md16lrk[0:8, 0:T])
        peek_out("btA", md16bt[0:8, 0:T])
        peek_out("lqdup", lq_dup[:, 0:512])
        peek_out("btok", tokmd[:, 0:1280])

        endphase("P6")
        PTRK = PA[:, 0:1024].bitcast(BF16)
        p.wait(dv, DV)
        for m in range(HK):
            p.wait(sync, DV); p.wait(sync, AC); p.wait(sync, SP)
            wld = p.inc(sync.dma_start(strip[:], kT_sp[m * 128:(m + 1) * 128, :]), LD, 16)
            p.wait(te, LD, wld)
            p.wait(te, DV); p.wait(te, AC)
            for tt in range(NT):
                ins = te.transpose(PTRK[:, tt * 128:(tt + 1) * 128],
                                   strip[:, tt * 128:(tt + 1) * 128], idn16[:])
            p.inc(ins, TE)
            p.wait(dv, TE); p.wait(ac, TE)
            for tt in range(NT):
                src = PTRK[:, tt * 128:(tt + 1) * 128]
                cK = lrk_tok[:, tt * 16 + m:tt * 16 + m + 1]
                cB = nbt_tok[:, tt * 16 + m:tt * 16 + m + 1]
                if tt < 8:
                    p.inc(dv.tensor_scalar_mul(stg16[0][:, tt * 128:(tt + 1) * 128], src, cK), DV)
                    p.inc(dv.tensor_scalar_mul(stg16[1][:, tt * 128:(tt + 1) * 128], src, cB), DV)
                else:
                    p.inc(ac.activation(stg16[0][:, tt * 128:(tt + 1) * 128], src,
                                        AF.Copy, scale=cK), AC)
                    p.inc(ac.activation(stg16[1][:, tt * 128:(tt + 1) * 128], src,
                                        AF.Copy, scale=cB), AC)
            p.wait(ac, DV); p.wait(ac, AC)
            km_dst = km_sp[m * T:(m + 1) * T, :].rearrange("(tt p) k -> p tt k", p=128)
            nbk_dst = nbk_sp[m * T:(m + 1) * T, :].rearrange("(tt p) k -> p tt k", p=128)
            p.wait(ac, AC)
            p.inc(ac.dma_start(km_dst, stg16[0][:].rearrange("p (tt k) -> p tt k", k=128)),
                  SP, 16)
            spill_mark[0] = p.inc(
                ac.dma_start(nbk_dst, stg16[1][:].rearrange("p (tt k) -> p tt k", k=128)),
                SP, 16)
            p.wait(dv, SP); p.wait(ac, SP)

        endphase("P7")
        PTRK8 = PB[:, 0:1024].bitcast(BF16)
        tr_mark = [0]
        p.wait(sync, DV); p.wait(sync, AC); p.wait(sync, TE)
        w_load(0, wv_h, 0)
        for m in range(NVT):
            w_cast(m % 2)
            if m + 1 < NVT:
                w_load((m + 1) % 2, wv_h, m + 1)
            proj_mms(m % 2, PA)
            h, vh = m // 2, m % 2
            p.wait(dv, TE, mm_mark[m % 2])
            evac_mark[0] = p.inc(dv.tensor_copy(pvtb[:, :], PA[:, :]), DV)
            p.wait(dv, AC)
            p.inc(dv.tensor_scalar(t0b[:, :], pvtb[:, :], vcw[:, m, 3:4], vcb[:, m:m + 1],
                                   op0=ALU.mult, op1=ALU.add), DV)
            for j, off in ((2, 1), (1, 2), (0, 3)):
                p.inc(dv.scalar_tensor_tensor(t0b[:, off:T], pvtb[:, 0:T - off],
                                              vcw[:, m, j:j + 1], t0b[:, off:T],
                                              op0=ALU.mult, op1=ALU.add), DV)
            p.wait(ac, DV)
            p.wait(ac, TE, tr_mark[0])
            p.inc(ac.activation(strip[:, :], t0b[:, :], AF.Silu), AC)
            p.wait(te, AC)
            p.wait(te, DV)
            for tt in range(NT):
                ins = te.transpose(PTRK8[:, tt * 128:(tt + 1) * 128],
                                   strip[:, tt * 128:(tt + 1) * 128], idn16[:])
            tr_mark[0] = p.inc(ins, TE)
            p.wait(dv, TE); p.wait(ac, TE)
            if vh == 0:
                p.wait(dv, SP, spill_mark[0]); p.wait(ac, SP, spill_mark[0])
            for tt in range(NT):
                src_ = PTRK8[:, tt * 128:(tt + 1) * 128]
                cB = b_tok[:, tt * 16 + h:tt * 16 + h + 1]
                dst = stg16v[vh][:, tt * 128:(tt + 1) * 128]
                if tt < 8:
                    p.inc(dv.tensor_scalar_mul(dst, src_, cB), DV)
                else:
                    p.inc(ac.activation(dst, src_, AF.Copy, scale=cB), AC)
            if vh == 1:
                p.wait(ac, DV); p.wait(ac, AC)
                dst0 = bv_sp[h * T:(h + 1) * T, 0:128].rearrange("(tt p) k -> p tt k", p=128)
                dst1 = bv_sp[h * T:(h + 1) * T, 128:256].rearrange("(tt p) k -> p tt k", p=128)
                p.inc(ac.dma_start(dst0, stg16v[0][:, :].rearrange("p (tt k) -> p tt k", k=128)),
                      SP, 16)
                spill_mark[0] = p.inc(
                    ac.dma_start(dst1, stg16v[1][:, :].rearrange("p (tt k) -> p tt k", k=128)),
                    SP, 16)
        p.wait(dv, SP); p.wait(ac, SP)

        endphase("P8")
        s1b, x3b, htb = WK1, WK2, WK3
        x1_mark = [0]
        x3_mark = [0]

        p.wait(sync, DV); p.wait(sync, AC); p.wait(sync, TE)
        w_load(0, wg_h.rearrange("e m p d -> (e m) p d"), 0)
        wgf = wg_h.rearrange("e m p d -> (e m) p d")
        wuf = wu_h.rearrange("e m p d -> (e m) p d")
        for e in range(E):
            p.wait(te, DV)
            p.wait(te, AC)
            for ng in range(4):
                ins = te.matmul(PB[0:1, ng * 512:(ng + 1) * 512], i8c[0:8, e:e + 1],
                                combT[0:8, ng * 512:(ng + 1) * 512], start=True, stop=True)
            p.inc(ins, TE)
            p.wait(dv, TE)
            p.inc(dv.tensor_copy(rowsg[0:1, :], PB[0:1, 0:2048]), DV)
            p.wait(dv, DV)
            p.wait(te, DV)
            for ng in range(4):
                ins = te.matmul(PA[:, ng * 512:(ng + 1) * 512], ones_row[0:1, :],
                                rowsg[0:1, ng * 512:(ng + 1) * 512], start=True, stop=True)
            p.inc(ins, TE)
            p.wait(dv, TE)
            evac_mark[0] = p.inc(dv.tensor_copy(combbc[:, :], PA[:, :]), DV)
            for m in range(8):
                gi = e * 8 + m
                w_cast(0)
                if gi + 1 < 64:
                    p.wait(sync, DV, cast_mark[0])
                    dma_mark[0] = p.inc(sync.dma_start(stgf[0][:, :], wgf[gi + 1]), LD, 16)
                p.wait(te, DV, max(cast_mark[0], evac_mark[0]))
                p.wait(te, AC, x1_mark[0])
                for ng in range(4):
                    for kd in range(NKD):
                        ins = te.matmul(PA[:, ng * 512:(ng + 1) * 512],
                                        stg16[0][:, kd * 128:(kd + 1) * 128],
                                        XNT[:, kd * T + ng * 512:kd * T + (ng + 1) * 512],
                                        start=(kd == 0), stop=(kd == NKD - 1))
                mma = mm_mark[0] = p.inc(ins, TE)
                if gi == 0:
                    p.wait(sync, DV, cast_mark[1])
                    dma_mark[1] = p.inc(sync.dma_start(stgf[1][:, :], wuf[gi]), LD, 16)
                w_cast(1)
                if gi + 1 < 64:
                    p.wait(sync, DV, cast_mark[1])
                    dma_mark[1] = p.inc(sync.dma_start(stgf[1][:, :], wuf[gi + 1]), LD, 16)
                p.wait(te, DV, max(cast_mark[1], x3_mark[0]))
                for ng in range(4):
                    for kd in range(NKD):
                        ins = te.matmul(PB[:, ng * 512:(ng + 1) * 512],
                                        stg16[1][:, kd * 128:(kd + 1) * 128],
                                        XNT[:, kd * T + ng * 512:kd * T + (ng + 1) * 512],
                                        start=(kd == 0), stop=(kd == NKD - 1))
                mmb = mm_mark[1] = p.inc(ins, TE)
                p.wait(ac, TE, mma)
                p.wait(ac, SP, spill_mark[0])
                x1_mark[0] = p.inc(ac.activation(s1b[:, :], PA[:, :], AF.Silu), AC)
                p.wait(dv, TE, mmb)
                p.wait(dv, SP, spill_mark[0])
                x3_mark[0] = p.inc(dv.tensor_tensor(x3b[:, :], PB[:, :], combbc[:, :],
                                                    op=ALU.mult), DV)
                p.wait(dv, AC, x1_mark[0])
                p.inc(dv.tensor_tensor(htb[:, :], s1b[:, :], x3b[:, :], op=ALU.mult), DV)
                p.wait(ac, DV)
                p.wait(ac, AC)
                spill_mark[0] = p.inc(
                    ac.dma_start(ht_sp[gi * 128:(gi + 1) * 128, :], htb[:, :]), SP, 16)
        peek_out("ht63", htb[:, 0:T])

        endphase("P9")
        p.wait(sync, DV, cast_mark[0])
        dma_mark[0] = p.inc(sync.dma_start(stgf[0][:, :], shg_h[0]), LD, 16)
        for m in range(16):
            w_cast(0)
            if m + 1 < 16:
                p.wait(sync, DV, cast_mark[0])
                dma_mark[0] = p.inc(sync.dma_start(stgf[0][:, :], shg_h[m + 1]), LD, 16)
            p.wait(te, DV, max(cast_mark[0], evac_mark[0]))
            p.wait(te, AC, x1_mark[0])
            for ng in range(4):
                for kd in range(NKD):
                    ins = te.matmul(PA[:, ng * 512:(ng + 1) * 512],
                                    stg16[0][:, kd * 128:(kd + 1) * 128],
                                    XNT[:, kd * T + ng * 512:kd * T + (ng + 1) * 512],
                                    start=(kd == 0), stop=(kd == NKD - 1))
            mma = mm_mark[0] = p.inc(ins, TE)
            if m == 0:
                p.wait(sync, DV, cast_mark[1])
                dma_mark[1] = p.inc(sync.dma_start(stgf[1][:, :], shu_h[0]), LD, 16)
            w_cast(1)
            if m + 1 < 16:
                p.wait(sync, DV, cast_mark[1])
                dma_mark[1] = p.inc(sync.dma_start(stgf[1][:, :], shu_h[m + 1]), LD, 16)
            p.wait(te, DV, max(cast_mark[1], x3_mark[0]))
            for ng in range(4):
                for kd in range(NKD):
                    ins = te.matmul(PB[:, ng * 512:(ng + 1) * 512],
                                    stg16[1][:, kd * 128:(kd + 1) * 128],
                                    XNT[:, kd * T + ng * 512:kd * T + (ng + 1) * 512],
                                    start=(kd == 0), stop=(kd == NKD - 1))
            mmb = mm_mark[1] = p.inc(ins, TE)
            p.wait(ac, TE, mma)
            p.wait(ac, SP, spill_mark[0])
            x1_mark[0] = p.inc(ac.activation(s1b[:, :], PA[:, :], AF.Silu), AC)
            p.wait(dv, TE, mmb)
            p.wait(dv, SP, spill_mark[0])
            x3_mark[0] = p.inc(dv.tensor_tensor(x3b[:, :], PB[:, :], sgbc[:, :],
                                                op=ALU.mult), DV)
            p.wait(dv, AC, x1_mark[0])
            p.inc(dv.tensor_tensor(htb[:, :], s1b[:, :], x3b[:, :], op=ALU.mult), DV)
            p.wait(ac, DV)
            p.wait(ac, AC)
            spill_mark[0] = p.inc(
                ac.dma_start(s_sp[m * 128:(m + 1) * 128, :], htb[:, :]), SP, 16)

        endphase("P10")
        for src_h, dst_sp, n in ((wd_h, wd_bf, E * 8), (shd_h, shd_bf, 16), (wo_h, wo_bf, 32)):
            for m in range(n):
                buf = stgf[m % 2]; buf16 = stg16[m % 2]
                p.wait(sync, SP)
                wld = p.inc(sync.dma_start(buf[:], src_h[m]), LD, 16)
                if m % 2 == 0:
                    p.wait(dv, LD, wld)
                    p.inc(dv.tensor_copy(buf16[:], buf[:]), DV)
                    p.wait(ac, DV)
                else:
                    p.wait(ac, LD, wld)
                    p.inc(ac.activation(buf16[:], buf[:], AF.Copy), AC)
                    p.wait(ac, AC)
                spill_mark[0] = p.inc(
                    ac.dma_start(dst_sp[m * 128:(m + 1) * 128, :], buf16[:]), SP, 16)
        p.wait(dv, SP); p.wait(ac, SP)

        endphase("P105")
        p.wait(dv, DV)
        p.inc(dv.memset(SR[:, :], 0.0), DV)
        p.inc(dv.memset(SR16[:, :], 0.0), DV)
        PD = PA[0:64, 0:1024]
        PKQ = PA[0:64, 1024:2048]
        PLAM = PB[0:64, 0:1024]
        PROW = PB[0:1, 1024:2048]
        PBB = PB[0:64, 0:1024]
        PTR2 = PB[0:64, 0:512].bitcast(BF16)
        PSQ = PB[0:64, 0:2048]
        PCB = PB[0:64, 0:1024]
        PG = PA[:, 0:1024]
        PUW = [PA[0:64, 0:2048], PB[0:64, 0:2048]]
        POW = [PA[0:64, 0:2048], PB[0:64, 0:2048]]
        PDLT = PA[:, 0:2048]
        POTR = PB[:, 0:512].bitcast(BF16)

        qk_mark = [0, 0]
        ch_mark = [0] * NCH

        def pref_quad(qd_):
            bq = qd_ % 2
            src_q = qT_sp.rearrange("(h p) t -> h p t", p=128)[:, :, qd_ * 256:(qd_ + 1) * 256]
            src_k = kT_sp.rearrange("(h p) t -> h p t", p=128)[:, :, qd_ * 256:(qd_ + 1) * 256]
            p.inc(sync.dma_start(qbuf[bq].rearrange("p (h t) -> p h t", t=256),
                                 src_q.rearrange("h p t -> p h t")), LD, 16)
            qk_mark[bq] = p.inc(
                sync.dma_start(kbuf[bq].rearrange("p (h t) -> p h t", t=256),
                               src_k.rearrange("h p t -> p h t")), LD, 16)

        def pref_chunk(c_):
            bcn = c_ % 2
            src = bv_sp.rearrange("(h t) k -> h t k", t=T)[:, c_ * CH:(c_ + 1) * CH, :]
            p.inc(sync.dma_start(
                bvb[bcn].rearrange("p (h k) -> p h k", k=VE),
                src.rearrange("h t k -> t h k")), LD, 16)
            src = km_sp.rearrange("(h t) k -> h t k", t=T)[:, c_ * CH:(c_ + 1) * CH, :]
            p.inc(sync.dma_start(
                kmb[bcn].rearrange("p (h k) -> p h k", k=DK),
                src.rearrange("h t k -> t h k")), LD, 16)
            src = nbk_sp.rearrange("(h t) k -> h t k", t=T)[:, c_ * CH:(c_ + 1) * CH, :]
            ch_mark[c_] = p.inc(sync.dma_start(
                nbkb[bcn].rearrange("p (h k) -> p h k", k=DK),
                src.rearrange("h t k -> t h k")), LD, 16)

        pref_quad(0)
        pref_chunk(0)
        pref_chunk(1)
        id64 = idn16[0:64, 0:64]

        for c in range(NCH):
            qd_, cq = c // 4, c % 4
            bcn = c % 2
            qb, kb = qbuf[qd_ % 2], kbuf[qd_ % 2]

            def ksl(h, cq=cq, kb=kb):
                return kb[:, h * 256 + cq * 64:h * 256 + cq * 64 + 64]

            def qsl(h, cq=cq, qb=qb):
                return qb[:, h * 256 + cq * 64:h * 256 + cq * 64 + 64]

            if cq == 3 and qd_ + 1 < 8:
                p.wait(sync, TE)
                pref_quad(qd_ + 1)
            p.wait(te, LD, max(qk_mark[qd_ % 2], ch_mark[c]))

            for grp, gn in GA:
                gr = grows(grp)
                p.inc(te.matmul(PLAM[:, grp * 512:(grp + 1) * 512],
                                md16lrk[gr, c * CH:(c + 1) * CH],
                                exp64[gr, :], start=True, stop=True), TE)
                for h in range(8):
                    p.inc(te.matmul(PROW[0:1, grp * 512 + h * 64:grp * 512 + (h + 1) * 64],
                                    i8c[gr, h:h + 1], md16bt[gr, c * CH:(c + 1) * CH],
                                    start=True, stop=True), TE)
            p.wait(dv, TE)
            p.inc(dv.tensor_tensor(lstr[:, :], PLAM, mstr[:], op=ALU.mult), DV)
            p.inc(dv.tensor_tensor(lincl[:, :], PLAM, mincl[:], op=ALU.mult), DV)
            p.inc(dv.tensor_copy(rowb[0:1, 0:1024], PROW), DV)
            p.wait(dv, DV)
            p.wait(te, DV)
            for half in range(2):
                p.inc(te.matmul(PBB[:, half * 512:(half + 1) * 512], ones_row[0:1, 0:64],
                                rowb[0:1, half * 512:(half + 1) * 512],
                                start=True, stop=True), TE)
            p.wait(ac, TE)
            p.inc(ac.activation(bbs[:, :], PBB, AF.Copy), AC)
            p.wait(te, AC)
            for h in range(HK):
                p.inc(te.matmul(PD[:, h * 64:(h + 1) * 64], ksl(h), ksl(h),
                                start=True, stop=True), TE)
                p.inc(te.matmul(PKQ[:, h * 64:(h + 1) * 64], ksl(h), qsl(h),
                                start=True, stop=True), TE)
            p.wait(dv, TE)
            p.inc(dv.tensor_tensor(Pcb[:, :], PD, lstr[:, :], op=ALU.mult), DV)
            p.wait(dv, DV)
            p.inc(dv.tensor_tensor(Pcb[:, :], Pcb[:, :], bbs[:, :], op=ALU.mult), DV)
            p.inc(dv.tensor_tensor(ATs[:, :], PKQ, lincl[:, :], op=ALU.mult), DV)
            if c == 0:
                peek_out("negMT0", Pcb[:, 0:1024])
            p.wait(te, DV)
            for h in range(HK):
                ins = te.transpose(PTR2[:, h * 64:(h + 1) * 64],
                                   Pcb[:, h * 64:(h + 1) * 64], id64)
            p.inc(ins, TE)
            p.wait(dv, TE)
            p.inc(dv.tensor_copy(PTcb[:, :], PTR2), DV)
            p.inc(dv.tensor_tensor(Tcb[:, :], Pcb[:, :], idup[:], op=ALU.add), DV)
            for lvl in range(5):
                p.wait(te, DV)
                for h in range(HK):
                    sl = slice(h * 64, (h + 1) * 64)
                    p.inc(te.matmul(PSQ[:, h * 128:h * 128 + 64], PTcb[:, sl],
                                    Pcb[:, sl], start=True, stop=True), TE)
                    p.inc(te.matmul(PSQ[:, h * 128 + 64:h * 128 + 128], Pcb[:, sl],
                                    PTcb[:, sl], start=True, stop=True), TE)
                p.wait(dv, TE)
                psq = PSQ.rearrange("p (h two j) -> p h two j", two=2, j=64)
                p.inc(dv.tensor_copy(
                    Pcb.rearrange("p (h j) -> p h j", j=64), psq[:, :, 0, :]), DV)
                p.inc(dv.tensor_copy(
                    PTcb.rearrange("p (h j) -> p h j", j=64), psq[:, :, 1, :]), DV)
                p.wait(te, DV)
                for h in range(HK):
                    sl = slice(h * 64, (h + 1) * 64)
                    p.inc(te.matmul(PCB[:, sl], PTcb[:, sl], Tcb[:, sl],
                                    start=True, stop=True), TE)
                p.wait(dv, TE)
                p.wait(dv, DV)
                p.inc(dv.tensor_tensor(Tcb[:, :], Tcb[:, :], PCB, op=ALU.add), DV)
            p.wait(te, DV)
            for h in range(HK):
                p.inc(te.matmul(PG[:, h * 64:(h + 1) * 64],
                                nbkb[bcn][:, h * 128:(h + 1) * 128],
                                Tcb[:, h * 64:(h + 1) * 64], start=True, stop=True), TE)
            p.wait(ac, TE)
            p.inc(ac.activation(Gts[:, :], PG, AF.Copy), AC)
            p.wait(te, AC)
            for wv in range(2):
                for hh in range(8):
                    h = wv * 8 + hh
                    p.inc(te.matmul(PUW[wv][:, hh * 256:(hh + 1) * 256],
                                    Tcb[:, h * 64:(h + 1) * 64],
                                    bvb[bcn][:, h * 256:(h + 1) * 256],
                                    start=True, stop=False), TE)
                    p.inc(te.matmul(PUW[wv][:, hh * 256:(hh + 1) * 256],
                                    Gts[:, h * 64:(h + 1) * 64],
                                    SR16[:, h * 256:(h + 1) * 256],
                                    start=False, stop=True), TE)
                p.wait(dv, TE)
                p.inc(dv.tensor_copy(Us[:, wv * 2048:(wv + 1) * 2048], PUW[wv]), DV)
            for wv in range(2):
                p.wait(te, DV)
                for hh in range(8):
                    h = wv * 8 + hh
                    p.inc(te.matmul(POW[wv][:, hh * 256:(hh + 1) * 256], qsl(h),
                                    SR16[:, h * 256:(h + 1) * 256], start=True, stop=False), TE)
                    p.inc(te.matmul(POW[wv][:, hh * 256:(hh + 1) * 256],
                                    ATs[:, h * 64:(h + 1) * 64],
                                    Us[:, h * 256:(h + 1) * 256],
                                    start=False, stop=True), TE)
                p.wait(dv, TE); p.wait(ac, TE)
                for hh in range(8):
                    h = wv * 8 + hh
                    colap = lq_dup[0:64, h * 32 + c:h * 32 + c + 1]
                    dst = obuf[:, h * 256:(h + 1) * 256]
                    if hh < 4:
                        p.inc(dv.tensor_scalar_mul(dst, POW[wv][:, hh * 256:(hh + 1) * 256],
                                                   colap), DV)
                    else:
                        p.inc(ac.activation(dst, POW[wv][:, hh * 256:(hh + 1) * 256],
                                            AF.Copy, scale=colap), AC)
            tq0 = (c % 4) * 64
            p.wait(te, DV); p.wait(te, AC)
            for rnd in range(2):
                for s in range(16):
                    vt = rnd * 16 + s
                    h, vh = vt // 2, vt % 2
                    ins = te.transpose(POTR[:, s * 64:(s + 1) * 64],
                                       obuf[:, h * 256 + vh * 128:h * 256 + (vh + 1) * 128],
                                       id64)
                p.inc(ins, TE)
                p.wait(dv, TE)
                dst = OTacc.rearrange("p (vt q) -> p vt q", q=OT_QT)[
                    :, rnd * 16:(rnd + 1) * 16, tq0:tq0 + 64]
                p.inc(dv.tensor_copy(
                    dst, POTR.rearrange("p (vt j) -> p vt j", j=64)[:, 0:16, :]), DV)
                p.wait(te, DV)
            for wv in range(2):
                p.wait(te, DV)
                for hh in range(8):
                    h = wv * 8 + hh
                    p.inc(te.matmul(PDLT[:, hh * 256:(hh + 1) * 256],
                                    kmb[bcn][:, h * 128:(h + 1) * 128],
                                    Us[:, h * 256:(h + 1) * 256],
                                    start=True, stop=True), TE)
                p.wait(dv, TE)
                p.inc(dv.tensor_tensor(SR[:, wv * 2048:(wv + 1) * 2048],
                                       SR[:, wv * 2048:(wv + 1) * 2048], PDLT,
                                       op=ALU.add), DV)
            p.wait(dv, DV)
            for h in range(HK):
                scol = scal[:, h * 32 + c:h * 32 + c + 1]
                p.inc(dv.tensor_scalar_mul(SR[:, h * 256:(h + 1) * 256],
                                           SR[:, h * 256:(h + 1) * 256], scol), DV)
            p.wait(dv, DV)
            p.wait(ac, DV)
            p.inc(ac.activation(SR16[:, :], SR[:, :], AF.Copy), AC)
            if c + 2 < NCH:
                p.wait(sync, TE)
                pref_chunk(c + 2)
            if c % 4 == 3:
                qd4 = c // 4
                p.wait(ac, DV)
                p.wait(ac, AC)
                spill_mark[0] = p.inc(ac.dma_start(
                    ot_sp.rearrange("(b vt p) q -> b p vt q", vt=32, p=128)[qd4],
                    OTacc.rearrange("p (vt q) -> p vt q", q=OT_QT)), SP, 16)
                if c + 1 < NCH:
                    p.wait(dv, SP)
            if c == 0:
                peek_out("TT0", Tcb[:, 0:1024])
                peek_out("U0", Us[:, 0:4096])
                peek_out("O0", obuf[:, 0:4096])
                peek_out("AT0", ATs[:, 0:1024])
        p.wait(dv, AC)
        peek_out("SR", SR[:, 0:4096])

        endphase("P11")
        nsrc = 112
        srcs = ([(wo_bf, vt) for vt in range(32)] + [(wd_bf, em) for em in range(64)]
                + [(shd_bf, m) for m in range(16)])
        NWB = 4
        wmark = [0] * NWB

        def wload(gi):
            dq_, si_ = gi // nsrc, gi % nsrc
            src, idx = srcs[si_]
            wmark[gi % NWB] = p.inc(sync.dma_start(
                wdn[gi % NWB][:, :], src[idx * 128:(idx + 1) * 128, dq_ * QT:(dq_ + 1) * QT]),
                LD, 16)

        for qd_ in range(QD):
            p.wait(sync, TE); p.wait(sync, SP)
            for hb in range(2):
                p.inc(sync.dma_start(
                    ACTS.rearrange("p (f t) -> p f t", t=QT)[:, 0:32, hb * 256:(hb + 1) * 256],
                    ot_sp.rearrange("(b vt p) q -> b p vt q", vt=32, p=128)[qd_ * 2 + hb]),
                    LD, 16)
            for h8 in range(4):
                p.inc(sync.dma_start(
                    ACTS.rearrange("p (f t) -> p f t", t=QT)[:, 32 + h8 * 16:32 + (h8 + 1) * 16, :],
                    ht_sp.rearrange("(f p) t -> p f t", p=128)[
                        :, h8 * 16:(h8 + 1) * 16, qd_ * QT:(qd_ + 1) * QT]), LD, 16)
            acts_ld = p.inc(sync.dma_start(
                ACTS.rearrange("p (f t) -> p f t", t=QT)[:, 96:112, :],
                s_sp.rearrange("(f p) t -> p f t", p=128)[:, :, qd_ * QT:(qd_ + 1) * QT]),
                LD, 16)
            for gi in range(NWB - 1):
                wload(gi)
            for dq in range(4):
                PQ = (PA if dq % 2 == 0 else PB)
                p.wait(te, DV)
                for si in range(nsrc):
                    gi = dq * nsrc + si
                    if gi + NWB - 1 < 4 * nsrc:
                        p.wait(sync, TE)
                        wload(gi + NWB - 1)
                    p.wait(te, LD, max(wmark[gi % NWB], acts_ld))
                    for sub in range(4):
                        p.inc(te.matmul(
                            PQ[:, sub * 512:(sub + 1) * 512],
                            ACTS[:, si * QT + sub * 128:si * QT + sub * 128 + 128],
                            wdn[gi % NWB][:, :], start=(si == 0), stop=(si == nsrc - 1)), TE)
                for sub in range(4):
                    tglob = qd_ * QT + sub * 128
                    p.wait(sync, DV)
                    wld = p.inc(sync.dma_start(
                        xres[:, :], x_h[tglob:tglob + 128, dq * QT:(dq + 1) * QT]), LD, 16)
                    p.wait(dv, LD, wld)
                    p.wait(dv, TE)
                    p.inc(dv.tensor_tensor(oute[:, :], PQ[:, sub * 512:(sub + 1) * 512],
                                           xres[:, :], op=ALU.add), DV)
                    p.wait(sync, DV)
                    p.inc(sync.dma_start(
                        out_h[tglob:tglob + 128, dq * QT:(dq + 1) * QT], oute[:, :]), LD, 16)

        p.wait(sync, LD)
        p.wait(sync, SP)
    except _Stop:
        sync.wait_ge(p.sem[LD], p.cnt[LD])
        sync.wait_ge(p.sem[SP], p.cnt[SP])
        stack.close()

    return nc


def kernel(**inputs):
    g = host_prep(inputs)
    x = np.asarray(inputs["L0_attn_x"], np.float32)
    nc = build()
    in_maps = []
    for core in range(8):
        m = dict(g)
        m["x"] = np.ascontiguousarray(x[core])
        in_maps.append(m)
    res = run_bass_kernel_spmd(nc, in_maps, core_ids=list(range(8)))
    out = np.stack([res.results[i]["out"] for i in range(8)])
    return out.astype(np.float32)




# version 06
# speedup vs baseline: 1.0734x; 1.0304x over previous
import numpy as np
import ml_dtypes
from contextlib import ExitStack

import concourse.bass as bass
import concourse.mybir as mybir
from concourse.bass_utils import run_bass_kernel_spmd

F32 = mybir.dt.float32
BF16 = mybir.dt.bfloat16
AF = mybir.ActivationFunctionType
ALU = mybir.AluOpType
AX = mybir.AxisListType
BF = ml_dtypes.bfloat16

T = 2048
DM = 2048
HK = 16
DK = 128
VE = 256
CH = 64
NCH = T // CH
E = 8
NT = T // 128
NKD = DM // 128
NVT = 4096 // 128
QD = 4
QT = T // QD


def _tile_w(w, nm):
    d = w.shape[0]
    nkd = d // 128
    return np.ascontiguousarray(
        w.reshape(nkd, 128, nm, 128).transpose(2, 1, 0, 3).reshape(nm, 128, nkd * 128))


def host_prep(inp):
    g = {}
    f32 = np.float32
    g["wq"] = _tile_w(np.asarray(inp["L0_attn_Wq"], f32), 16)
    g["wk"] = _tile_w(np.asarray(inp["L0_attn_Wk"], f32), 16)
    g["wv"] = _tile_w(np.asarray(inp["L0_attn_Wv"], f32), 32)
    for nm, key in (("wa", "L0_attn_Wa"), ("wb", "L0_attn_Wb")):
        w = np.asarray(inp[key], f32)
        g[nm] = np.ascontiguousarray(
            w.reshape(NKD, 128, 16).transpose(1, 0, 2).reshape(128, NKD * 16))
    for nm, wkey, bkey, ntile in (("qc", "L0_attn_qcw", "L0_attn_qcb", 16),
                                  ("kc", "L0_attn_kcw", "L0_attn_kcb", 16),
                                  ("vc", "L0_attn_vcw", "L0_attn_vcb", 32)):
        w = np.asarray(inp[wkey], f32).reshape(ntile, 128, 4).transpose(1, 0, 2)
        g[nm + "w"] = np.ascontiguousarray(w.reshape(128, ntile * 4))
        g[nm + "b"] = np.ascontiguousarray(np.asarray(inp[bkey], f32).reshape(ntile, 128).T)
    gw = np.asarray(inp["L0_ffn_gate_w"], f32).T
    g["gw"] = np.ascontiguousarray(gw.reshape(NKD, 128, 8).transpose(1, 0, 2).reshape(128, NKD * 8))
    se = np.asarray(inp["L0_ffn_se_gate_w"], f32).T
    g["se"] = np.ascontiguousarray(se.reshape(NKD, 128, 1).transpose(1, 0, 2).reshape(128, NKD))
    wg = np.asarray(inp["L0_ffn_w_gate"], f32)
    wu = np.asarray(inp["L0_ffn_w_up"], f32)
    g["wg"] = np.stack([_tile_w(wg[e].T, 8) for e in range(E)])
    g["wu"] = np.stack([_tile_w(wu[e].T, 8) for e in range(E)])
    g["shg"] = _tile_w(np.asarray(inp["L0_ffn_sh_gate"], f32).T, 16)
    g["shu"] = _tile_w(np.asarray(inp["L0_ffn_sh_up"], f32).T, 16)
    wd = np.asarray(inp["L0_ffn_w_down"], f32)
    g["wd"] = np.ascontiguousarray(
        np.stack([wd[e].T for e in range(E)]).reshape(E * 8, 128, DM))
    g["shd"] = np.ascontiguousarray(np.asarray(inp["L0_ffn_sh_down"], f32).T.reshape(16, 128, DM))
    g["wo"] = np.ascontiguousarray(np.asarray(inp["L0_attn_Wo"], f32).reshape(32, 128, DM))

    g["ident16"] = np.eye(128, dtype=BF)
    g["ident32"] = np.eye(128, dtype=f32)
    g["ones_col16"] = np.ones((128, 1), dtype=BF)
    g["ones_row16"] = np.ones((1, 128), dtype=BF)
    i8d = np.zeros((128, 8), dtype=BF)
    i8d[0:8] = np.eye(8)
    i8d[64:72] = np.eye(8)
    g["i8dup"] = i8d
    ju = np.arange(64)[:, None]
    iu = np.arange(64)[None, :]
    strict = -(iu > ju).astype(f32)
    incl = (iu >= ju).astype(f32)
    g["mask_strict"] = np.ascontiguousarray(np.tile(strict, (1, 16)))
    g["mask_incl"] = np.ascontiguousarray(np.tile(incl, (1, 16)))
    i64 = np.eye(64, dtype=BF)
    g["idup"] = np.ascontiguousarray(np.tile(i64, (1, 16)))
    ex = np.zeros((128, 512), dtype=BF)
    for h in range(8):
        ex[h, h * 64:(h + 1) * 64] = 1
        ex[64 + h, h * 64:(h + 1) * 64] = 1
    g["exp64"] = ex
    g["fconst"] = np.tile(np.array([1e-5, -0.5, -1.0, 0.0], f32), (128, 1))
    return g


class Prog:
    def __init__(self, nc, stack):
        self.nc = nc
        self.stack = stack
        self.sem = {}
        self.cnt = {}

    def sem_new(self, name):
        self.sem[name] = self.stack.enter_context(self.nc.semaphore(name))
        self.cnt[name] = 0

    def inc(self, instr, name, amount=1):
        instr.then_inc(self.sem[name], amount)
        self.cnt[name] += amount
        return self.cnt[name]

    def wait(self, eng, name, val=None):
        eng.wait_ge(self.sem[name], self.cnt[name] if val is None else val)


class _Stop(Exception):
    pass


def build(peek=(), last_phase=None):
    peek = set(peek)
    nc = bass.Bass("TRN2", target_bir_lowering=False)
    stack = ExitStack()
    p = Prog(nc, stack)
    sync, te, dv, ac, gp = nc.sync, nc.tensor, nc.vector, nc.scalar, nc.gpsimd
    LD, SP, TE, DV, AC = "ld", "sp", "te", "dv", "ac"

    def din(name, shape, dtype=F32):
        return nc.dram_tensor(name, list(shape), dtype, kind="ExternalInput").ap()

    x_h = din("x", [T, DM])
    wq_h = din("wq", [16, 128, DM]); wk_h = din("wk", [16, 128, DM]); wv_h = din("wv", [32, 128, DM])
    wa_h = din("wa", [128, NKD * 16]); wb_h = din("wb", [128, NKD * 16])
    qcw_h = din("qcw", [128, 64]); qcb_h = din("qcb", [128, 16])
    kcw_h = din("kcw", [128, 64]); kcb_h = din("kcb", [128, 16])
    vcw_h = din("vcw", [128, 128]); vcb_h = din("vcb", [128, 32])
    gw_h = din("gw", [128, NKD * 8]); se_h = din("se", [128, NKD])
    wg_h = din("wg", [E, 8, 128, DM]); wu_h = din("wu", [E, 8, 128, DM])
    shg_h = din("shg", [16, 128, DM]); shu_h = din("shu", [16, 128, DM])
    wd_h = din("wd", [E * 8, 128, DM]); shd_h = din("shd", [16, 128, DM])
    wo_h = din("wo", [32, 128, DM])
    ident16_h = din("ident16", [128, 128], BF16)
    ident32_h = din("ident32", [128, 128], F32)
    ones_col_h = din("ones_col16", [128, 1], BF16)
    ones_row_h = din("ones_row16", [1, 128], BF16)
    i8_h = din("i8dup", [128, 8], BF16)
    mstr_h = din("mask_strict", [64, 1024], F32)
    mincl_h = din("mask_incl", [64, 1024], F32)
    idup_h = din("idup", [64, 1024], BF16)
    exp64_h = din("exp64", [128, 512], BF16)
    fconst_h = din("fconst", [128, 4], F32)
    out_h = nc.dram_tensor("out", [T, DM], F32, kind="ExternalOutput").ap()

    def scr(name, shape, dtype=BF16):
        return nc.dram_tensor(name, list(shape), dtype, kind="Internal").ap()

    qT_sp = scr("qT_sp", [HK * 128, T])
    kT_sp = scr("kT_sp", [HK * 128, T])
    km_sp = scr("km_sp", [HK * T, DK])
    nbk_sp = scr("nbk_sp", [HK * T, DK])
    bv_sp = scr("bv_sp", [HK * T, VE])
    ht_sp = scr("ht_sp", [E * 8 * 128, T])
    s_sp = scr("s_sp", [16 * 128, T])
    ot_sp = scr("ot_sp", [8 * 32 * 128, 256])
    wd_bf = scr("wd_bf", [E * 8 * 128, DM])
    shd_bf = scr("shd_bf", [16 * 128, DM])
    wo_bf = scr("wo_bf", [32 * 128, DM])

    try:
      with stack:
        for s in (LD, SP, TE, DV, AC):
            p.sem_new(s)

        SB = lambda n, sh, dt: stack.enter_context(nc.sbuf_tensor(n, sh, dt))
        AR1 = SB("AR1", [128, 57344], BF16)
        AR2 = SB("AR2", [128, 12288], BF16)
        AR1f = AR1[:].bitcast(F32)
        AR2f = AR2[:].bitcast(F32)

        idn16 = SB("idn16", [128, 128], BF16)
        idn32 = SB("idn32", [128, 128], F32)
        ones_col = SB("ones_col", [128, 1], BF16)
        ones_row = SB("ones_row", [1, 128], BF16)
        i8c = SB("i8c", [128, 8], BF16)
        mstr = SB("mstr", [64, 1024], F32)
        mincl = SB("mincl", [64, 1024], F32)
        idup = SB("idup_c", [64, 1024], BF16)
        exp64 = SB("exp64_c", [128, 512], BF16)
        fcn = SB("fcn", [128, 4], F32)
        convw = SB("convw", [128, 320], F32)
        wab = SB("wab", [128, NKD * 32], BF16)
        gwc = SB("gwc", [128, NKD * 8], BF16)
        sec = SB("sec", [128, NKD], BF16)
        md16lrk = SB("md16lrk", [128, T], BF16)
        md16bt = SB("md16bt", [128, T], BF16)
        tokmd = SB("tokmd", [128, 1280], F32)
        b_tok = tokmd[:, 0:256]
        lrk_tok = tokmd[:, 256:512]
        nbt_tok = tokmd[:, 512:768]
        rk_tok = tokmd[:, 768:1024]
        rq_tok = tokmd[:, 1024:1280]
        lqscal = SB("lqscal", [128, 1024], F32)
        lq_dup = lqscal[:, 0:512]
        scal = lqscal[:, 512:1024]
        sgbc = SB("sgbc", [128, T], BF16)
        combT = SB("combT", [8, T], BF16)
        combbc = SB("combbc", [128, T], BF16)
        WK1 = SB("WK1", [128, DM], BF16)
        WK2 = SB("WK2", [128, DM], BF16)
        WK3 = SB("WK3", [128, DM], BF16)
        colt = SB("colt", [128, 40], F32)
        rowsg = SB("rowsg", [1, T], BF16)
        rowb = rowsg[0:1, 0:1024]

        stgf = [AR2f[:, i * 2048:(i + 1) * 2048] for i in range(2)]
        stg16 = [AR2[:, 8192 + i * 2048:8192 + (i + 1) * 2048] for i in range(2)]
        xin = stgf
        wdn = [AR2[:, i * QT:(i + 1) * QT] for i in range(4)]
        xres = AR2f[:, 1024:1536]
        oute = AR2f[:, 1536:2048]

        XNT = AR1[:, 0:32768]
        R = [AR1f[:, 16384 + i * 2048:16384 + (i + 1) * 2048] for i in range(6)]
        LTv, bTv, TMP2, rkTv, rqTv, TMP5 = R
        sqscr = R[0]
        sgrow = TMP2[0:1, :]

        def a1b(c0, n):
            return AR1[:, c0:c0 + n]

        def a1f(c0, n):
            return AR1f[:, c0 // 2:(c0 + n) // 2]

        qbuf = [a1b(0, 4096), a1b(4096, 4096)]
        kbuf = [a1b(8192, 4096), a1b(12288, 4096)]

        def a1b64(c0, n):
            return AR1[0:64, c0:c0 + n]

        bvb = [a1b64(16384, 4096), a1b64(20480, 4096)]
        kmb = [a1b64(24576, 2048), a1b64(26624, 2048)]
        nbkb = [a1b64(28672, 2048), a1b64(30720, 2048)]
        Us = a1b64(32768, 4096)
        obuf = a1b64(36864, 4096)
        Gts = a1b(40960, 1024)
        ATs = a1b64(41984, 1024)
        Tcb = a1b64(43008, 1024)
        Pcb = a1b64(44032, 1024)
        PTcb = a1b64(45056, 1024)
        lstr = AR1[0:64, 46080:47104]
        lincl = AR1[0:64, 47104:48128]
        bbs = AR1[0:64, 48128:49152]
        SR = a1f(49152, 8192)
        SR16 = AR2[:, 0:4096]
        OT_QT = 256
        OTacc = AR2[:, 4096:12288]
        ACTS = AR1[:, 0:112 * QT]

        PA = stack.enter_context(nc.psum_tensor("PA", [128, 2048], F32))
        PB = stack.enter_context(nc.psum_tensor("PB", [128, 2048], F32))

        def peek_out(tag, ap):
            if tag not in peek:
                return
            t = nc.dram_tensor("peek_" + tag, list(ap.shape), ap.dtype,
                               kind="ExternalOutput").ap()
            p.wait(sync, DV); p.wait(sync, AC); p.wait(sync, TE)
            p.inc(sync.dma_start(t, ap), LD, 16)

        GA = ((0, "A"), (1, "B"))

        def endphase(name):
            if last_phase == name:
                raise _Stop()


        def grows(grp, n=8):
            return slice(grp * 64, grp * 64 + n)

        for dst, src in ((idn16, ident16_h), (idn32, ident32_h), (ones_col, ones_col_h),
                         (ones_row, ones_row_h), (i8c, i8_h), (mstr, mstr_h),
                         (mincl, mincl_h), (idup, idup_h), (exp64, exp64_h),
                         (fcn, fconst_h)):
            p.inc(sync.dma_start(dst[:], src), LD, 16)
        for (c0, c1), src in (((0, 64), qcw_h), ((64, 80), qcb_h), ((80, 144), kcw_h),
                              ((144, 160), kcb_h), ((160, 288), vcw_h), ((288, 320), vcb_h)):
            p.inc(sync.dma_start(convw[:, c0:c1], src), LD, 16)
        p.inc(sync.dma_start(stgf[0][:, 0:256], wa_h), LD, 16)
        p.inc(sync.dma_start(stgf[0][:, 256:512], wb_h), LD, 16)
        p.inc(sync.dma_start(stgf[1][:, 0:128], gw_h), LD, 16)
        p.inc(sync.dma_start(stgf[1][:, 128:144], se_h), LD, 16)
        p.wait(dv, LD)
        p.inc(dv.tensor_copy(wab[:], stgf[0][:, 0:512]), DV)
        p.inc(dv.tensor_copy(gwc[:], stgf[1][:, 0:128]), DV)
        p.inc(dv.tensor_copy(sec[:], stgf[1][:, 128:144]), DV)

        qcw = convw[:, 0:64].rearrange("p (m j) -> p m j", j=4)
        qcb = convw[:, 64:80]
        kcw = convw[:, 80:144].rearrange("p (m j) -> p m j", j=4)
        kcb = convw[:, 144:160]
        vcw = convw[:, 160:288].rearrange("p (m j) -> p m j", j=4)
        vcb = convw[:, 288:320]

        strip = WK3
        PTR = PB[:, 0:512].bitcast(BF16)
        p.wait(sync, DV)
        ld_mark = p.inc(sync.dma_start(xin[0][:], x_h[0:128, :]), LD, 16)
        for tt in range(NT):
            buf = xin[tt % 2]
            my_ld = ld_mark
            if tt + 1 < NT:
                if tt >= 1:
                    p.wait(sync, DV)
                ld_mark = p.inc(
                    sync.dma_start(xin[(tt + 1) % 2][:], x_h[(tt + 1) * 128:(tt + 2) * 128, :]),
                    LD, 16)
            p.wait(dv, LD, my_ld)
            p.inc(dv.reduce_sum(colt[:, 0:1], buf[:], axis=AX.X), DV)
            p.wait(dv, DV)
            p.inc(dv.tensor_scalar_mul(colt[:, 1:2], colt[:, 0:1], 1.0 / DM), DV)
            p.wait(dv, DV)
            p.wait(ac, LD, my_ld)
            p.inc(ac.activation(sqscr[:], buf[:], AF.Square, accum_out=colt[:, 2:3]), AC)
            p.wait(dv, AC)
            p.inc(dv.tensor_scalar_mul(colt[:, 3:4], colt[:, 2:3], 1.0 / DM), DV)
            p.inc(dv.tensor_tensor(colt[:, 4:5], colt[:, 1:2], colt[:, 1:2], op=ALU.mult), DV)
            p.wait(dv, DV)
            p.inc(dv.tensor_tensor(colt[:, 5:6], colt[:, 3:4], colt[:, 4:5], op=ALU.subtract), DV)
            p.wait(ac, DV)
            p.inc(ac.activation(colt[:, 6:7], colt[:, 5:6], AF.Sqrt, bias=fcn[:, 0:1]), AC)
            p.wait(dv, AC)
            p.inc(dv.reciprocal(colt[:, 7:8], colt[:, 6:7]), DV)
            p.wait(dv, DV)
            p.inc(dv.tensor_scalar(strip[:], buf[:], colt[:, 1:2], colt[:, 7:8],
                                   op0=ALU.subtract, op1=ALU.mult), DV)
            p.wait(te, DV)
            for half in range(2):
                for k8 in range(8):
                    kd = half * 8 + k8
                    ins = te.transpose(PTR[:, k8 * 128:(k8 + 1) * 128],
                                       strip[:, kd * 128:(kd + 1) * 128], idn16[:])
                p.inc(ins, TE)
                p.wait(dv, TE)
                dst = XNT.rearrange("p (kd t) -> p kd t", t=T)[
                    :, half * 8:(half + 1) * 8, tt * 128:(tt + 1) * 128]
                p.inc(dv.tensor_copy(dst, PTR.rearrange("p (k c) -> p k c", c=128)), DV)
                if half == 0:
                    p.wait(te, DV)
        peek_out("xnt", XNT[:, 0:4096])

        endphase("P1")
        PAB = PA[0:8, :]
        PABB = PA[64:72, :]
        for which in ("a", "b"):
            for grp, gn in GA:
                pab = PAB if grp == 0 else PABB
                for kd in range(NKD):
                    for ng in range(4):
                        off = (0 if which == "a" else 256) + kd * 16 + grp * 8
                        ins = te.matmul(pab[:, ng * 512:(ng + 1) * 512],
                                        wab[:, off:off + 8],
                                        XNT[:, kd * T + ng * 512:kd * T + (ng + 1) * 512],
                                        start=(kd == 0), stop=(kd == NKD - 1))
                    p.inc(ins, TE)
                p.wait(ac, TE)
                gr = grows(grp)
                if which == "a":
                    p.inc(ac.activation(TMP2[gr, :], pab[:], AF.Sigmoid), AC)
                    p.wait(ac, AC)
                    p.inc(ac.activation(LTv[gr, :], TMP2[gr, :], AF.Ln), AC)
                else:
                    p.inc(ac.activation(bTv[gr, :], pab[:], AF.Sigmoid), AC)
                p.wait(te, AC)
        p.wait(dv, AC)
        for grp, gn in GA:
            lt = LTv[grows(grp), :]
            for c in range(NCH):
                p.inc(dv.tensor_tensor_scan(
                    lt[:, c * CH:(c + 1) * CH], lt[:, c * CH:(c + 1) * CH],
                    lt[:, c * CH:(c + 1) * CH], 0.0, op0=ALU.add, op1=ALU.bypass), DV)
        peek_out("LA", LTv[0:8, 0:T])
        peek_out("bA", bTv[0:8, 0:T])

        endphase("P2")
        PSC = PB[:, 0:16]
        PTRN = PB[0:8, 512:1024]
        for tt in range(NT):
            for kd in range(NKD):
                ins = te.matmul(PSC[:, 0:8], XNT[:, kd * T + tt * 128:kd * T + (tt + 1) * 128],
                                gwc[:, kd * 8:(kd + 1) * 8], start=(kd == 0), stop=(kd == NKD - 1))
            p.inc(ins, TE)
            for kd in range(NKD):
                ins = te.matmul(PSC[:, 8:9], XNT[:, kd * T + tt * 128:kd * T + (tt + 1) * 128],
                                sec[:, kd:kd + 1], start=(kd == 0), stop=(kd == NKD - 1))
            p.inc(ins, TE)
            p.wait(dv, TE)
            c_ = colt
            p.inc(dv.tensor_copy(c_[:, 8:16], PSC[:, 0:8]), DV)
            p.wait(dv, DV)
            p.inc(dv.reduce_max(c_[:, 16:17], c_[:, 8:16], axis=AX.X), DV)
            p.wait(dv, DV)
            p.inc(dv.tensor_scalar(c_[:, 17:25], c_[:, 8:16], c_[:, 16:17], None,
                                   op0=ALU.is_equal), DV)
            p.wait(dv, DV)
            p.inc(dv.scalar_tensor_tensor(c_[:, 25:33], c_[:, 17:25], -1e30, c_[:, 8:16],
                                          op0=ALU.mult, op1=ALU.add), DV)
            p.wait(dv, DV)
            p.inc(dv.reduce_max(c_[:, 33:34], c_[:, 25:33], axis=AX.X), DV)
            p.wait(dv, DV)
            p.inc(dv.tensor_scalar(c_[:, 25:33], c_[:, 25:33], c_[:, 33:34], None,
                                   op0=ALU.is_equal), DV)
            p.inc(dv.tensor_tensor(c_[:, 34:35], c_[:, 16:17], c_[:, 33:34], op=ALU.subtract), DV)
            p.wait(dv, DV)
            p.wait(ac, DV)
            p.inc(ac.activation(c_[:, 35:36], c_[:, 34:35], AF.Sigmoid), AC)
            p.inc(ac.activation(c_[:, 36:37], PSC[:, 8:9], AF.Sigmoid), AC)
            p.wait(dv, AC)
            p.inc(dv.tensor_scalar(c_[:, 37:38], c_[:, 35:36], -1.0, 1.0,
                                   op0=ALU.mult, op1=ALU.add), DV)
            p.inc(dv.tensor_scalar_mul(c_[:, 17:25], c_[:, 17:25], c_[:, 35:36]), DV)
            p.wait(dv, DV)
            p.inc(dv.scalar_tensor_tensor(c_[:, 17:25], c_[:, 25:33], c_[:, 37:38],
                                          c_[:, 17:25], op0=ALU.mult, op1=ALU.add), DV)
            p.wait(te, DV)
            p.inc(te.transpose(PTRN[0:8, 0:128], c_[:, 17:25], idn32[:]), TE)
            p.inc(te.transpose(PTRN[0:1, 128:256], c_[:, 36:37], idn32[:]), TE)
            p.wait(dv, TE)
            p.inc(dv.tensor_copy(combT[:, tt * 128:(tt + 1) * 128], PTRN[0:8, 0:128]), DV)
            p.inc(dv.tensor_copy(sgrow[0:1, tt * 128:(tt + 1) * 128], PTRN[0:1, 128:256]), DV)
            p.wait(te, DV)
        p.inc(dv.tensor_copy(rowsg[0:1, :], sgrow[0:1, :]), DV)
        p.wait(te, DV)
        for ng in range(4):
            ins = te.matmul(PA[:, ng * 512:(ng + 1) * 512], ones_row[0:1, :],
                            rowsg[0:1, ng * 512:(ng + 1) * 512], start=True, stop=True)
        p.inc(ins, TE)
        p.wait(ac, TE)
        p.inc(ac.activation(sgbc[:], PA[:], AF.Copy), AC)
        p.wait(te, AC)
        peek_out("comb", combT[:, 0:T])

        endphase("P3")
        spill_mark = [0]
        t0b = WK1
        sq16 = WK2
        strip = WK3
        pvt = s1b_alias = None
        pvtb = SB("pvtb", [128, DM], BF16)
        stg16v = [SB(f"stg16v{i}", [128, DM], BF16) for i in range(2)]

        cast_mark = [0, 0]
        mm_mark = [0, 0]
        dma_mark = [0, 0]
        evac_mark = [0]

        def w_load(i, w_h, m):
            p.wait(sync, DV, cast_mark[i])
            dma_mark[i] = p.inc(sync.dma_start(stgf[i][:, :], w_h[m]), LD, 16)

        def w_cast(i):
            p.wait(dv, LD, dma_mark[i])
            p.wait(dv, TE, mm_mark[i])
            cast_mark[i] = p.inc(dv.tensor_copy(stg16[i][:, :], stgf[i][:, :]), DV)

        def proj_mms(i, dst):
            p.wait(te, DV, max(cast_mark[i], evac_mark[0]))
            for kd in range(NKD):
                for ng in range(4):
                    ins = te.matmul(dst[:, ng * 512:(ng + 1) * 512],
                                    stg16[i][:, kd * 128:(kd + 1) * 128],
                                    XNT[:, kd * T + ng * 512:kd * T + (ng + 1) * 512],
                                    start=(kd == 0), stop=(kd == NKD - 1))
            mm_mark[i] = p.inc(ins, TE)

        def conv_silu(m, wcol, bcol, sq_dst):
            p.wait(dv, TE, mm_mark[m % 2])
            evac_mark[0] = p.inc(dv.tensor_copy(pvtb[:, :], PA[:, :]), DV)
            p.wait(dv, AC)
            p.inc(dv.tensor_scalar(t0b[:, :], pvtb[:, :], wcol[:, m, 3:4], bcol[:, m:m + 1],
                                   op0=ALU.mult, op1=ALU.add), DV)
            for j, off in ((2, 1), (1, 2), (0, 3)):
                p.inc(dv.scalar_tensor_tensor(t0b[:, off:T], pvtb[:, 0:T - off],
                                              wcol[:, m, j:j + 1], t0b[:, off:T],
                                              op0=ALU.mult, op1=ALU.add), DV)
            p.wait(ac, DV)
            p.wait(ac, SP, spill_mark[0])
            p.inc(ac.activation(strip[:, :], t0b[:, :], AF.Silu), AC)
            if sq_dst is not None:
                p.wait(dv, AC)
                p.inc(dv.tensor_tensor(sq16[:, :], strip[:, :], strip[:, :], op=ALU.mult), DV)
                p.wait(te, DV)
                for tt in range(NT):
                    p.inc(te.matmul(sq_dst(tt), sq16[:, tt * 128:(tt + 1) * 128],
                                    ones_col[:], start=True, stop=True), TE)

        PSS = PB[:, 512:1024]

        p.wait(sync, DV); p.wait(sync, AC); p.wait(sync, TE)
        w_load(0, wq_h, 0)
        for m in range(HK):
            w_cast(m % 2)
            if m + 1 < HK:
                w_load((m + 1) % 2, wq_h, m + 1)
            proj_mms(m % 2, PA)
            conv_silu(m, qcw, qcb,
                      lambda tt, m=m: PSS[:, tt * 16 + m:tt * 16 + m + 1])
            p.wait(ac, AC)
            spill_mark[0] = p.inc(ac.dma_start(qT_sp[m * 128:(m + 1) * 128, :], strip[:, :]),
                                  SP, 16)
        peek_out("q15", strip[:, 0:T])

        endphase("P4")
        w_load(0, wk_h, 0)
        for m in range(HK):
            w_cast(m % 2)
            if m + 1 < HK:
                w_load((m + 1) % 2, wk_h, m + 1)
            proj_mms(m % 2, PA)
            conv_silu(m, kcw, kcb,
                      lambda tt, m=m: PSS[:, 256 + tt * 16 + m:256 + tt * 16 + m + 1])
            p.wait(ac, AC)
            spill_mark[0] = p.inc(ac.dma_start(kT_sp[m * 128:(m + 1) * 128, :], strip[:, :]),
                                  SP, 16)
        endphase("P5")
        ts0 = stgf[0][:, 0:512]
        ts1 = stgf[0][:, 512:1024]
        p.wait(dv, TE)
        p.wait(dv, SP, spill_mark[0])
        p.inc(dv.tensor_scalar_max(ts0, PSS[:], 1e-24), DV)
        p.wait(ac, DV)
        p.inc(ac.activation(ts1, ts0, AF.Ln), AC)
        p.wait(ac, AC)
        p.inc(ac.activation(ts0, ts1, AF.Exp, scale=fcn[:, 1:2]), AC)
        p.wait(dv, AC)
        p.inc(dv.tensor_copy(rq_tok, ts0[:, 0:256]), DV)
        p.inc(dv.tensor_copy(rk_tok, ts0[:, 256:512]), DV)
        PT8 = PB[:, 0:160]
        p.wait(te, DV)
        for tt in range(NT):
            for grp, gn in GA:
                gr = grows(grp)
                p.inc(te.transpose(PT8[0:8, 32:160], rk_tok[:, tt * 16 + grp * 8:
                                                            tt * 16 + grp * 8 + 8],
                                   idn32[:]), TE)
                p.wait(dv, TE)
                p.inc(dv.tensor_copy(rkTv[gr, tt * 128:(tt + 1) * 128], PT8[0:8, 32:160]), DV)
                p.wait(te, DV)
                p.inc(te.transpose(PT8[0:8, 32:160], rq_tok[:, tt * 16 + grp * 8:
                                                            tt * 16 + grp * 8 + 8],
                                   idn32[:]), TE)
                p.wait(dv, TE)
                p.inc(dv.tensor_copy(rqTv[gr, tt * 128:(tt + 1) * 128], PT8[0:8, 32:160]), DV)
                p.wait(te, DV)
        for tt in range(NT):
            for grp, gn in GA:
                gr = grows(grp)
                cs = tt * 16 + grp * 8
                p.inc(te.transpose(PT8[0:128, 32:40], bTv[gr, tt * 128:(tt + 1) * 128],
                                   idn32[gr, grp * 64:grp * 64 + 8]), TE)
                p.wait(dv, TE)
                p.inc(dv.tensor_copy(b_tok[:, cs:cs + 8], PT8[0:128, 32:40]), DV)
                p.wait(te, DV)
        for grp, gn in GA:
            gr = grows(grp)
            lt = LTv[gr, :]
            p.wait(ac, DV); p.wait(ac, TE)
            for c in range(NCH):
                mid = c * CH + CH // 2 - 1
                p.inc(ac.activation(TMP5[gr, c * CH:(c + 1) * CH], lt[:, c * CH:(c + 1) * CH],
                                    AF.Exp, scale=fcn[gr, 2:3], bias=lt[:, mid:mid + 1]), AC)
            p.wait(dv, AC)
            p.inc(dv.tensor_tensor(TMP5[gr, :], TMP5[gr, :], rkTv[gr, :], op=ALU.mult), DV)
            p.wait(dv, DV)
            p.inc(dv.tensor_copy(md16lrk[gr, :], TMP5[gr, :]), DV)
            p.wait(te, DV)
            for tt in range(NT):
                cs = tt * 16 + grp * 8
                p.inc(te.transpose(PT8[0:128, 32:40], TMP5[gr, tt * 128:(tt + 1) * 128],
                                   idn32[gr, grp * 64:grp * 64 + 8]), TE)
                p.wait(dv, TE)
                p.inc(dv.tensor_copy(lrk_tok[:, cs:cs + 8], PT8[0:128, 32:40]), DV)
                p.wait(te, DV)
            for c in range(NCH):
                mid = c * CH + CH // 2 - 1
                p.wait(dv, AC)
                p.inc(dv.tensor_scalar_mul(colt[gr, 38:39], lt[:, mid:mid + 1], -1.0), DV)
                p.wait(dv, DV)
                p.wait(ac, DV)
                p.inc(ac.activation(TMP2[gr, c * CH:(c + 1) * CH], lt[:, c * CH:(c + 1) * CH],
                                    AF.Exp, bias=colt[gr, 38:39]), AC)
            p.wait(dv, AC)
            p.inc(dv.tensor_tensor(TMP5[gr, :], TMP2[gr, :], bTv[gr, :], op=ALU.mult), DV)
            p.wait(dv, DV)
            p.inc(dv.tensor_tensor(TMP5[gr, :], TMP5[gr, :], rkTv[gr, :], op=ALU.mult), DV)
            p.wait(dv, DV)
            p.inc(dv.tensor_copy(md16bt[gr, :], TMP5[gr, :]), DV)
            p.wait(te, DV)
            for tt in range(NT):
                cs = tt * 16 + grp * 8
                p.inc(te.transpose(PT8[0:128, 32:40], TMP5[gr, tt * 128:(tt + 1) * 128],
                                   idn32[gr, grp * 64:grp * 64 + 8]), TE)
                p.wait(dv, TE)
                p.inc(dv.tensor_scalar_mul(nbt_tok[:, cs:cs + 8], PT8[0:128, 32:40], -1.0), DV)
                p.wait(te, DV)
            p.inc(dv.tensor_tensor(TMP2[gr, :], TMP2[gr, :], rqTv[gr, :], op=ALU.mult), DV)
            p.wait(dv, DV)
            p.wait(te, DV)
            for c in range(NCH):
                p.inc(te.transpose(PT8[0:64, 32:40],
                                   TMP2[gr, c * CH:(c + 1) * CH],
                                   idn32[gr, grp * 64:grp * 64 + 8]), TE)
                p.wait(dv, TE)
                dst = lq_dup[0:64, :].rearrange("p (h c) -> p h c", c=32)[
                    :, grp * 8:grp * 8 + 8, c]
                p.inc(dv.tensor_copy(dst, PT8[0:64, 32:40]), DV)
                p.wait(te, DV)
        for grp, gn in GA:
            gr = grows(grp)
            lt = LTv[gr, :]
            lc = lt.rearrange("p (c j) -> p c j", j=CH)[:, :, CH - 1]
            lm_ = lt.rearrange("p (c j) -> p c j", j=CH)[:, :, CH // 2 - 1]
            p.inc(dv.tensor_tensor(colt[gr, 0:32], lc, lm_, op=ALU.subtract), DV)
            p.inc(dv.memset(colt[gr, 32:36], 0.0), DV)
            nxt = lt[:, CH + CH // 2 - 1:T:CH]
            p.inc(dv.tensor_tensor(colt[gr, 0:31], colt[gr, 0:31], nxt, op=ALU.add), DV)
            p.wait(ac, DV)
            p.inc(ac.activation(colt[gr, 0:32], colt[gr, 0:32], AF.Exp), AC)
            p.wait(dv, AC)
            p.inc(dv.tensor_copy(sq16[gr, 0:32], colt[gr, 0:32]), DV)
            p.wait(te, DV)
            for h in range(8):
                p.inc(te.matmul(PB[0:1, grp * 256 + h * 32:grp * 256 + (h + 1) * 32],
                                i8c[gr, h:h + 1], sq16[gr, 0:32], start=True, stop=True), TE)
        p.wait(dv, TE)
        p.inc(dv.tensor_copy(rowb[0:1, 0:512], PB[0:1, 0:512]), DV)
        p.wait(te, DV)
        for ng in range(4):
            ins = te.matmul(PB[:, 512 + ng * 128:512 + (ng + 1) * 128], ones_row[0:1, :],
                            rowb[0:1, ng * 128:(ng + 1) * 128], start=True, stop=True)
        p.inc(ins, TE)
        p.wait(dv, TE)
        p.inc(dv.tensor_copy(scal[:, :], PB[:, 512:1024]), DV)
        peek_out("scal", scal[:, 0:512])
        peek_out("lrkA", md16lrk[0:8, 0:T])
        peek_out("btA", md16bt[0:8, 0:T])
        peek_out("lqdup", lq_dup[:, 0:512])
        peek_out("btok", tokmd[:, 0:1280])

        endphase("P6")
        PTRK = PA[:, 0:1024].bitcast(BF16)
        p.wait(dv, DV)
        for m in range(HK):
            p.wait(sync, DV); p.wait(sync, AC); p.wait(sync, SP)
            wld = p.inc(sync.dma_start(strip[:], kT_sp[m * 128:(m + 1) * 128, :]), LD, 16)
            p.wait(te, LD, wld)
            p.wait(te, DV); p.wait(te, AC)
            for tt in range(NT):
                ins = te.transpose(PTRK[:, tt * 128:(tt + 1) * 128],
                                   strip[:, tt * 128:(tt + 1) * 128], idn16[:])
            p.inc(ins, TE)
            p.wait(dv, TE); p.wait(ac, TE)
            for tt in range(NT):
                src = PTRK[:, tt * 128:(tt + 1) * 128]
                cK = lrk_tok[:, tt * 16 + m:tt * 16 + m + 1]
                cB = nbt_tok[:, tt * 16 + m:tt * 16 + m + 1]
                if tt < 8:
                    p.inc(dv.tensor_scalar_mul(stg16[0][:, tt * 128:(tt + 1) * 128], src, cK), DV)
                    p.inc(dv.tensor_scalar_mul(stg16[1][:, tt * 128:(tt + 1) * 128], src, cB), DV)
                else:
                    p.inc(ac.activation(stg16[0][:, tt * 128:(tt + 1) * 128], src,
                                        AF.Copy, scale=cK), AC)
                    p.inc(ac.activation(stg16[1][:, tt * 128:(tt + 1) * 128], src,
                                        AF.Copy, scale=cB), AC)
            p.wait(ac, DV); p.wait(ac, AC)
            km_dst = km_sp[m * T:(m + 1) * T, :].rearrange("(tt p) k -> p tt k", p=128)
            nbk_dst = nbk_sp[m * T:(m + 1) * T, :].rearrange("(tt p) k -> p tt k", p=128)
            p.wait(ac, AC)
            p.inc(ac.dma_start(km_dst, stg16[0][:].rearrange("p (tt k) -> p tt k", k=128)),
                  SP, 16)
            spill_mark[0] = p.inc(
                ac.dma_start(nbk_dst, stg16[1][:].rearrange("p (tt k) -> p tt k", k=128)),
                SP, 16)
            p.wait(dv, SP); p.wait(ac, SP)

        endphase("P7")
        PTRK8 = PB[:, 0:1024].bitcast(BF16)
        tr_mark = [0]
        p.wait(sync, DV); p.wait(sync, AC); p.wait(sync, TE)
        w_load(0, wv_h, 0)
        for m in range(NVT):
            w_cast(m % 2)
            if m + 1 < NVT:
                w_load((m + 1) % 2, wv_h, m + 1)
            proj_mms(m % 2, PA)
            h, vh = m // 2, m % 2
            p.wait(dv, TE, mm_mark[m % 2])
            evac_mark[0] = p.inc(dv.tensor_copy(pvtb[:, :], PA[:, :]), DV)
            p.wait(dv, AC)
            p.inc(dv.tensor_scalar(t0b[:, :], pvtb[:, :], vcw[:, m, 3:4], vcb[:, m:m + 1],
                                   op0=ALU.mult, op1=ALU.add), DV)
            for j, off in ((2, 1), (1, 2), (0, 3)):
                p.inc(dv.scalar_tensor_tensor(t0b[:, off:T], pvtb[:, 0:T - off],
                                              vcw[:, m, j:j + 1], t0b[:, off:T],
                                              op0=ALU.mult, op1=ALU.add), DV)
            p.wait(ac, DV)
            p.wait(ac, TE, tr_mark[0])
            p.inc(ac.activation(strip[:, :], t0b[:, :], AF.Silu), AC)
            p.wait(te, AC)
            p.wait(te, DV)
            for tt in range(NT):
                ins = te.transpose(PTRK8[:, tt * 128:(tt + 1) * 128],
                                   strip[:, tt * 128:(tt + 1) * 128], idn16[:])
            tr_mark[0] = p.inc(ins, TE)
            p.wait(dv, TE); p.wait(ac, TE)
            if vh == 0:
                p.wait(dv, SP, spill_mark[0]); p.wait(ac, SP, spill_mark[0])
            for tt in range(NT):
                src_ = PTRK8[:, tt * 128:(tt + 1) * 128]
                cB = b_tok[:, tt * 16 + h:tt * 16 + h + 1]
                dst = stg16v[vh][:, tt * 128:(tt + 1) * 128]
                if tt < 8:
                    p.inc(dv.tensor_scalar_mul(dst, src_, cB), DV)
                else:
                    p.inc(ac.activation(dst, src_, AF.Copy, scale=cB), AC)
            if vh == 1:
                p.wait(ac, DV); p.wait(ac, AC)
                dst0 = bv_sp[h * T:(h + 1) * T, 0:128].rearrange("(tt p) k -> p tt k", p=128)
                dst1 = bv_sp[h * T:(h + 1) * T, 128:256].rearrange("(tt p) k -> p tt k", p=128)
                p.inc(ac.dma_start(dst0, stg16v[0][:, :].rearrange("p (tt k) -> p tt k", k=128)),
                      SP, 16)
                spill_mark[0] = p.inc(
                    ac.dma_start(dst1, stg16v[1][:, :].rearrange("p (tt k) -> p tt k", k=128)),
                    SP, 16)
        p.wait(dv, SP); p.wait(ac, SP)

        endphase("P8")
        s1b, x3b, htb = WK1, WK2, WK3
        x1_mark = [0]
        x3_mark = [0]

        p.wait(sync, DV); p.wait(sync, AC); p.wait(sync, TE)
        w_load(0, wg_h.rearrange("e m p d -> (e m) p d"), 0)
        wgf = wg_h.rearrange("e m p d -> (e m) p d")
        wuf = wu_h.rearrange("e m p d -> (e m) p d")
        for e in range(E):
            p.wait(te, DV)
            p.wait(te, AC)
            for ng in range(4):
                ins = te.matmul(PB[0:1, ng * 512:(ng + 1) * 512], i8c[0:8, e:e + 1],
                                combT[0:8, ng * 512:(ng + 1) * 512], start=True, stop=True)
            p.inc(ins, TE)
            p.wait(dv, TE)
            p.inc(dv.tensor_copy(rowsg[0:1, :], PB[0:1, 0:2048]), DV)
            p.wait(dv, DV)
            p.wait(te, DV)
            for ng in range(4):
                ins = te.matmul(PA[:, ng * 512:(ng + 1) * 512], ones_row[0:1, :],
                                rowsg[0:1, ng * 512:(ng + 1) * 512], start=True, stop=True)
            p.inc(ins, TE)
            p.wait(dv, TE)
            evac_mark[0] = p.inc(dv.tensor_copy(combbc[:, :], PA[:, :]), DV)
            for m in range(8):
                gi = e * 8 + m
                w_cast(0)
                if gi + 1 < 64:
                    p.wait(sync, DV, cast_mark[0])
                    dma_mark[0] = p.inc(sync.dma_start(stgf[0][:, :], wgf[gi + 1]), LD, 16)
                p.wait(te, DV, max(cast_mark[0], evac_mark[0]))
                p.wait(te, AC, x1_mark[0])
                for kd in range(NKD):
                    for ng in range(4):
                        ins = te.matmul(PA[:, ng * 512:(ng + 1) * 512],
                                        stg16[0][:, kd * 128:(kd + 1) * 128],
                                        XNT[:, kd * T + ng * 512:kd * T + (ng + 1) * 512],
                                        start=(kd == 0), stop=(kd == NKD - 1))
                mma = mm_mark[0] = p.inc(ins, TE)
                if gi == 0:
                    p.wait(sync, DV, cast_mark[1])
                    dma_mark[1] = p.inc(sync.dma_start(stgf[1][:, :], wuf[gi]), LD, 16)
                w_cast(1)
                if gi + 1 < 64:
                    p.wait(sync, DV, cast_mark[1])
                    dma_mark[1] = p.inc(sync.dma_start(stgf[1][:, :], wuf[gi + 1]), LD, 16)
                p.wait(te, DV, max(cast_mark[1], x3_mark[0]))
                for kd in range(NKD):
                    for ng in range(4):
                        ins = te.matmul(PB[:, ng * 512:(ng + 1) * 512],
                                        stg16[1][:, kd * 128:(kd + 1) * 128],
                                        XNT[:, kd * T + ng * 512:kd * T + (ng + 1) * 512],
                                        start=(kd == 0), stop=(kd == NKD - 1))
                mmb = mm_mark[1] = p.inc(ins, TE)
                p.wait(ac, TE, mma)
                p.wait(ac, SP, spill_mark[0])
                x1_mark[0] = p.inc(ac.activation(s1b[:, :], PA[:, :], AF.Silu), AC)
                p.wait(dv, TE, mmb)
                p.wait(dv, SP, spill_mark[0])
                x3_mark[0] = p.inc(dv.tensor_tensor(x3b[:, :], PB[:, :], combbc[:, :],
                                                    op=ALU.mult), DV)
                p.wait(dv, AC, x1_mark[0])
                p.inc(dv.tensor_tensor(htb[:, :], s1b[:, :], x3b[:, :], op=ALU.mult), DV)
                p.wait(ac, DV)
                p.wait(ac, AC)
                spill_mark[0] = p.inc(
                    ac.dma_start(ht_sp[gi * 128:(gi + 1) * 128, :], htb[:, :]), SP, 16)
        peek_out("ht63", htb[:, 0:T])

        endphase("P9")
        p.wait(sync, DV, cast_mark[0])
        dma_mark[0] = p.inc(sync.dma_start(stgf[0][:, :], shg_h[0]), LD, 16)
        for m in range(16):
            w_cast(0)
            if m + 1 < 16:
                p.wait(sync, DV, cast_mark[0])
                dma_mark[0] = p.inc(sync.dma_start(stgf[0][:, :], shg_h[m + 1]), LD, 16)
            p.wait(te, DV, max(cast_mark[0], evac_mark[0]))
            p.wait(te, AC, x1_mark[0])
            for kd in range(NKD):
                for ng in range(4):
                    ins = te.matmul(PA[:, ng * 512:(ng + 1) * 512],
                                    stg16[0][:, kd * 128:(kd + 1) * 128],
                                    XNT[:, kd * T + ng * 512:kd * T + (ng + 1) * 512],
                                    start=(kd == 0), stop=(kd == NKD - 1))
            mma = mm_mark[0] = p.inc(ins, TE)
            if m == 0:
                p.wait(sync, DV, cast_mark[1])
                dma_mark[1] = p.inc(sync.dma_start(stgf[1][:, :], shu_h[0]), LD, 16)
            w_cast(1)
            if m + 1 < 16:
                p.wait(sync, DV, cast_mark[1])
                dma_mark[1] = p.inc(sync.dma_start(stgf[1][:, :], shu_h[m + 1]), LD, 16)
            p.wait(te, DV, max(cast_mark[1], x3_mark[0]))
            for kd in range(NKD):
                for ng in range(4):
                    ins = te.matmul(PB[:, ng * 512:(ng + 1) * 512],
                                    stg16[1][:, kd * 128:(kd + 1) * 128],
                                    XNT[:, kd * T + ng * 512:kd * T + (ng + 1) * 512],
                                    start=(kd == 0), stop=(kd == NKD - 1))
            mmb = mm_mark[1] = p.inc(ins, TE)
            p.wait(ac, TE, mma)
            p.wait(ac, SP, spill_mark[0])
            x1_mark[0] = p.inc(ac.activation(s1b[:, :], PA[:, :], AF.Silu), AC)
            p.wait(dv, TE, mmb)
            p.wait(dv, SP, spill_mark[0])
            x3_mark[0] = p.inc(dv.tensor_tensor(x3b[:, :], PB[:, :], sgbc[:, :],
                                                op=ALU.mult), DV)
            p.wait(dv, AC, x1_mark[0])
            p.inc(dv.tensor_tensor(htb[:, :], s1b[:, :], x3b[:, :], op=ALU.mult), DV)
            p.wait(ac, DV)
            p.wait(ac, AC)
            spill_mark[0] = p.inc(
                ac.dma_start(s_sp[m * 128:(m + 1) * 128, :], htb[:, :]), SP, 16)

        endphase("P10")
        for src_h, dst_sp, n in ((wd_h, wd_bf, E * 8), (shd_h, shd_bf, 16), (wo_h, wo_bf, 32)):
            for m in range(n):
                buf = stgf[m % 2]; buf16 = stg16[m % 2]
                p.wait(sync, SP)
                wld = p.inc(sync.dma_start(buf[:], src_h[m]), LD, 16)
                if m % 2 == 0:
                    p.wait(dv, LD, wld)
                    p.inc(dv.tensor_copy(buf16[:], buf[:]), DV)
                    p.wait(ac, DV)
                else:
                    p.wait(ac, LD, wld)
                    p.inc(ac.activation(buf16[:], buf[:], AF.Copy), AC)
                    p.wait(ac, AC)
                spill_mark[0] = p.inc(
                    ac.dma_start(dst_sp[m * 128:(m + 1) * 128, :], buf16[:]), SP, 16)
        p.wait(dv, SP); p.wait(ac, SP)

        endphase("P105")
        p.wait(dv, DV)
        p.inc(dv.memset(SR[:, :], 0.0), DV)
        p.inc(dv.memset(SR16[:, :], 0.0), DV)
        PD = PA[0:64, 0:1024]
        PKQ = PA[0:64, 1024:2048]
        PLAM = PB[0:64, 0:1024]
        PROW = PB[0:1, 1024:2048]
        PBB = PB[0:64, 0:1024]
        PTR2 = PB[0:64, 0:512].bitcast(BF16)
        PSQ = PB[0:64, 0:2048]
        PCB = PB[0:64, 0:1024]
        PG = PA[:, 0:1024]
        PUW = [PA[0:64, 0:2048], PB[0:64, 0:2048]]
        POW = [PA[0:64, 0:2048], PB[0:64, 0:2048]]
        PDLT = PA[:, 0:2048]
        POTR = PB[:, 0:512].bitcast(BF16)

        qk_mark = [0, 0]
        ch_mark = [0] * NCH

        def pref_quad(qd_):
            bq = qd_ % 2
            src_q = qT_sp.rearrange("(h p) t -> h p t", p=128)[:, :, qd_ * 256:(qd_ + 1) * 256]
            src_k = kT_sp.rearrange("(h p) t -> h p t", p=128)[:, :, qd_ * 256:(qd_ + 1) * 256]
            p.inc(sync.dma_start(qbuf[bq].rearrange("p (h t) -> p h t", t=256),
                                 src_q.rearrange("h p t -> p h t")), LD, 16)
            qk_mark[bq] = p.inc(
                sync.dma_start(kbuf[bq].rearrange("p (h t) -> p h t", t=256),
                               src_k.rearrange("h p t -> p h t")), LD, 16)

        def pref_chunk(c_):
            bcn = c_ % 2
            src = bv_sp.rearrange("(h t) k -> h t k", t=T)[:, c_ * CH:(c_ + 1) * CH, :]
            p.inc(sync.dma_start(
                bvb[bcn].rearrange("p (h k) -> p h k", k=VE),
                src.rearrange("h t k -> t h k")), LD, 16)
            src = km_sp.rearrange("(h t) k -> h t k", t=T)[:, c_ * CH:(c_ + 1) * CH, :]
            p.inc(sync.dma_start(
                kmb[bcn].rearrange("p (h k) -> p h k", k=DK),
                src.rearrange("h t k -> t h k")), LD, 16)
            src = nbk_sp.rearrange("(h t) k -> h t k", t=T)[:, c_ * CH:(c_ + 1) * CH, :]
            ch_mark[c_] = p.inc(sync.dma_start(
                nbkb[bcn].rearrange("p (h k) -> p h k", k=DK),
                src.rearrange("h t k -> t h k")), LD, 16)

        pref_quad(0)
        pref_chunk(0)
        pref_chunk(1)
        id64 = idn16[0:64, 0:64]

        for c in range(NCH):
            qd_, cq = c // 4, c % 4
            bcn = c % 2
            qb, kb = qbuf[qd_ % 2], kbuf[qd_ % 2]

            def ksl(h, cq=cq, kb=kb):
                return kb[:, h * 256 + cq * 64:h * 256 + cq * 64 + 64]

            def qsl(h, cq=cq, qb=qb):
                return qb[:, h * 256 + cq * 64:h * 256 + cq * 64 + 64]

            if cq == 3 and qd_ + 1 < 8:
                p.wait(sync, TE)
                pref_quad(qd_ + 1)
            p.wait(te, LD, max(qk_mark[qd_ % 2], ch_mark[c]))

            for grp, gn in GA:
                gr = grows(grp)
                p.inc(te.matmul(PLAM[:, grp * 512:(grp + 1) * 512],
                                md16lrk[gr, c * CH:(c + 1) * CH],
                                exp64[gr, :], start=True, stop=True), TE)
                for h in range(8):
                    p.inc(te.matmul(PROW[0:1, grp * 512 + h * 64:grp * 512 + (h + 1) * 64],
                                    i8c[gr, h:h + 1], md16bt[gr, c * CH:(c + 1) * CH],
                                    start=True, stop=True), TE)
            p.wait(dv, TE)
            p.inc(dv.tensor_tensor(lstr[:, :], PLAM, mstr[:], op=ALU.mult), DV)
            p.inc(dv.tensor_tensor(lincl[:, :], PLAM, mincl[:], op=ALU.mult), DV)
            p.inc(dv.tensor_copy(rowb[0:1, 0:1024], PROW), DV)
            p.wait(dv, DV)
            p.wait(te, DV)
            for half in range(2):
                p.inc(te.matmul(PBB[:, half * 512:(half + 1) * 512], ones_row[0:1, 0:64],
                                rowb[0:1, half * 512:(half + 1) * 512],
                                start=True, stop=True), TE)
            p.wait(ac, TE)
            p.inc(ac.activation(bbs[:, :], PBB, AF.Copy), AC)
            p.wait(te, AC)
            for h in range(HK):
                p.inc(te.matmul(PD[:, h * 64:(h + 1) * 64], ksl(h), ksl(h),
                                start=True, stop=True), TE)
                p.inc(te.matmul(PKQ[:, h * 64:(h + 1) * 64], ksl(h), qsl(h),
                                start=True, stop=True), TE)
            p.wait(dv, TE)
            p.inc(dv.tensor_tensor(Pcb[:, :], PD, lstr[:, :], op=ALU.mult), DV)
            p.wait(dv, DV)
            p.inc(dv.tensor_tensor(Pcb[:, :], Pcb[:, :], bbs[:, :], op=ALU.mult), DV)
            p.inc(dv.tensor_tensor(ATs[:, :], PKQ, lincl[:, :], op=ALU.mult), DV)
            if c == 0:
                peek_out("negMT0", Pcb[:, 0:1024])
            p.wait(te, DV)
            for h in range(HK):
                ins = te.transpose(PTR2[:, h * 64:(h + 1) * 64],
                                   Pcb[:, h * 64:(h + 1) * 64], id64)
            p.inc(ins, TE)
            p.wait(dv, TE)
            p.inc(dv.tensor_copy(PTcb[:, :], PTR2), DV)
            p.inc(dv.tensor_tensor(Tcb[:, :], Pcb[:, :], idup[:], op=ALU.add), DV)
            for lvl in range(5):
                p.wait(te, DV)
                for h in range(HK):
                    sl = slice(h * 64, (h + 1) * 64)
                    p.inc(te.matmul(PSQ[:, h * 128:h * 128 + 64], PTcb[:, sl],
                                    Pcb[:, sl], start=True, stop=True), TE)
                    p.inc(te.matmul(PSQ[:, h * 128 + 64:h * 128 + 128], Pcb[:, sl],
                                    PTcb[:, sl], start=True, stop=True), TE)
                p.wait(dv, TE)
                psq = PSQ.rearrange("p (h two j) -> p h two j", two=2, j=64)
                p.inc(dv.tensor_copy(
                    Pcb.rearrange("p (h j) -> p h j", j=64), psq[:, :, 0, :]), DV)
                p.inc(dv.tensor_copy(
                    PTcb.rearrange("p (h j) -> p h j", j=64), psq[:, :, 1, :]), DV)
                p.wait(te, DV)
                for h in range(HK):
                    sl = slice(h * 64, (h + 1) * 64)
                    p.inc(te.matmul(PCB[:, sl], PTcb[:, sl], Tcb[:, sl],
                                    start=True, stop=True), TE)
                p.wait(dv, TE)
                p.wait(dv, DV)
                p.inc(dv.tensor_tensor(Tcb[:, :], Tcb[:, :], PCB, op=ALU.add), DV)
            p.wait(te, DV)
            for h in range(HK):
                p.inc(te.matmul(PG[:, h * 64:(h + 1) * 64],
                                nbkb[bcn][:, h * 128:(h + 1) * 128],
                                Tcb[:, h * 64:(h + 1) * 64], start=True, stop=True), TE)
            p.wait(ac, TE)
            p.inc(ac.activation(Gts[:, :], PG, AF.Copy), AC)
            p.wait(te, AC)
            for wv in range(2):
                for hh in range(8):
                    h = wv * 8 + hh
                    p.inc(te.matmul(PUW[wv][:, hh * 256:(hh + 1) * 256],
                                    Tcb[:, h * 64:(h + 1) * 64],
                                    bvb[bcn][:, h * 256:(h + 1) * 256],
                                    start=True, stop=False), TE)
                    p.inc(te.matmul(PUW[wv][:, hh * 256:(hh + 1) * 256],
                                    Gts[:, h * 64:(h + 1) * 64],
                                    SR16[:, h * 256:(h + 1) * 256],
                                    start=False, stop=True), TE)
                p.wait(dv, TE)
                p.inc(dv.tensor_copy(Us[:, wv * 2048:(wv + 1) * 2048], PUW[wv]), DV)
            for wv in range(2):
                p.wait(te, DV)
                for hh in range(8):
                    h = wv * 8 + hh
                    p.inc(te.matmul(POW[wv][:, hh * 256:(hh + 1) * 256], qsl(h),
                                    SR16[:, h * 256:(h + 1) * 256], start=True, stop=False), TE)
                    p.inc(te.matmul(POW[wv][:, hh * 256:(hh + 1) * 256],
                                    ATs[:, h * 64:(h + 1) * 64],
                                    Us[:, h * 256:(h + 1) * 256],
                                    start=False, stop=True), TE)
                p.wait(dv, TE); p.wait(ac, TE)
                for hh in range(8):
                    h = wv * 8 + hh
                    colap = lq_dup[0:64, h * 32 + c:h * 32 + c + 1]
                    dst = obuf[:, h * 256:(h + 1) * 256]
                    if hh < 4:
                        p.inc(dv.tensor_scalar_mul(dst, POW[wv][:, hh * 256:(hh + 1) * 256],
                                                   colap), DV)
                    else:
                        p.inc(ac.activation(dst, POW[wv][:, hh * 256:(hh + 1) * 256],
                                            AF.Copy, scale=colap), AC)
            tq0 = (c % 4) * 64
            p.wait(te, DV); p.wait(te, AC)
            for rnd in range(2):
                for s in range(16):
                    vt = rnd * 16 + s
                    h, vh = vt // 2, vt % 2
                    ins = te.transpose(POTR[:, s * 64:(s + 1) * 64],
                                       obuf[:, h * 256 + vh * 128:h * 256 + (vh + 1) * 128],
                                       id64)
                p.inc(ins, TE)
                p.wait(dv, TE)
                dst = OTacc.rearrange("p (vt q) -> p vt q", q=OT_QT)[
                    :, rnd * 16:(rnd + 1) * 16, tq0:tq0 + 64]
                p.inc(dv.tensor_copy(
                    dst, POTR.rearrange("p (vt j) -> p vt j", j=64)[:, 0:16, :]), DV)
                p.wait(te, DV)
            for wv in range(2):
                p.wait(te, DV)
                for hh in range(8):
                    h = wv * 8 + hh
                    p.inc(te.matmul(PDLT[:, hh * 256:(hh + 1) * 256],
                                    kmb[bcn][:, h * 128:(h + 1) * 128],
                                    Us[:, h * 256:(h + 1) * 256],
                                    start=True, stop=True), TE)
                p.wait(dv, TE)
                p.inc(dv.tensor_tensor(SR[:, wv * 2048:(wv + 1) * 2048],
                                       SR[:, wv * 2048:(wv + 1) * 2048], PDLT,
                                       op=ALU.add), DV)
            p.wait(dv, DV)
            for h in range(HK):
                scol = scal[:, h * 32 + c:h * 32 + c + 1]
                p.inc(dv.tensor_scalar_mul(SR[:, h * 256:(h + 1) * 256],
                                           SR[:, h * 256:(h + 1) * 256], scol), DV)
            p.wait(dv, DV)
            p.wait(ac, DV)
            p.inc(ac.activation(SR16[:, :], SR[:, :], AF.Copy), AC)
            if c + 2 < NCH:
                p.wait(sync, TE)
                pref_chunk(c + 2)
            if c % 4 == 3:
                qd4 = c // 4
                p.wait(ac, DV)
                p.wait(ac, AC)
                spill_mark[0] = p.inc(ac.dma_start(
                    ot_sp.rearrange("(b vt p) q -> b p vt q", vt=32, p=128)[qd4],
                    OTacc.rearrange("p (vt q) -> p vt q", q=OT_QT)), SP, 16)
                if c + 1 < NCH:
                    p.wait(dv, SP)
            if c == 0:
                peek_out("TT0", Tcb[:, 0:1024])
                peek_out("U0", Us[:, 0:4096])
                peek_out("O0", obuf[:, 0:4096])
                peek_out("AT0", ATs[:, 0:1024])
        p.wait(dv, AC)
        peek_out("SR", SR[:, 0:4096])

        endphase("P11")
        nsrc = 112
        srcs = ([(wo_bf, vt) for vt in range(32)] + [(wd_bf, em) for em in range(64)]
                + [(shd_bf, m) for m in range(16)])
        NWB = 4
        wmark = [0] * NWB

        def wload(gi):
            dq_, si_ = gi // nsrc, gi % nsrc
            src, idx = srcs[si_]
            wmark[gi % NWB] = p.inc(sync.dma_start(
                wdn[gi % NWB][:, :], src[idx * 128:(idx + 1) * 128, dq_ * QT:(dq_ + 1) * QT]),
                LD, 16)

        for qd_ in range(QD):
            p.wait(sync, TE); p.wait(sync, SP)
            for hb in range(2):
                p.inc(sync.dma_start(
                    ACTS.rearrange("p (f t) -> p f t", t=QT)[:, 0:32, hb * 256:(hb + 1) * 256],
                    ot_sp.rearrange("(b vt p) q -> b p vt q", vt=32, p=128)[qd_ * 2 + hb]),
                    LD, 16)
            for h8 in range(4):
                p.inc(sync.dma_start(
                    ACTS.rearrange("p (f t) -> p f t", t=QT)[:, 32 + h8 * 16:32 + (h8 + 1) * 16, :],
                    ht_sp.rearrange("(f p) t -> p f t", p=128)[
                        :, h8 * 16:(h8 + 1) * 16, qd_ * QT:(qd_ + 1) * QT]), LD, 16)
            acts_ld = p.inc(sync.dma_start(
                ACTS.rearrange("p (f t) -> p f t", t=QT)[:, 96:112, :],
                s_sp.rearrange("(f p) t -> p f t", p=128)[:, :, qd_ * QT:(qd_ + 1) * QT]),
                LD, 16)
            for gi in range(NWB - 1):
                wload(gi)
            for dq in range(4):
                PQ = (PA if dq % 2 == 0 else PB)
                p.wait(te, DV)
                for si in range(nsrc):
                    gi = dq * nsrc + si
                    if gi + NWB - 1 < 4 * nsrc:
                        p.wait(sync, TE)
                        wload(gi + NWB - 1)
                    p.wait(te, LD, max(wmark[gi % NWB], acts_ld))
                    for sub in range(4):
                        p.inc(te.matmul(
                            PQ[:, sub * 512:(sub + 1) * 512],
                            ACTS[:, si * QT + sub * 128:si * QT + sub * 128 + 128],
                            wdn[gi % NWB][:, :], start=(si == 0), stop=(si == nsrc - 1)), TE)
                for sub in range(4):
                    tglob = qd_ * QT + sub * 128
                    p.wait(sync, DV)
                    wld = p.inc(sync.dma_start(
                        xres[:, :], x_h[tglob:tglob + 128, dq * QT:(dq + 1) * QT]), LD, 16)
                    p.wait(dv, LD, wld)
                    p.wait(dv, TE)
                    p.inc(dv.tensor_tensor(oute[:, :], PQ[:, sub * 512:(sub + 1) * 512],
                                           xres[:, :], op=ALU.add), DV)
                    p.wait(sync, DV)
                    p.inc(sync.dma_start(
                        out_h[tglob:tglob + 128, dq * QT:(dq + 1) * QT], oute[:, :]), LD, 16)

        p.wait(sync, LD)
        p.wait(sync, SP)
    except _Stop:
        sync.wait_ge(p.sem[LD], p.cnt[LD])
        sync.wait_ge(p.sem[SP], p.cnt[SP])
        stack.close()

    return nc


def kernel(**inputs):
    g = host_prep(inputs)
    x = np.asarray(inputs["L0_attn_x"], np.float32)
    nc = build()
    in_maps = []
    for core in range(8):
        m = dict(g)
        m["x"] = np.ascontiguousarray(x[core])
        in_maps.append(m)
    res = run_bass_kernel_spmd(nc, in_maps, core_ids=list(range(8)))
    out = np.stack([res.results[i]["out"] for i in range(8)])
    return out.astype(np.float32)




# version 09
# speedup vs baseline: 1.0813x; 1.0073x over previous
import numpy as np
import ml_dtypes
from contextlib import ExitStack

import concourse.bass as bass
import concourse.mybir as mybir
from concourse.bass_utils import run_bass_kernel_spmd

F32 = mybir.dt.float32
BF16 = mybir.dt.bfloat16
AF = mybir.ActivationFunctionType
ALU = mybir.AluOpType
AX = mybir.AxisListType
BF = ml_dtypes.bfloat16

T = 2048
DM = 2048
HK = 16
DK = 128
VE = 256
CH = 64
NCH = T // CH
E = 8
NT = T // 128
NKD = DM // 128
NVT = 4096 // 128
QD = 4
QT = T // QD


def _tile_w(w, nm):
    d = w.shape[0]
    nkd = d // 128
    return np.ascontiguousarray(
        w.reshape(nkd, 128, nm, 128).transpose(2, 1, 0, 3).reshape(nm, 128, nkd * 128))


def host_prep(inp):
    g = {}
    f32 = np.float32
    g["wq"] = _tile_w(np.asarray(inp["L0_attn_Wq"], f32), 16)
    g["wk"] = _tile_w(np.asarray(inp["L0_attn_Wk"], f32), 16)
    g["wv"] = _tile_w(np.asarray(inp["L0_attn_Wv"], f32), 32)
    for nm, key in (("wa", "L0_attn_Wa"), ("wb", "L0_attn_Wb")):
        w = np.asarray(inp[key], f32)
        g[nm] = np.ascontiguousarray(
            w.reshape(NKD, 128, 16).transpose(1, 0, 2).reshape(128, NKD * 16))
    for nm, wkey, bkey, ntile in (("qc", "L0_attn_qcw", "L0_attn_qcb", 16),
                                  ("kc", "L0_attn_kcw", "L0_attn_kcb", 16),
                                  ("vc", "L0_attn_vcw", "L0_attn_vcb", 32)):
        w = np.asarray(inp[wkey], f32).reshape(ntile, 128, 4).transpose(1, 0, 2)
        g[nm + "w"] = np.ascontiguousarray(w.reshape(128, ntile * 4))
        g[nm + "b"] = np.ascontiguousarray(np.asarray(inp[bkey], f32).reshape(ntile, 128).T)
    gw = np.asarray(inp["L0_ffn_gate_w"], f32).T
    g["gw"] = np.ascontiguousarray(gw.reshape(NKD, 128, 8).transpose(1, 0, 2).reshape(128, NKD * 8))
    se = np.asarray(inp["L0_ffn_se_gate_w"], f32).T
    g["se"] = np.ascontiguousarray(se.reshape(NKD, 128, 1).transpose(1, 0, 2).reshape(128, NKD))
    wg = np.asarray(inp["L0_ffn_w_gate"], f32)
    wu = np.asarray(inp["L0_ffn_w_up"], f32)
    g["wg"] = np.stack([_tile_w(wg[e].T, 8) for e in range(E)])
    g["wu"] = np.stack([_tile_w(wu[e].T, 8) for e in range(E)])
    g["shg"] = _tile_w(np.asarray(inp["L0_ffn_sh_gate"], f32).T, 16)
    g["shu"] = _tile_w(np.asarray(inp["L0_ffn_sh_up"], f32).T, 16)
    wd = np.asarray(inp["L0_ffn_w_down"], f32)
    g["wd"] = np.ascontiguousarray(
        np.stack([wd[e].T for e in range(E)]).reshape(E * 8, 128, DM))
    g["shd"] = np.ascontiguousarray(np.asarray(inp["L0_ffn_sh_down"], f32).T.reshape(16, 128, DM))
    g["wo"] = np.ascontiguousarray(np.asarray(inp["L0_attn_Wo"], f32).reshape(32, 128, DM))

    g["ident16"] = np.eye(128, dtype=BF)
    g["ident32"] = np.eye(128, dtype=f32)
    g["ones_col16"] = np.ones((128, 1), dtype=BF)
    g["ones_row16"] = np.ones((1, 128), dtype=BF)
    i8d = np.zeros((128, 8), dtype=BF)
    i8d[0:8] = np.eye(8)
    i8d[64:72] = np.eye(8)
    g["i8dup"] = i8d
    ju = np.arange(64)[:, None]
    iu = np.arange(64)[None, :]
    strict = -(iu > ju).astype(f32)
    incl = (iu >= ju).astype(f32)
    g["mask_strict"] = np.ascontiguousarray(np.tile(strict, (1, 16)))
    g["mask_incl"] = np.ascontiguousarray(np.tile(incl, (1, 16)))
    i64 = np.eye(64, dtype=BF)
    g["idup"] = np.ascontiguousarray(np.tile(i64, (1, 16)))
    ex = np.zeros((128, 512), dtype=BF)
    for h in range(8):
        ex[h, h * 64:(h + 1) * 64] = 1
        ex[64 + h, h * 64:(h + 1) * 64] = 1
    g["exp64"] = ex
    g["fconst"] = np.tile(np.array([1e-5, -0.5, -1.0, 0.0], f32), (128, 1))
    return g


class Prog:
    def __init__(self, nc, stack):
        self.nc = nc
        self.stack = stack
        self.sem = {}
        self.cnt = {}

    def sem_new(self, name):
        self.sem[name] = self.stack.enter_context(self.nc.semaphore(name))
        self.cnt[name] = 0

    def inc(self, instr, name, amount=1):
        instr.then_inc(self.sem[name], amount)
        self.cnt[name] += amount
        return self.cnt[name]

    def wait(self, eng, name, val=None):
        eng.wait_ge(self.sem[name], self.cnt[name] if val is None else val)


class _Stop(Exception):
    pass


def build(peek=(), last_phase=None):
    peek = set(peek)
    nc = bass.Bass("TRN2", target_bir_lowering=False)
    stack = ExitStack()
    p = Prog(nc, stack)
    sync, te, dv, ac, gp = nc.sync, nc.tensor, nc.vector, nc.scalar, nc.gpsimd
    LD, SP, TE, DV, AC = "ld", "sp", "te", "dv", "ac"

    def din(name, shape, dtype=F32):
        return nc.dram_tensor(name, list(shape), dtype, kind="ExternalInput").ap()

    x_h = din("x", [T, DM])
    wq_h = din("wq", [16, 128, DM]); wk_h = din("wk", [16, 128, DM]); wv_h = din("wv", [32, 128, DM])
    wa_h = din("wa", [128, NKD * 16]); wb_h = din("wb", [128, NKD * 16])
    qcw_h = din("qcw", [128, 64]); qcb_h = din("qcb", [128, 16])
    kcw_h = din("kcw", [128, 64]); kcb_h = din("kcb", [128, 16])
    vcw_h = din("vcw", [128, 128]); vcb_h = din("vcb", [128, 32])
    gw_h = din("gw", [128, NKD * 8]); se_h = din("se", [128, NKD])
    wg_h = din("wg", [E, 8, 128, DM]); wu_h = din("wu", [E, 8, 128, DM])
    shg_h = din("shg", [16, 128, DM]); shu_h = din("shu", [16, 128, DM])
    wd_h = din("wd", [E * 8, 128, DM]); shd_h = din("shd", [16, 128, DM])
    wo_h = din("wo", [32, 128, DM])
    ident16_h = din("ident16", [128, 128], BF16)
    ident32_h = din("ident32", [128, 128], F32)
    ones_col_h = din("ones_col16", [128, 1], BF16)
    ones_row_h = din("ones_row16", [1, 128], BF16)
    i8_h = din("i8dup", [128, 8], BF16)
    mstr_h = din("mask_strict", [64, 1024], F32)
    mincl_h = din("mask_incl", [64, 1024], F32)
    idup_h = din("idup", [64, 1024], BF16)
    exp64_h = din("exp64", [128, 512], BF16)
    fconst_h = din("fconst", [128, 4], F32)
    out_h = nc.dram_tensor("out", [T, DM], F32, kind="ExternalOutput").ap()

    def scr(name, shape, dtype=BF16):
        return nc.dram_tensor(name, list(shape), dtype, kind="Internal").ap()

    qT_sp = scr("qT_sp", [HK * 128, T])
    kT_sp = scr("kT_sp", [HK * 128, T])
    km_sp = scr("km_sp", [HK * T, DK])
    nbk_sp = scr("nbk_sp", [HK * T, DK])
    bv_sp = scr("bv_sp", [HK * T, VE])
    ht_sp = scr("ht_sp", [E * 8 * 128, T])
    s_sp = scr("s_sp", [16 * 128, T])
    ot_sp = scr("ot_sp", [8 * 32 * 128, 256])
    wd_bf = scr("wd_bf", [E * 8 * 128, DM])
    shd_bf = scr("shd_bf", [16 * 128, DM])
    wo_bf = scr("wo_bf", [32 * 128, DM])

    try:
      with stack:
        for s in (LD, SP, TE, DV, AC):
            p.sem_new(s)

        SB = lambda n, sh, dt: stack.enter_context(nc.sbuf_tensor(n, sh, dt))
        AR1 = SB("AR1", [128, 57344], BF16)
        AR2 = SB("AR2", [128, 12288], BF16)
        AR1f = AR1[:].bitcast(F32)
        AR2f = AR2[:].bitcast(F32)

        idn16 = SB("idn16", [128, 128], BF16)
        idn32 = SB("idn32", [128, 128], F32)
        ones_col = SB("ones_col", [128, 1], BF16)
        ones_row = SB("ones_row", [1, 128], BF16)
        i8c = SB("i8c", [128, 8], BF16)
        mstr = SB("mstr", [64, 1024], F32)
        mincl = SB("mincl", [64, 1024], F32)
        idup = SB("idup_c", [64, 1024], BF16)
        exp64 = SB("exp64_c", [128, 512], BF16)
        fcn = SB("fcn", [128, 4], F32)
        convw = SB("convw", [128, 320], F32)
        wab = SB("wab", [128, NKD * 32], BF16)
        gwc = SB("gwc", [128, NKD * 8], BF16)
        sec = SB("sec", [128, NKD], BF16)
        md16lrk = SB("md16lrk", [128, T], BF16)
        md16bt = SB("md16bt", [128, T], BF16)
        tokmd = SB("tokmd", [128, 1280], F32)
        b_tok = tokmd[:, 0:256]
        lrk_tok = tokmd[:, 256:512]
        nbt_tok = tokmd[:, 512:768]
        rk_tok = tokmd[:, 768:1024]
        rq_tok = tokmd[:, 1024:1280]
        lqscal = SB("lqscal", [128, 1024], F32)
        lq_dup = lqscal[:, 0:512]
        scal = lqscal[:, 512:1024]
        sgbc = SB("sgbc", [128, T], BF16)
        combT = SB("combT", [8, T], BF16)
        combbc = SB("combbc", [128, T], BF16)
        WK1 = SB("WK1", [128, DM], BF16)
        WK2 = SB("WK2", [128, DM], BF16)
        WK3 = SB("WK3", [128, DM], BF16)
        colt = SB("colt", [128, 40], F32)
        rowsg = SB("rowsg", [1, T], BF16)
        rowb = rowsg[0:1, 0:1024]

        stgf = [AR2f[:, i * 2048:(i + 1) * 2048] for i in range(2)]
        stg16 = [AR2[:, 8192 + i * 2048:8192 + (i + 1) * 2048] for i in range(2)]
        xin = stgf
        wdn = [AR2[:, i * QT:(i + 1) * QT] for i in range(4)]
        xres = AR2f[:, 1024:1536]
        oute = AR2f[:, 1536:2048]

        XNT = AR1[:, 0:32768]
        R = [AR1f[:, 16384 + i * 2048:16384 + (i + 1) * 2048] for i in range(6)]
        LTv, bTv, TMP2, rkTv, rqTv, TMP5 = R
        sqscr = R[0]
        sgrow = TMP2[0:1, :]

        def a1b(c0, n):
            return AR1[:, c0:c0 + n]

        def a1f(c0, n):
            return AR1f[:, c0 // 2:(c0 + n) // 2]

        qbuf = [a1b(0, 4096), a1b(4096, 4096)]
        kbuf = [a1b(8192, 4096), a1b(12288, 4096)]

        def a1b64(c0, n):
            return AR1[0:64, c0:c0 + n]

        bvb = [a1b64(16384, 4096), a1b64(20480, 4096)]
        kmb = [a1b64(24576, 2048), a1b64(26624, 2048)]
        nbkb = [a1b64(28672, 2048), a1b64(30720, 2048)]
        Us = a1b64(32768, 4096)
        obuf = a1b64(36864, 4096)
        Gts = a1b(40960, 1024)
        ATs = a1b64(41984, 1024)
        Tcb = a1b64(43008, 1024)
        Pcb = a1b64(44032, 1024)
        PTcb = a1b64(45056, 1024)
        lstr = AR1[0:64, 46080:47104]
        lincl = AR1[0:64, 47104:48128]
        bbs = AR1[0:64, 48128:49152]
        SR = a1f(49152, 8192)
        SR16 = AR2[:, 0:4096]
        OT_QT = 256
        OTacc = AR2[:, 4096:12288]
        ACTS = AR1[:, 0:112 * QT]

        PA = stack.enter_context(nc.psum_tensor("PA", [128, 2048], F32))
        PB = stack.enter_context(nc.psum_tensor("PB", [128, 2048], F32))

        def peek_out(tag, ap):
            if tag not in peek:
                return
            t = nc.dram_tensor("peek_" + tag, list(ap.shape), ap.dtype,
                               kind="ExternalOutput").ap()
            p.wait(sync, DV); p.wait(sync, AC); p.wait(sync, TE)
            p.inc(sync.dma_start(t, ap), LD, 16)

        GA = ((0, "A"), (1, "B"))

        def endphase(name):
            if last_phase == name:
                raise _Stop()


        def grows(grp, n=8):
            return slice(grp * 64, grp * 64 + n)

        for dst, src in ((idn16, ident16_h), (idn32, ident32_h), (ones_col, ones_col_h),
                         (ones_row, ones_row_h), (i8c, i8_h), (mstr, mstr_h),
                         (mincl, mincl_h), (idup, idup_h), (exp64, exp64_h),
                         (fcn, fconst_h)):
            p.inc(sync.dma_start(dst[:], src), LD, 16)
        for (c0, c1), src in (((0, 64), qcw_h), ((64, 80), qcb_h), ((80, 144), kcw_h),
                              ((144, 160), kcb_h), ((160, 288), vcw_h), ((288, 320), vcb_h)):
            p.inc(sync.dma_start(convw[:, c0:c1], src), LD, 16)
        p.inc(sync.dma_start(stgf[0][:, 0:256], wa_h), LD, 16)
        p.inc(sync.dma_start(stgf[0][:, 256:512], wb_h), LD, 16)
        p.inc(sync.dma_start(stgf[1][:, 0:128], gw_h), LD, 16)
        p.inc(sync.dma_start(stgf[1][:, 128:144], se_h), LD, 16)
        p.wait(dv, LD)
        p.inc(dv.tensor_copy(wab[:], stgf[0][:, 0:512]), DV)
        p.inc(dv.tensor_copy(gwc[:], stgf[1][:, 0:128]), DV)
        p.inc(dv.tensor_copy(sec[:], stgf[1][:, 128:144]), DV)

        qcw = convw[:, 0:64].rearrange("p (m j) -> p m j", j=4)
        qcb = convw[:, 64:80]
        kcw = convw[:, 80:144].rearrange("p (m j) -> p m j", j=4)
        kcb = convw[:, 144:160]
        vcw = convw[:, 160:288].rearrange("p (m j) -> p m j", j=4)
        vcb = convw[:, 288:320]

        strip = WK3
        PTR = PB[:, 0:512].bitcast(BF16)
        p.wait(sync, DV)
        ld_mark = p.inc(sync.dma_start(xin[0][:], x_h[0:128, :]), LD, 16)
        for tt in range(NT):
            buf = xin[tt % 2]
            my_ld = ld_mark
            if tt + 1 < NT:
                if tt >= 1:
                    p.wait(sync, DV)
                ld_mark = p.inc(
                    sync.dma_start(xin[(tt + 1) % 2][:], x_h[(tt + 1) * 128:(tt + 2) * 128, :]),
                    LD, 16)
            p.wait(dv, LD, my_ld)
            p.inc(dv.reduce_sum(colt[:, 0:1], buf[:], axis=AX.X), DV)
            p.wait(dv, DV)
            p.inc(dv.tensor_scalar_mul(colt[:, 1:2], colt[:, 0:1], 1.0 / DM), DV)
            p.wait(dv, DV)
            p.wait(ac, LD, my_ld)
            p.inc(ac.activation(sqscr[:], buf[:], AF.Square, accum_out=colt[:, 2:3]), AC)
            p.wait(dv, AC)
            p.inc(dv.tensor_scalar_mul(colt[:, 3:4], colt[:, 2:3], 1.0 / DM), DV)
            p.inc(dv.tensor_tensor(colt[:, 4:5], colt[:, 1:2], colt[:, 1:2], op=ALU.mult), DV)
            p.wait(dv, DV)
            p.inc(dv.tensor_tensor(colt[:, 5:6], colt[:, 3:4], colt[:, 4:5], op=ALU.subtract), DV)
            p.wait(ac, DV)
            p.inc(ac.activation(colt[:, 6:7], colt[:, 5:6], AF.Sqrt, bias=fcn[:, 0:1]), AC)
            p.wait(dv, AC)
            p.inc(dv.reciprocal(colt[:, 7:8], colt[:, 6:7]), DV)
            p.wait(dv, DV)
            p.inc(dv.tensor_scalar(strip[:], buf[:], colt[:, 1:2], colt[:, 7:8],
                                   op0=ALU.subtract, op1=ALU.mult), DV)
            p.wait(te, DV)
            for half in range(2):
                for k8 in range(8):
                    kd = half * 8 + k8
                    ins = te.transpose(PTR[:, k8 * 128:(k8 + 1) * 128],
                                       strip[:, kd * 128:(kd + 1) * 128], idn16[:])
                p.inc(ins, TE)
                p.wait(dv, TE)
                dst = XNT.rearrange("p (kd t) -> p kd t", t=T)[
                    :, half * 8:(half + 1) * 8, tt * 128:(tt + 1) * 128]
                p.inc(dv.tensor_copy(dst, PTR.rearrange("p (k c) -> p k c", c=128)), DV)
                if half == 0:
                    p.wait(te, DV)
        peek_out("xnt", XNT[:, 0:4096])

        endphase("P1")
        PAB = PA[0:8, :]
        PABB = PA[64:72, :]
        for which in ("a", "b"):
            for grp, gn in GA:
                pab = PAB if grp == 0 else PABB
                for kd in range(NKD):
                    for ng in range(4):
                        off = (0 if which == "a" else 256) + kd * 16 + grp * 8
                        ins = te.matmul(pab[:, ng * 512:(ng + 1) * 512],
                                        wab[:, off:off + 8],
                                        XNT[:, kd * T + ng * 512:kd * T + (ng + 1) * 512],
                                        start=(kd == 0), stop=(kd == NKD - 1))
                    p.inc(ins, TE)
                p.wait(ac, TE)
                gr = grows(grp)
                if which == "a":
                    p.inc(ac.activation(TMP2[gr, :], pab[:], AF.Sigmoid), AC)
                    p.wait(ac, AC)
                    p.inc(ac.activation(LTv[gr, :], TMP2[gr, :], AF.Ln), AC)
                else:
                    p.inc(ac.activation(bTv[gr, :], pab[:], AF.Sigmoid), AC)
                p.wait(te, AC)
        p.wait(dv, AC)
        for grp, gn in GA:
            lt = LTv[grows(grp), :]
            for c in range(NCH):
                p.inc(dv.tensor_tensor_scan(
                    lt[:, c * CH:(c + 1) * CH], lt[:, c * CH:(c + 1) * CH],
                    lt[:, c * CH:(c + 1) * CH], 0.0, op0=ALU.add, op1=ALU.bypass), DV)
        peek_out("LA", LTv[0:8, 0:T])
        peek_out("bA", bTv[0:8, 0:T])

        endphase("P2")
        PSC = PB[:, 0:16]
        PTRN = PB[0:8, 512:1024]
        for tt in range(NT):
            for kd in range(NKD):
                ins = te.matmul(PSC[:, 0:8], XNT[:, kd * T + tt * 128:kd * T + (tt + 1) * 128],
                                gwc[:, kd * 8:(kd + 1) * 8], start=(kd == 0), stop=(kd == NKD - 1))
            p.inc(ins, TE)
            for kd in range(NKD):
                ins = te.matmul(PSC[:, 8:9], XNT[:, kd * T + tt * 128:kd * T + (tt + 1) * 128],
                                sec[:, kd:kd + 1], start=(kd == 0), stop=(kd == NKD - 1))
            p.inc(ins, TE)
            p.wait(dv, TE)
            c_ = colt
            p.inc(dv.tensor_copy(c_[:, 8:16], PSC[:, 0:8]), DV)
            p.wait(dv, DV)
            p.inc(dv.reduce_max(c_[:, 16:17], c_[:, 8:16], axis=AX.X), DV)
            p.wait(dv, DV)
            p.inc(dv.tensor_scalar(c_[:, 17:25], c_[:, 8:16], c_[:, 16:17], None,
                                   op0=ALU.is_equal), DV)
            p.wait(dv, DV)
            p.inc(dv.scalar_tensor_tensor(c_[:, 25:33], c_[:, 17:25], -1e30, c_[:, 8:16],
                                          op0=ALU.mult, op1=ALU.add), DV)
            p.wait(dv, DV)
            p.inc(dv.reduce_max(c_[:, 33:34], c_[:, 25:33], axis=AX.X), DV)
            p.wait(dv, DV)
            p.inc(dv.tensor_scalar(c_[:, 25:33], c_[:, 25:33], c_[:, 33:34], None,
                                   op0=ALU.is_equal), DV)
            p.inc(dv.tensor_tensor(c_[:, 34:35], c_[:, 16:17], c_[:, 33:34], op=ALU.subtract), DV)
            p.wait(dv, DV)
            p.wait(ac, DV)
            p.inc(ac.activation(c_[:, 35:36], c_[:, 34:35], AF.Sigmoid), AC)
            p.inc(ac.activation(c_[:, 36:37], PSC[:, 8:9], AF.Sigmoid), AC)
            p.wait(dv, AC)
            p.inc(dv.tensor_scalar(c_[:, 37:38], c_[:, 35:36], -1.0, 1.0,
                                   op0=ALU.mult, op1=ALU.add), DV)
            p.inc(dv.tensor_scalar_mul(c_[:, 17:25], c_[:, 17:25], c_[:, 35:36]), DV)
            p.wait(dv, DV)
            p.inc(dv.scalar_tensor_tensor(c_[:, 17:25], c_[:, 25:33], c_[:, 37:38],
                                          c_[:, 17:25], op0=ALU.mult, op1=ALU.add), DV)
            p.wait(te, DV)
            p.inc(te.transpose(PTRN[0:8, 0:128], c_[:, 17:25], idn32[:]), TE)
            p.inc(te.transpose(PTRN[0:1, 128:256], c_[:, 36:37], idn32[:]), TE)
            p.wait(dv, TE)
            p.inc(dv.tensor_copy(combT[:, tt * 128:(tt + 1) * 128], PTRN[0:8, 0:128]), DV)
            p.inc(dv.tensor_copy(sgrow[0:1, tt * 128:(tt + 1) * 128], PTRN[0:1, 128:256]), DV)
            p.wait(te, DV)
        p.inc(dv.tensor_copy(rowsg[0:1, :], sgrow[0:1, :]), DV)
        p.wait(te, DV)
        for ng in range(4):
            ins = te.matmul(PA[:, ng * 512:(ng + 1) * 512], ones_row[0:1, :],
                            rowsg[0:1, ng * 512:(ng + 1) * 512], start=True, stop=True)
        p.inc(ins, TE)
        p.wait(ac, TE)
        p.inc(ac.activation(sgbc[:], PA[:], AF.Copy), AC)
        p.wait(te, AC)
        peek_out("comb", combT[:, 0:T])

        endphase("P3")
        spill_mark = [0]
        t0b = WK1
        sq16 = WK2
        strip = WK3
        pvt = s1b_alias = None
        pvtb = SB("pvtb", [128, DM], BF16)
        stg16v = [SB(f"stg16v{i}", [128, DM], BF16) for i in range(2)]

        cast_mark = [0, 0]
        mm_mark = [0, 0]
        dma_mark = [0, 0]
        evac_mark = [0]

        def w_load(i, w_h, m):
            p.wait(sync, DV, cast_mark[i])
            dma_mark[i] = p.inc(sync.dma_start(stgf[i][:, :], w_h[m]), LD, 16)

        def w_cast(i):
            p.wait(dv, LD, dma_mark[i])
            p.wait(dv, TE, mm_mark[i])
            cast_mark[i] = p.inc(dv.tensor_copy(stg16[i][:, :], stgf[i][:, :]), DV)

        def proj_mms(i, dst):
            p.wait(te, DV, max(cast_mark[i], evac_mark[0]))
            for kd in range(NKD):
                for ng in range(4):
                    ins = te.matmul(dst[:, ng * 512:(ng + 1) * 512],
                                    stg16[i][:, kd * 128:(kd + 1) * 128],
                                    XNT[:, kd * T + ng * 512:kd * T + (ng + 1) * 512],
                                    start=(kd == 0), stop=(kd == NKD - 1))
            mm_mark[i] = p.inc(ins, TE)

        def conv_silu(m, wcol, bcol, sq_dst):
            p.wait(dv, TE, mm_mark[m % 2])
            evac_mark[0] = p.inc(dv.tensor_copy(pvtb[:, :], PA[:, :]), DV)
            p.wait(dv, AC)
            p.inc(dv.tensor_scalar(t0b[:, :], pvtb[:, :], wcol[:, m, 3:4], bcol[:, m:m + 1],
                                   op0=ALU.mult, op1=ALU.add), DV)
            for j, off in ((2, 1), (1, 2), (0, 3)):
                p.inc(dv.scalar_tensor_tensor(t0b[:, off:T], pvtb[:, 0:T - off],
                                              wcol[:, m, j:j + 1], t0b[:, off:T],
                                              op0=ALU.mult, op1=ALU.add), DV)
            p.wait(ac, DV)
            p.wait(ac, SP, spill_mark[0])
            p.inc(ac.activation(strip[:, :], t0b[:, :], AF.Silu), AC)
            if sq_dst is not None:
                p.wait(dv, AC)
                p.inc(dv.tensor_tensor(sq16[:, :], strip[:, :], strip[:, :], op=ALU.mult), DV)
                p.wait(te, DV)
                for tt in range(NT):
                    p.inc(te.matmul(sq_dst(tt), sq16[:, tt * 128:(tt + 1) * 128],
                                    ones_col[:], start=True, stop=True), TE)

        PSS = PB[:, 512:1024]

        p.wait(sync, DV); p.wait(sync, AC); p.wait(sync, TE)
        w_load(0, wq_h, 0)
        for m in range(HK):
            w_cast(m % 2)
            if m + 1 < HK:
                w_load((m + 1) % 2, wq_h, m + 1)
            proj_mms(m % 2, PA)
            conv_silu(m, qcw, qcb,
                      lambda tt, m=m: PSS[:, tt * 16 + m:tt * 16 + m + 1])
            p.wait(ac, AC)
            spill_mark[0] = p.inc(ac.dma_start(qT_sp[m * 128:(m + 1) * 128, :], strip[:, :]),
                                  SP, 16)
        peek_out("q15", strip[:, 0:T])

        endphase("P4")
        w_load(0, wk_h, 0)
        for m in range(HK):
            w_cast(m % 2)
            if m + 1 < HK:
                w_load((m + 1) % 2, wk_h, m + 1)
            proj_mms(m % 2, PA)
            conv_silu(m, kcw, kcb,
                      lambda tt, m=m: PSS[:, 256 + tt * 16 + m:256 + tt * 16 + m + 1])
            p.wait(ac, AC)
            spill_mark[0] = p.inc(ac.dma_start(kT_sp[m * 128:(m + 1) * 128, :], strip[:, :]),
                                  SP, 16)
        endphase("P5")
        ts0 = stgf[0][:, 0:512]
        ts1 = stgf[0][:, 512:1024]
        p.wait(dv, TE)
        p.wait(dv, SP, spill_mark[0])
        p.inc(dv.tensor_scalar_max(ts0, PSS[:], 1e-24), DV)
        p.wait(ac, DV)
        p.inc(ac.activation(ts1, ts0, AF.Ln), AC)
        p.wait(ac, AC)
        p.inc(ac.activation(ts0, ts1, AF.Exp, scale=fcn[:, 1:2]), AC)
        p.wait(dv, AC)
        p.inc(dv.tensor_copy(rq_tok, ts0[:, 0:256]), DV)
        p.inc(dv.tensor_copy(rk_tok, ts0[:, 256:512]), DV)
        PT8 = PB[:, 0:160]
        p.wait(te, DV)
        for tt in range(NT):
            for grp, gn in GA:
                gr = grows(grp)
                p.inc(te.transpose(PT8[0:8, 32:160], rk_tok[:, tt * 16 + grp * 8:
                                                            tt * 16 + grp * 8 + 8],
                                   idn32[:]), TE)
                p.wait(dv, TE)
                p.inc(dv.tensor_copy(rkTv[gr, tt * 128:(tt + 1) * 128], PT8[0:8, 32:160]), DV)
                p.wait(te, DV)
                p.inc(te.transpose(PT8[0:8, 32:160], rq_tok[:, tt * 16 + grp * 8:
                                                            tt * 16 + grp * 8 + 8],
                                   idn32[:]), TE)
                p.wait(dv, TE)
                p.inc(dv.tensor_copy(rqTv[gr, tt * 128:(tt + 1) * 128], PT8[0:8, 32:160]), DV)
                p.wait(te, DV)
        for tt in range(NT):
            for grp, gn in GA:
                gr = grows(grp)
                cs = tt * 16 + grp * 8
                p.inc(te.transpose(PT8[0:128, 32:40], bTv[gr, tt * 128:(tt + 1) * 128],
                                   idn32[gr, grp * 64:grp * 64 + 8]), TE)
                p.wait(dv, TE)
                p.inc(dv.tensor_copy(b_tok[:, cs:cs + 8], PT8[0:128, 32:40]), DV)
                p.wait(te, DV)
        for grp, gn in GA:
            gr = grows(grp)
            lt = LTv[gr, :]
            p.wait(ac, DV); p.wait(ac, TE)
            for c in range(NCH):
                mid = c * CH + CH // 2 - 1
                p.inc(ac.activation(TMP5[gr, c * CH:(c + 1) * CH], lt[:, c * CH:(c + 1) * CH],
                                    AF.Exp, scale=fcn[gr, 2:3], bias=lt[:, mid:mid + 1]), AC)
            p.wait(dv, AC)
            p.inc(dv.tensor_tensor(TMP5[gr, :], TMP5[gr, :], rkTv[gr, :], op=ALU.mult), DV)
            p.wait(dv, DV)
            p.inc(dv.tensor_copy(md16lrk[gr, :], TMP5[gr, :]), DV)
            p.wait(te, DV)
            for tt in range(NT):
                cs = tt * 16 + grp * 8
                p.inc(te.transpose(PT8[0:128, 32:40], TMP5[gr, tt * 128:(tt + 1) * 128],
                                   idn32[gr, grp * 64:grp * 64 + 8]), TE)
                p.wait(dv, TE)
                p.inc(dv.tensor_copy(lrk_tok[:, cs:cs + 8], PT8[0:128, 32:40]), DV)
                p.wait(te, DV)
            for c in range(NCH):
                mid = c * CH + CH // 2 - 1
                p.wait(dv, AC)
                p.inc(dv.tensor_scalar_mul(colt[gr, 38:39], lt[:, mid:mid + 1], -1.0), DV)
                p.wait(dv, DV)
                p.wait(ac, DV)
                p.inc(ac.activation(TMP2[gr, c * CH:(c + 1) * CH], lt[:, c * CH:(c + 1) * CH],
                                    AF.Exp, bias=colt[gr, 38:39]), AC)
            p.wait(dv, AC)
            p.inc(dv.tensor_tensor(TMP5[gr, :], TMP2[gr, :], bTv[gr, :], op=ALU.mult), DV)
            p.wait(dv, DV)
            p.inc(dv.tensor_tensor(TMP5[gr, :], TMP5[gr, :], rkTv[gr, :], op=ALU.mult), DV)
            p.wait(dv, DV)
            p.inc(dv.tensor_copy(md16bt[gr, :], TMP5[gr, :]), DV)
            p.wait(te, DV)
            for tt in range(NT):
                cs = tt * 16 + grp * 8
                p.inc(te.transpose(PT8[0:128, 32:40], TMP5[gr, tt * 128:(tt + 1) * 128],
                                   idn32[gr, grp * 64:grp * 64 + 8]), TE)
                p.wait(dv, TE)
                p.inc(dv.tensor_scalar_mul(nbt_tok[:, cs:cs + 8], PT8[0:128, 32:40], -1.0), DV)
                p.wait(te, DV)
            p.inc(dv.tensor_tensor(TMP2[gr, :], TMP2[gr, :], rqTv[gr, :], op=ALU.mult), DV)
            p.wait(dv, DV)
            p.wait(te, DV)
            for c in range(NCH):
                p.inc(te.transpose(PT8[0:64, 32:40],
                                   TMP2[gr, c * CH:(c + 1) * CH],
                                   idn32[gr, grp * 64:grp * 64 + 8]), TE)
                p.wait(dv, TE)
                dst = lq_dup[0:64, :].rearrange("p (h c) -> p h c", c=32)[
                    :, grp * 8:grp * 8 + 8, c]
                p.inc(dv.tensor_copy(dst, PT8[0:64, 32:40]), DV)
                p.wait(te, DV)
        for grp, gn in GA:
            gr = grows(grp)
            lt = LTv[gr, :]
            lc = lt.rearrange("p (c j) -> p c j", j=CH)[:, :, CH - 1]
            lm_ = lt.rearrange("p (c j) -> p c j", j=CH)[:, :, CH // 2 - 1]
            p.inc(dv.tensor_tensor(colt[gr, 0:32], lc, lm_, op=ALU.subtract), DV)
            p.inc(dv.memset(colt[gr, 32:36], 0.0), DV)
            nxt = lt[:, CH + CH // 2 - 1:T:CH]
            p.inc(dv.tensor_tensor(colt[gr, 0:31], colt[gr, 0:31], nxt, op=ALU.add), DV)
            p.wait(ac, DV)
            p.inc(ac.activation(colt[gr, 0:32], colt[gr, 0:32], AF.Exp), AC)
            p.wait(dv, AC)
            p.inc(dv.tensor_copy(sq16[gr, 0:32], colt[gr, 0:32]), DV)
            p.wait(te, DV)
            for h in range(8):
                p.inc(te.matmul(PB[0:1, grp * 256 + h * 32:grp * 256 + (h + 1) * 32],
                                i8c[gr, h:h + 1], sq16[gr, 0:32], start=True, stop=True), TE)
        p.wait(dv, TE)
        p.inc(dv.tensor_copy(rowb[0:1, 0:512], PB[0:1, 0:512]), DV)
        p.wait(te, DV)
        for ng in range(4):
            ins = te.matmul(PB[:, 512 + ng * 128:512 + (ng + 1) * 128], ones_row[0:1, :],
                            rowb[0:1, ng * 128:(ng + 1) * 128], start=True, stop=True)
        p.inc(ins, TE)
        p.wait(dv, TE)
        p.inc(dv.tensor_copy(scal[:, :], PB[:, 512:1024]), DV)
        peek_out("scal", scal[:, 0:512])
        peek_out("lrkA", md16lrk[0:8, 0:T])
        peek_out("btA", md16bt[0:8, 0:T])
        peek_out("lqdup", lq_dup[:, 0:512])
        peek_out("btok", tokmd[:, 0:1280])

        endphase("P6")
        PTRK = PA[:, 0:1024].bitcast(BF16)
        p.wait(dv, DV)
        for m in range(HK):
            p.wait(sync, DV); p.wait(sync, AC); p.wait(sync, SP)
            wld = p.inc(sync.dma_start(strip[:], kT_sp[m * 128:(m + 1) * 128, :]), LD, 16)
            p.wait(te, LD, wld)
            p.wait(te, DV); p.wait(te, AC)
            for tt in range(NT):
                ins = te.transpose(PTRK[:, tt * 128:(tt + 1) * 128],
                                   strip[:, tt * 128:(tt + 1) * 128], idn16[:])
            p.inc(ins, TE)
            p.wait(dv, TE); p.wait(ac, TE)
            for tt in range(NT):
                src = PTRK[:, tt * 128:(tt + 1) * 128]
                cK = lrk_tok[:, tt * 16 + m:tt * 16 + m + 1]
                cB = nbt_tok[:, tt * 16 + m:tt * 16 + m + 1]
                if tt < 8:
                    p.inc(dv.tensor_scalar_mul(stg16[0][:, tt * 128:(tt + 1) * 128], src, cK), DV)
                    p.inc(dv.tensor_scalar_mul(stg16[1][:, tt * 128:(tt + 1) * 128], src, cB), DV)
                else:
                    p.inc(ac.activation(stg16[0][:, tt * 128:(tt + 1) * 128], src,
                                        AF.Copy, scale=cK), AC)
                    p.inc(ac.activation(stg16[1][:, tt * 128:(tt + 1) * 128], src,
                                        AF.Copy, scale=cB), AC)
            p.wait(ac, DV); p.wait(ac, AC)
            km_dst = km_sp[m * T:(m + 1) * T, :].rearrange("(tt p) k -> p tt k", p=128)
            nbk_dst = nbk_sp[m * T:(m + 1) * T, :].rearrange("(tt p) k -> p tt k", p=128)
            p.wait(ac, AC)
            p.inc(ac.dma_start(km_dst, stg16[0][:].rearrange("p (tt k) -> p tt k", k=128)),
                  SP, 16)
            spill_mark[0] = p.inc(
                ac.dma_start(nbk_dst, stg16[1][:].rearrange("p (tt k) -> p tt k", k=128)),
                SP, 16)
            p.wait(dv, SP); p.wait(ac, SP)

        endphase("P7")
        PTRK8 = PB[:, 0:1024].bitcast(BF16)
        tr_mark = [0]
        p.wait(sync, DV); p.wait(sync, AC); p.wait(sync, TE)
        w_load(0, wv_h, 0)
        for m in range(NVT):
            w_cast(m % 2)
            if m + 1 < NVT:
                w_load((m + 1) % 2, wv_h, m + 1)
            proj_mms(m % 2, PA)
            h, vh = m // 2, m % 2
            p.wait(dv, TE, mm_mark[m % 2])
            evac_mark[0] = p.inc(dv.tensor_copy(pvtb[:, :], PA[:, :]), DV)
            p.wait(dv, AC)
            p.inc(dv.tensor_scalar(t0b[:, :], pvtb[:, :], vcw[:, m, 3:4], vcb[:, m:m + 1],
                                   op0=ALU.mult, op1=ALU.add), DV)
            for j, off in ((2, 1), (1, 2), (0, 3)):
                p.inc(dv.scalar_tensor_tensor(t0b[:, off:T], pvtb[:, 0:T - off],
                                              vcw[:, m, j:j + 1], t0b[:, off:T],
                                              op0=ALU.mult, op1=ALU.add), DV)
            p.wait(ac, DV)
            p.wait(ac, TE, tr_mark[0])
            p.inc(ac.activation(strip[:, :], t0b[:, :], AF.Silu), AC)
            p.wait(te, AC)
            p.wait(te, DV)
            for tt in range(NT):
                ins = te.transpose(PTRK8[:, tt * 128:(tt + 1) * 128],
                                   strip[:, tt * 128:(tt + 1) * 128], idn16[:])
            tr_mark[0] = p.inc(ins, TE)
            p.wait(dv, TE); p.wait(ac, TE)
            if vh == 0:
                p.wait(dv, SP, spill_mark[0]); p.wait(ac, SP, spill_mark[0])
            for tt in range(NT):
                src_ = PTRK8[:, tt * 128:(tt + 1) * 128]
                cB = b_tok[:, tt * 16 + h:tt * 16 + h + 1]
                dst = stg16v[vh][:, tt * 128:(tt + 1) * 128]
                if tt < 8:
                    p.inc(dv.tensor_scalar_mul(dst, src_, cB), DV)
                else:
                    p.inc(ac.activation(dst, src_, AF.Copy, scale=cB), AC)
            if vh == 1:
                p.wait(ac, DV); p.wait(ac, AC)
                dst0 = bv_sp[h * T:(h + 1) * T, 0:128].rearrange("(tt p) k -> p tt k", p=128)
                dst1 = bv_sp[h * T:(h + 1) * T, 128:256].rearrange("(tt p) k -> p tt k", p=128)
                p.inc(ac.dma_start(dst0, stg16v[0][:, :].rearrange("p (tt k) -> p tt k", k=128)),
                      SP, 16)
                spill_mark[0] = p.inc(
                    ac.dma_start(dst1, stg16v[1][:, :].rearrange("p (tt k) -> p tt k", k=128)),
                    SP, 16)
        p.wait(dv, SP); p.wait(ac, SP)

        dwsrcs = ([(wd_h, wd_bf, i) for i in range(E * 8)]
                  + [(shd_h, shd_bf, i) for i in range(16)]
                  + [(wo_h, wo_bf, i) for i in range(32)])
        dw_state = {"i": 0, "dma": 0, "cast": 0, "spill": 0}
        dwstg = rqTv
        dwstg16 = AR1[:, 53248:55296]

        def dw_item():
            if dw_state["i"] >= len(dwsrcs):
                return
            src_h, dst_sp, mi = dwsrcs[dw_state["i"]]
            dw_state["i"] += 1
            p.wait(sync, DV, dw_state["cast"])
            dw_state["dma"] = p.inc(sync.dma_start(dwstg[:, :], src_h[mi]), LD, 16)
            p.wait(dv, LD, dw_state["dma"])
            p.wait(dv, SP, dw_state["spill"])
            dw_state["cast"] = p.inc(dv.tensor_copy(dwstg16[:, :], dwstg[:, :]), DV)
            p.wait(ac, DV, dw_state["cast"])
            dw_state["spill"] = p.inc(
                ac.dma_start(dst_sp[mi * 128:(mi + 1) * 128, :], dwstg16[:, :]), SP, 16)

        endphase("P8")
        s1b, x3b, htb = WK1, WK2, WK3
        x1_mark = [0]
        x3_mark = [0]

        p.wait(sync, DV); p.wait(sync, AC); p.wait(sync, TE)
        w_load(0, wg_h.rearrange("e m p d -> (e m) p d"), 0)
        wgf = wg_h.rearrange("e m p d -> (e m) p d")
        wuf = wu_h.rearrange("e m p d -> (e m) p d")
        for e in range(E):
            p.wait(te, DV)
            p.wait(te, AC)
            for ng in range(4):
                ins = te.matmul(PB[0:1, ng * 512:(ng + 1) * 512], i8c[0:8, e:e + 1],
                                combT[0:8, ng * 512:(ng + 1) * 512], start=True, stop=True)
            p.inc(ins, TE)
            p.wait(dv, TE)
            p.inc(dv.tensor_copy(rowsg[0:1, :], PB[0:1, 0:2048]), DV)
            p.wait(dv, DV)
            p.wait(te, DV)
            for ng in range(4):
                ins = te.matmul(PA[:, ng * 512:(ng + 1) * 512], ones_row[0:1, :],
                                rowsg[0:1, ng * 512:(ng + 1) * 512], start=True, stop=True)
            p.inc(ins, TE)
            p.wait(dv, TE)
            evac_mark[0] = p.inc(dv.tensor_copy(combbc[:, :], PA[:, :]), DV)
            for m in range(8):
                gi = e * 8 + m
                w_cast(0)
                if gi + 1 < 64:
                    p.wait(sync, DV, cast_mark[0])
                    dma_mark[0] = p.inc(sync.dma_start(stgf[0][:, :], wgf[gi + 1]), LD, 16)
                p.wait(te, DV, max(cast_mark[0], evac_mark[0]))
                p.wait(te, AC, x1_mark[0])
                for kd in range(NKD):
                    for ng in range(4):
                        ins = te.matmul(PA[:, ng * 512:(ng + 1) * 512],
                                        stg16[0][:, kd * 128:(kd + 1) * 128],
                                        XNT[:, kd * T + ng * 512:kd * T + (ng + 1) * 512],
                                        start=(kd == 0), stop=(kd == NKD - 1))
                mma = mm_mark[0] = p.inc(ins, TE)
                if gi == 0:
                    p.wait(sync, DV, cast_mark[1])
                    dma_mark[1] = p.inc(sync.dma_start(stgf[1][:, :], wuf[gi]), LD, 16)
                w_cast(1)
                if gi + 1 < 64:
                    p.wait(sync, DV, cast_mark[1])
                    dma_mark[1] = p.inc(sync.dma_start(stgf[1][:, :], wuf[gi + 1]), LD, 16)
                p.wait(te, DV, max(cast_mark[1], x3_mark[0]))
                for kd in range(NKD):
                    for ng in range(4):
                        ins = te.matmul(PB[:, ng * 512:(ng + 1) * 512],
                                        stg16[1][:, kd * 128:(kd + 1) * 128],
                                        XNT[:, kd * T + ng * 512:kd * T + (ng + 1) * 512],
                                        start=(kd == 0), stop=(kd == NKD - 1))
                mmb = mm_mark[1] = p.inc(ins, TE)
                p.wait(ac, TE, mma)
                p.wait(ac, SP, spill_mark[0])
                x1_mark[0] = p.inc(ac.activation(s1b[:, :], PA[:, :], AF.Silu), AC)
                p.wait(dv, TE, mmb)
                p.wait(dv, SP, spill_mark[0])
                x3_mark[0] = p.inc(dv.tensor_tensor(x3b[:, :], PB[:, :], combbc[:, :],
                                                    op=ALU.mult), DV)
                p.wait(dv, AC, x1_mark[0])
                p.inc(dv.tensor_tensor(htb[:, :], s1b[:, :], x3b[:, :], op=ALU.mult), DV)
                p.wait(ac, DV)
                p.wait(ac, AC)
                spill_mark[0] = p.inc(
                    ac.dma_start(ht_sp[gi * 128:(gi + 1) * 128, :], htb[:, :]), SP, 16)
        peek_out("ht63", htb[:, 0:T])

        endphase("P9")
        p.wait(sync, DV, cast_mark[0])
        dma_mark[0] = p.inc(sync.dma_start(stgf[0][:, :], shg_h[0]), LD, 16)
        for m in range(16):
            w_cast(0)
            if m + 1 < 16:
                p.wait(sync, DV, cast_mark[0])
                dma_mark[0] = p.inc(sync.dma_start(stgf[0][:, :], shg_h[m + 1]), LD, 16)
            p.wait(te, DV, max(cast_mark[0], evac_mark[0]))
            p.wait(te, AC, x1_mark[0])
            for kd in range(NKD):
                for ng in range(4):
                    ins = te.matmul(PA[:, ng * 512:(ng + 1) * 512],
                                    stg16[0][:, kd * 128:(kd + 1) * 128],
                                    XNT[:, kd * T + ng * 512:kd * T + (ng + 1) * 512],
                                    start=(kd == 0), stop=(kd == NKD - 1))
            mma = mm_mark[0] = p.inc(ins, TE)
            if m == 0:
                p.wait(sync, DV, cast_mark[1])
                dma_mark[1] = p.inc(sync.dma_start(stgf[1][:, :], shu_h[0]), LD, 16)
            w_cast(1)
            if m + 1 < 16:
                p.wait(sync, DV, cast_mark[1])
                dma_mark[1] = p.inc(sync.dma_start(stgf[1][:, :], shu_h[m + 1]), LD, 16)
            p.wait(te, DV, max(cast_mark[1], x3_mark[0]))
            for kd in range(NKD):
                for ng in range(4):
                    ins = te.matmul(PB[:, ng * 512:(ng + 1) * 512],
                                    stg16[1][:, kd * 128:(kd + 1) * 128],
                                    XNT[:, kd * T + ng * 512:kd * T + (ng + 1) * 512],
                                    start=(kd == 0), stop=(kd == NKD - 1))
            mmb = mm_mark[1] = p.inc(ins, TE)
            p.wait(ac, TE, mma)
            p.wait(ac, SP, spill_mark[0])
            x1_mark[0] = p.inc(ac.activation(s1b[:, :], PA[:, :], AF.Silu), AC)
            p.wait(dv, TE, mmb)
            p.wait(dv, SP, spill_mark[0])
            x3_mark[0] = p.inc(dv.tensor_tensor(x3b[:, :], PB[:, :], sgbc[:, :],
                                                op=ALU.mult), DV)
            p.wait(dv, AC, x1_mark[0])
            p.inc(dv.tensor_tensor(htb[:, :], s1b[:, :], x3b[:, :], op=ALU.mult), DV)
            p.wait(ac, DV)
            p.wait(ac, AC)
            spill_mark[0] = p.inc(
                ac.dma_start(s_sp[m * 128:(m + 1) * 128, :], htb[:, :]), SP, 16)

        endphase("P10")
        while dw_state["i"] < len(dwsrcs):
            dw_item()
        p.wait(dv, SP); p.wait(ac, SP)
        endphase("P105")
        p.wait(dv, DV)
        p.inc(dv.memset(SR[:, :], 0.0), DV)
        p.inc(dv.memset(SR16[:, :], 0.0), DV)
        PD = PA[0:64, 0:1024]
        PKQ = PA[0:64, 1024:2048]
        PLAM = PB[0:64, 0:1024]
        PROW = PB[0:1, 1024:2048]
        PBB = PB[0:64, 0:1024]
        PTR2 = PB[0:64, 0:512].bitcast(BF16)
        PSQ = PB[0:64, 0:2048]
        PCB = PB[0:64, 0:1024]
        PG = PA[:, 0:1024]
        PUW = [PA[0:64, 0:2048], PB[0:64, 0:2048]]
        POW = [PA[0:64, 0:2048], PB[0:64, 0:2048]]
        PDLT = PA[:, 0:2048]
        POTR = PB[:, 0:512].bitcast(BF16)

        qk_mark = [0, 0]
        ch_mark = [0] * NCH

        def pref_quad(qd_):
            bq = qd_ % 2
            src_q = qT_sp.rearrange("(h p) t -> h p t", p=128)[:, :, qd_ * 256:(qd_ + 1) * 256]
            src_k = kT_sp.rearrange("(h p) t -> h p t", p=128)[:, :, qd_ * 256:(qd_ + 1) * 256]
            p.inc(sync.dma_start(qbuf[bq].rearrange("p (h t) -> p h t", t=256),
                                 src_q.rearrange("h p t -> p h t")), LD, 16)
            qk_mark[bq] = p.inc(
                sync.dma_start(kbuf[bq].rearrange("p (h t) -> p h t", t=256),
                               src_k.rearrange("h p t -> p h t")), LD, 16)

        def pref_chunk(c_):
            bcn = c_ % 2
            src = bv_sp.rearrange("(h t) k -> h t k", t=T)[:, c_ * CH:(c_ + 1) * CH, :]
            p.inc(sync.dma_start(
                bvb[bcn].rearrange("p (h k) -> p h k", k=VE),
                src.rearrange("h t k -> t h k")), LD, 16)
            src = km_sp.rearrange("(h t) k -> h t k", t=T)[:, c_ * CH:(c_ + 1) * CH, :]
            p.inc(sync.dma_start(
                kmb[bcn].rearrange("p (h k) -> p h k", k=DK),
                src.rearrange("h t k -> t h k")), LD, 16)
            src = nbk_sp.rearrange("(h t) k -> h t k", t=T)[:, c_ * CH:(c_ + 1) * CH, :]
            ch_mark[c_] = p.inc(sync.dma_start(
                nbkb[bcn].rearrange("p (h k) -> p h k", k=DK),
                src.rearrange("h t k -> t h k")), LD, 16)

        pref_quad(0)
        pref_chunk(0)
        pref_chunk(1)
        id64 = idn16[0:64, 0:64]

        for c in range(NCH):
            qd_, cq = c // 4, c % 4
            bcn = c % 2
            qb, kb = qbuf[qd_ % 2], kbuf[qd_ % 2]

            def ksl(h, cq=cq, kb=kb):
                return kb[:, h * 256 + cq * 64:h * 256 + cq * 64 + 64]

            def qsl(h, cq=cq, qb=qb):
                return qb[:, h * 256 + cq * 64:h * 256 + cq * 64 + 64]

            if cq == 3 and qd_ + 1 < 8:
                p.wait(sync, TE)
                pref_quad(qd_ + 1)
            p.wait(te, LD, max(qk_mark[qd_ % 2], ch_mark[c]))

            for grp, gn in GA:
                gr = grows(grp)
                p.inc(te.matmul(PLAM[:, grp * 512:(grp + 1) * 512],
                                md16lrk[gr, c * CH:(c + 1) * CH],
                                exp64[gr, :], start=True, stop=True), TE)
                for h in range(8):
                    p.inc(te.matmul(PROW[0:1, grp * 512 + h * 64:grp * 512 + (h + 1) * 64],
                                    i8c[gr, h:h + 1], md16bt[gr, c * CH:(c + 1) * CH],
                                    start=True, stop=True), TE)
            p.wait(dv, TE)
            p.inc(dv.tensor_tensor(lstr[:, :], PLAM, mstr[:], op=ALU.mult), DV)
            p.inc(dv.tensor_tensor(lincl[:, :], PLAM, mincl[:], op=ALU.mult), DV)
            p.inc(dv.tensor_copy(rowb[0:1, 0:1024], PROW), DV)
            p.wait(dv, DV)
            p.wait(te, DV)
            for half in range(2):
                p.inc(te.matmul(PBB[:, half * 512:(half + 1) * 512], ones_row[0:1, 0:64],
                                rowb[0:1, half * 512:(half + 1) * 512],
                                start=True, stop=True), TE)
            p.wait(ac, TE)
            p.inc(ac.activation(bbs[:, :], PBB, AF.Copy), AC)
            p.wait(te, AC)
            for h in range(HK):
                p.inc(te.matmul(PD[:, h * 64:(h + 1) * 64], ksl(h), ksl(h),
                                start=True, stop=True), TE)
                p.inc(te.matmul(PKQ[:, h * 64:(h + 1) * 64], ksl(h), qsl(h),
                                start=True, stop=True), TE)
            p.wait(dv, TE)
            p.inc(dv.tensor_tensor(Pcb[:, :], PD, lstr[:, :], op=ALU.mult), DV)
            p.wait(dv, DV)
            p.inc(dv.tensor_tensor(Pcb[:, :], Pcb[:, :], bbs[:, :], op=ALU.mult), DV)
            p.inc(dv.tensor_tensor(ATs[:, :], PKQ, lincl[:, :], op=ALU.mult), DV)
            if c == 0:
                peek_out("negMT0", Pcb[:, 0:1024])
            p.wait(te, DV)
            for h in range(HK):
                ins = te.transpose(PTR2[:, h * 64:(h + 1) * 64],
                                   Pcb[:, h * 64:(h + 1) * 64], id64)
            p.inc(ins, TE)
            p.wait(dv, TE)
            p.inc(dv.tensor_copy(PTcb[:, :], PTR2), DV)
            p.inc(dv.tensor_tensor(Tcb[:, :], Pcb[:, :], idup[:], op=ALU.add), DV)
            for lvl in range(5):
                p.wait(te, DV)
                for h in range(HK):
                    sl = slice(h * 64, (h + 1) * 64)
                    p.inc(te.matmul(PSQ[:, h * 128:h * 128 + 64], PTcb[:, sl],
                                    Pcb[:, sl], start=True, stop=True), TE)
                    p.inc(te.matmul(PSQ[:, h * 128 + 64:h * 128 + 128], Pcb[:, sl],
                                    PTcb[:, sl], start=True, stop=True), TE)
                p.wait(dv, TE)
                psq = PSQ.rearrange("p (h two j) -> p h two j", two=2, j=64)
                p.inc(dv.tensor_copy(
                    Pcb.rearrange("p (h j) -> p h j", j=64), psq[:, :, 0, :]), DV)
                p.inc(dv.tensor_copy(
                    PTcb.rearrange("p (h j) -> p h j", j=64), psq[:, :, 1, :]), DV)
                p.wait(te, DV)
                for h in range(HK):
                    sl = slice(h * 64, (h + 1) * 64)
                    p.inc(te.matmul(PCB[:, sl], PTcb[:, sl], Tcb[:, sl],
                                    start=True, stop=True), TE)
                p.wait(dv, TE)
                p.wait(dv, DV)
                p.inc(dv.tensor_tensor(Tcb[:, :], Tcb[:, :], PCB, op=ALU.add), DV)
            p.wait(te, DV)
            for h in range(HK):
                p.inc(te.matmul(PG[:, h * 64:(h + 1) * 64],
                                nbkb[bcn][:, h * 128:(h + 1) * 128],
                                Tcb[:, h * 64:(h + 1) * 64], start=True, stop=True), TE)
            p.wait(ac, TE)
            p.inc(ac.activation(Gts[:, :], PG, AF.Copy), AC)
            p.wait(te, AC)
            for wv in range(2):
                for hh in range(8):
                    h = wv * 8 + hh
                    p.inc(te.matmul(PUW[wv][:, hh * 256:(hh + 1) * 256],
                                    Tcb[:, h * 64:(h + 1) * 64],
                                    bvb[bcn][:, h * 256:(h + 1) * 256],
                                    start=True, stop=False), TE)
                    p.inc(te.matmul(PUW[wv][:, hh * 256:(hh + 1) * 256],
                                    Gts[:, h * 64:(h + 1) * 64],
                                    SR16[:, h * 256:(h + 1) * 256],
                                    start=False, stop=True), TE)
                p.wait(dv, TE)
                p.inc(dv.tensor_copy(Us[:, wv * 2048:(wv + 1) * 2048], PUW[wv]), DV)
            for wv in range(2):
                p.wait(te, DV)
                for hh in range(8):
                    h = wv * 8 + hh
                    p.inc(te.matmul(POW[wv][:, hh * 256:(hh + 1) * 256], qsl(h),
                                    SR16[:, h * 256:(h + 1) * 256], start=True, stop=False), TE)
                    p.inc(te.matmul(POW[wv][:, hh * 256:(hh + 1) * 256],
                                    ATs[:, h * 64:(h + 1) * 64],
                                    Us[:, h * 256:(h + 1) * 256],
                                    start=False, stop=True), TE)
                p.wait(dv, TE); p.wait(ac, TE)
                for hh in range(8):
                    h = wv * 8 + hh
                    colap = lq_dup[0:64, h * 32 + c:h * 32 + c + 1]
                    dst = obuf[:, h * 256:(h + 1) * 256]
                    if hh < 4:
                        p.inc(dv.tensor_scalar_mul(dst, POW[wv][:, hh * 256:(hh + 1) * 256],
                                                   colap), DV)
                    else:
                        p.inc(ac.activation(dst, POW[wv][:, hh * 256:(hh + 1) * 256],
                                            AF.Copy, scale=colap), AC)
            tq0 = (c % 4) * 64
            p.wait(te, DV); p.wait(te, AC)
            for rnd in range(2):
                for s in range(16):
                    vt = rnd * 16 + s
                    h, vh = vt // 2, vt % 2
                    ins = te.transpose(POTR[:, s * 64:(s + 1) * 64],
                                       obuf[:, h * 256 + vh * 128:h * 256 + (vh + 1) * 128],
                                       id64)
                p.inc(ins, TE)
                p.wait(dv, TE)
                dst = OTacc.rearrange("p (vt q) -> p vt q", q=OT_QT)[
                    :, rnd * 16:(rnd + 1) * 16, tq0:tq0 + 64]
                p.inc(dv.tensor_copy(
                    dst, POTR.rearrange("p (vt j) -> p vt j", j=64)[:, 0:16, :]), DV)
                p.wait(te, DV)
            for wv in range(2):
                p.wait(te, DV)
                for hh in range(8):
                    h = wv * 8 + hh
                    p.inc(te.matmul(PDLT[:, hh * 256:(hh + 1) * 256],
                                    kmb[bcn][:, h * 128:(h + 1) * 128],
                                    Us[:, h * 256:(h + 1) * 256],
                                    start=True, stop=True), TE)
                p.wait(dv, TE)
                p.inc(dv.tensor_tensor(SR[:, wv * 2048:(wv + 1) * 2048],
                                       SR[:, wv * 2048:(wv + 1) * 2048], PDLT,
                                       op=ALU.add), DV)
            p.wait(dv, DV)
            for h in range(HK):
                scol = scal[:, h * 32 + c:h * 32 + c + 1]
                p.inc(dv.tensor_scalar_mul(SR[:, h * 256:(h + 1) * 256],
                                           SR[:, h * 256:(h + 1) * 256], scol), DV)
            p.wait(dv, DV)
            p.wait(ac, DV)
            p.inc(ac.activation(SR16[:, :], SR[:, :], AF.Copy), AC)
            if c + 2 < NCH:
                p.wait(sync, TE)
                pref_chunk(c + 2)
            if c % 4 == 3:
                qd4 = c // 4
                p.wait(ac, DV)
                p.wait(ac, AC)
                spill_mark[0] = p.inc(ac.dma_start(
                    ot_sp.rearrange("(b vt p) q -> b p vt q", vt=32, p=128)[qd4],
                    OTacc.rearrange("p (vt q) -> p vt q", q=OT_QT)), SP, 16)
                if c + 1 < NCH:
                    p.wait(dv, SP)
            if c == 0:
                peek_out("TT0", Tcb[:, 0:1024])
                peek_out("U0", Us[:, 0:4096])
                peek_out("O0", obuf[:, 0:4096])
                peek_out("AT0", ATs[:, 0:1024])
        p.wait(dv, AC)
        peek_out("SR", SR[:, 0:4096])

        endphase("P11")
        nsrc = 112
        srcs = ([(wo_bf, vt) for vt in range(32)] + [(wd_bf, em) for em in range(64)]
                + [(shd_bf, m) for m in range(16)])
        NWB = 4
        wmark = [0] * NWB

        def wload(gi):
            dq_, si_ = gi // nsrc, gi % nsrc
            src, idx = srcs[si_]
            wmark[gi % NWB] = p.inc(sync.dma_start(
                wdn[gi % NWB][:, :], src[idx * 128:(idx + 1) * 128, dq_ * QT:(dq_ + 1) * QT]),
                LD, 16)

        for qd_ in range(QD):
            p.wait(sync, TE); p.wait(sync, SP)
            for hb in range(2):
                p.inc(sync.dma_start(
                    ACTS.rearrange("p (f t) -> p f t", t=QT)[:, 0:32, hb * 256:(hb + 1) * 256],
                    ot_sp.rearrange("(b vt p) q -> b p vt q", vt=32, p=128)[qd_ * 2 + hb]),
                    LD, 16)
            for h8 in range(4):
                p.inc(sync.dma_start(
                    ACTS.rearrange("p (f t) -> p f t", t=QT)[:, 32 + h8 * 16:32 + (h8 + 1) * 16, :],
                    ht_sp.rearrange("(f p) t -> p f t", p=128)[
                        :, h8 * 16:(h8 + 1) * 16, qd_ * QT:(qd_ + 1) * QT]), LD, 16)
            acts_ld = p.inc(sync.dma_start(
                ACTS.rearrange("p (f t) -> p f t", t=QT)[:, 96:112, :],
                s_sp.rearrange("(f p) t -> p f t", p=128)[:, :, qd_ * QT:(qd_ + 1) * QT]),
                LD, 16)
            for gi in range(NWB - 1):
                wload(gi)
            for dq in range(4):
                PQ = (PA if dq % 2 == 0 else PB)
                p.wait(te, DV)
                for si in range(nsrc):
                    gi = dq * nsrc + si
                    if gi + NWB - 1 < 4 * nsrc:
                        p.wait(sync, TE)
                        wload(gi + NWB - 1)
                    p.wait(te, LD, max(wmark[gi % NWB], acts_ld))
                    for sub in range(4):
                        p.inc(te.matmul(
                            PQ[:, sub * 512:(sub + 1) * 512],
                            ACTS[:, si * QT + sub * 128:si * QT + sub * 128 + 128],
                            wdn[gi % NWB][:, :], start=(si == 0), stop=(si == nsrc - 1)), TE)
                for sub in range(4):
                    tglob = qd_ * QT + sub * 128
                    p.wait(sync, DV)
                    wld = p.inc(sync.dma_start(
                        xres[:, :], x_h[tglob:tglob + 128, dq * QT:(dq + 1) * QT]), LD, 16)
                    p.wait(dv, LD, wld)
                    p.wait(dv, TE)
                    p.inc(dv.tensor_tensor(oute[:, :], PQ[:, sub * 512:(sub + 1) * 512],
                                           xres[:, :], op=ALU.add), DV)
                    p.wait(sync, DV)
                    p.inc(sync.dma_start(
                        out_h[tglob:tglob + 128, dq * QT:(dq + 1) * QT], oute[:, :]), LD, 16)

        p.wait(sync, LD)
        p.wait(sync, SP)
    except _Stop:
        sync.wait_ge(p.sem[LD], p.cnt[LD])
        sync.wait_ge(p.sem[SP], p.cnt[SP])
        stack.close()

    return nc


def kernel(**inputs):
    g = host_prep(inputs)
    x = np.asarray(inputs["L0_attn_x"], np.float32)
    nc = build()
    in_maps = []
    for core in range(8):
        m = dict(g)
        m["x"] = np.ascontiguousarray(x[core])
        in_maps.append(m)
    res = run_bass_kernel_spmd(nc, in_maps, core_ids=list(range(8)))
    out = np.stack([res.results[i]["out"] for i in range(8)])
    return out.astype(np.float32)




# version 10
# speedup vs baseline: 1.1309x; 1.0459x over previous
import numpy as np
import ml_dtypes
from contextlib import ExitStack

import concourse.bass as bass
import concourse.mybir as mybir
from concourse.bass_utils import run_bass_kernel_spmd

F32 = mybir.dt.float32
BF16 = mybir.dt.bfloat16
AF = mybir.ActivationFunctionType
ALU = mybir.AluOpType
AX = mybir.AxisListType
BF = ml_dtypes.bfloat16

T = 2048
DM = 2048
HK = 16
DK = 128
VE = 256
CH = 64
NCH = T // CH
E = 8
NT = T // 128
NKD = DM // 128
NVT = 4096 // 128
QD = 4
QT = T // QD


def _tile_w(w, nm):
    d = w.shape[0]
    nkd = d // 128
    return np.ascontiguousarray(
        w.reshape(nkd, 128, nm, 128).transpose(2, 1, 0, 3).reshape(nm, 128, nkd * 128))


def host_prep(inp):
    g = {}
    f32 = np.float32
    g["wq"] = _tile_w(np.asarray(inp["L0_attn_Wq"], f32), 16)
    g["wk"] = _tile_w(np.asarray(inp["L0_attn_Wk"], f32), 16)
    g["wv"] = _tile_w(np.asarray(inp["L0_attn_Wv"], f32), 32)
    for nm, key in (("wa", "L0_attn_Wa"), ("wb", "L0_attn_Wb")):
        w = np.asarray(inp[key], f32)
        g[nm] = np.ascontiguousarray(
            w.reshape(NKD, 128, 16).transpose(1, 0, 2).reshape(128, NKD * 16))
    for nm, wkey, bkey, ntile in (("qc", "L0_attn_qcw", "L0_attn_qcb", 16),
                                  ("kc", "L0_attn_kcw", "L0_attn_kcb", 16),
                                  ("vc", "L0_attn_vcw", "L0_attn_vcb", 32)):
        w = np.asarray(inp[wkey], f32).reshape(ntile, 128, 4).transpose(1, 0, 2)
        g[nm + "w"] = np.ascontiguousarray(w.reshape(128, ntile * 4))
        g[nm + "b"] = np.ascontiguousarray(np.asarray(inp[bkey], f32).reshape(ntile, 128).T)
    gw = np.asarray(inp["L0_ffn_gate_w"], f32).T
    g["gw"] = np.ascontiguousarray(gw.reshape(NKD, 128, 8).transpose(1, 0, 2).reshape(128, NKD * 8))
    se = np.asarray(inp["L0_ffn_se_gate_w"], f32).T
    g["se"] = np.ascontiguousarray(se.reshape(NKD, 128, 1).transpose(1, 0, 2).reshape(128, NKD))
    wg = np.asarray(inp["L0_ffn_w_gate"], f32)
    wu = np.asarray(inp["L0_ffn_w_up"], f32)
    g["wg"] = np.stack([_tile_w(wg[e].T, 8) for e in range(E)])
    g["wu"] = np.stack([_tile_w(wu[e].T, 8) for e in range(E)])
    g["shg"] = _tile_w(np.asarray(inp["L0_ffn_sh_gate"], f32).T, 16)
    g["shu"] = _tile_w(np.asarray(inp["L0_ffn_sh_up"], f32).T, 16)
    wd = np.asarray(inp["L0_ffn_w_down"], f32)
    g["wd"] = np.ascontiguousarray(
        np.stack([wd[e].T for e in range(E)]).reshape(E * 8, 128, DM))
    g["shd"] = np.ascontiguousarray(np.asarray(inp["L0_ffn_sh_down"], f32).T.reshape(16, 128, DM))
    g["wo"] = np.ascontiguousarray(np.asarray(inp["L0_attn_Wo"], f32).reshape(32, 128, DM))

    g["ident16"] = np.eye(128, dtype=BF)
    g["ident32"] = np.eye(128, dtype=f32)
    g["ones_col16"] = np.ones((128, 1), dtype=BF)
    g["ones_row16"] = np.ones((1, 128), dtype=BF)
    i8d = np.zeros((128, 8), dtype=BF)
    i8d[0:8] = np.eye(8)
    i8d[64:72] = np.eye(8)
    g["i8dup"] = i8d
    ju = np.arange(64)[:, None]
    iu = np.arange(64)[None, :]
    strict = -(iu > ju).astype(f32)
    incl = (iu >= ju).astype(f32)
    g["mask_strict"] = np.ascontiguousarray(np.tile(strict, (1, 16)))
    g["mask_incl"] = np.ascontiguousarray(np.tile(incl, (1, 16)))
    i64 = np.eye(64, dtype=BF)
    g["idup"] = np.ascontiguousarray(np.tile(i64, (1, 16)))
    ex = np.zeros((128, 512), dtype=BF)
    for h in range(8):
        ex[h, h * 64:(h + 1) * 64] = 1
        ex[64 + h, h * 64:(h + 1) * 64] = 1
    g["exp64"] = ex
    g["fconst"] = np.tile(np.array([1e-5, -0.5, -1.0, 0.0], f32), (128, 1))
    return g


class Prog:
    def __init__(self, nc, stack):
        self.nc = nc
        self.stack = stack
        self.sem = {}
        self.cnt = {}

    def sem_new(self, name):
        self.sem[name] = self.stack.enter_context(self.nc.semaphore(name))
        self.cnt[name] = 0

    def inc(self, instr, name, amount=1):
        instr.then_inc(self.sem[name], amount)
        self.cnt[name] += amount
        return self.cnt[name]

    def wait(self, eng, name, val=None):
        eng.wait_ge(self.sem[name], self.cnt[name] if val is None else val)


class _Stop(Exception):
    pass


def build(peek=(), last_phase=None):
    peek = set(peek)
    nc = bass.Bass("TRN2", target_bir_lowering=False)
    stack = ExitStack()
    p = Prog(nc, stack)
    sync, te, dv, ac, gp = nc.sync, nc.tensor, nc.vector, nc.scalar, nc.gpsimd
    LD, SP, TE, DV, AC = "ld", "sp", "te", "dv", "ac"

    def din(name, shape, dtype=F32):
        return nc.dram_tensor(name, list(shape), dtype, kind="ExternalInput").ap()

    x_h = din("x", [T, DM])
    wq_h = din("wq", [16, 128, DM]); wk_h = din("wk", [16, 128, DM]); wv_h = din("wv", [32, 128, DM])
    wa_h = din("wa", [128, NKD * 16]); wb_h = din("wb", [128, NKD * 16])
    qcw_h = din("qcw", [128, 64]); qcb_h = din("qcb", [128, 16])
    kcw_h = din("kcw", [128, 64]); kcb_h = din("kcb", [128, 16])
    vcw_h = din("vcw", [128, 128]); vcb_h = din("vcb", [128, 32])
    gw_h = din("gw", [128, NKD * 8]); se_h = din("se", [128, NKD])
    wg_h = din("wg", [E, 8, 128, DM]); wu_h = din("wu", [E, 8, 128, DM])
    shg_h = din("shg", [16, 128, DM]); shu_h = din("shu", [16, 128, DM])
    wd_h = din("wd", [E * 8, 128, DM]); shd_h = din("shd", [16, 128, DM])
    wo_h = din("wo", [32, 128, DM])
    ident16_h = din("ident16", [128, 128], BF16)
    ident32_h = din("ident32", [128, 128], F32)
    ones_col_h = din("ones_col16", [128, 1], BF16)
    ones_row_h = din("ones_row16", [1, 128], BF16)
    i8_h = din("i8dup", [128, 8], BF16)
    mstr_h = din("mask_strict", [64, 1024], F32)
    mincl_h = din("mask_incl", [64, 1024], F32)
    idup_h = din("idup", [64, 1024], BF16)
    exp64_h = din("exp64", [128, 512], BF16)
    fconst_h = din("fconst", [128, 4], F32)
    out_h = nc.dram_tensor("out", [T, DM], F32, kind="ExternalOutput").ap()

    def scr(name, shape, dtype=BF16):
        return nc.dram_tensor(name, list(shape), dtype, kind="Internal").ap()

    qT_sp = scr("qT_sp", [HK * 128, T])
    kT_sp = scr("kT_sp", [HK * 128, T])
    km_sp = scr("km_sp", [HK * T, DK])
    nbk_sp = scr("nbk_sp", [HK * T, DK])
    bv_sp = scr("bv_sp", [HK * T, VE])
    ht_sp = scr("ht_sp", [E * 8 * 128, T])
    s_sp = scr("s_sp", [16 * 128, T])
    ot_sp = scr("ot_sp", [8 * 32 * 128, 256])
    wd_bf = scr("wd_bf", [E * 8 * 128, DM])
    shd_bf = scr("shd_bf", [16 * 128, DM])
    wo_bf = scr("wo_bf", [32 * 128, DM])

    try:
      with stack:
        for s in (LD, SP, TE, DV, AC):
            p.sem_new(s)

        SB = lambda n, sh, dt: stack.enter_context(nc.sbuf_tensor(n, sh, dt))
        AR1 = SB("AR1", [128, 57344], BF16)
        AR2 = SB("AR2", [128, 12288], BF16)
        AR1f = AR1[:].bitcast(F32)
        AR2f = AR2[:].bitcast(F32)

        idn16 = SB("idn16", [128, 128], BF16)
        idn32 = SB("idn32", [128, 128], F32)
        ones_col = SB("ones_col", [128, 1], BF16)
        ones_row = SB("ones_row", [1, 128], BF16)
        i8c = SB("i8c", [128, 8], BF16)
        mstr = SB("mstr", [64, 1024], F32)
        mincl = SB("mincl", [64, 1024], F32)
        idup = SB("idup_c", [64, 1024], BF16)
        exp64 = SB("exp64_c", [128, 512], BF16)
        fcn = SB("fcn", [128, 4], F32)
        convw = SB("convw", [128, 320], F32)
        wab = SB("wab", [128, NKD * 32], BF16)
        gwc = SB("gwc", [128, NKD * 8], BF16)
        sec = SB("sec", [128, NKD], BF16)
        md16lrk = SB("md16lrk", [128, T], BF16)
        md16bt = SB("md16bt", [128, T], BF16)
        tokmd = SB("tokmd", [128, 1280], F32)
        b_tok = tokmd[:, 0:256]
        lrk_tok = tokmd[:, 256:512]
        nbt_tok = tokmd[:, 512:768]
        rk_tok = tokmd[:, 768:1024]
        rq_tok = tokmd[:, 1024:1280]
        lqscal = SB("lqscal", [128, 1024], F32)
        lq_dup = lqscal[:, 0:512]
        scal = lqscal[:, 512:1024]
        sgbc = SB("sgbc", [128, T], BF16)
        combT = SB("combT", [8, T], BF16)
        combbc = SB("combbc", [128, T], BF16)
        WK1 = SB("WK1", [128, DM], BF16)
        WK2 = SB("WK2", [128, DM], BF16)
        WK3 = SB("WK3", [128, DM], BF16)
        colt = SB("colt", [128, 40], F32)
        rowsg = SB("rowsg", [1, T], BF16)
        rowb = rowsg[0:1, 0:1024]

        stgf = [AR2f[:, i * 2048:(i + 1) * 2048] for i in range(2)]
        stg16 = [AR2[:, 8192 + i * 2048:8192 + (i + 1) * 2048] for i in range(2)]
        xin = stgf
        wdn = [AR2[:, i * QT:(i + 1) * QT] for i in range(4)]
        xres = AR2f[:, 1024:1536]
        oute = AR2f[:, 1536:2048]

        XNT = AR1[:, 0:32768]
        R = [AR1f[:, 16384 + i * 2048:16384 + (i + 1) * 2048] for i in range(6)]
        LTv, bTv, TMP2, rkTv, rqTv, TMP5 = R
        sqscr = R[0]
        sgrow = TMP2[0:1, :]

        def a1b(c0, n):
            return AR1[:, c0:c0 + n]

        def a1f(c0, n):
            return AR1f[:, c0 // 2:(c0 + n) // 2]

        qbuf = [a1b(0, 4096), a1b(4096, 4096)]
        kbuf = [a1b(8192, 4096), a1b(12288, 4096)]

        def a1b64(c0, n):
            return AR1[0:64, c0:c0 + n]

        bvb = [a1b64(16384, 4096), a1b64(20480, 4096)]
        kmb = [a1b64(24576, 2048), a1b64(26624, 2048)]
        nbkb = [a1b64(28672, 2048), a1b64(30720, 2048)]
        Us = a1b64(32768, 4096)
        obuf = a1b64(36864, 4096)
        Gts = a1b(40960, 1024)
        ATs = a1b64(41984, 1024)
        Tcb = a1b64(43008, 1024)
        Pcb = a1b64(44032, 1024)
        PTcb = a1b64(45056, 1024)
        lstr = AR1[0:64, 46080:47104]
        lincl = AR1[0:64, 47104:48128]
        bbs = AR1[0:64, 48128:49152]
        SR = a1f(49152, 8192)
        SR16 = AR2[:, 0:4096]
        OT_QT = 256
        OTacc = AR2[:, 4096:12288]
        ACTS = AR1[:, 0:112 * QT]

        PA = stack.enter_context(nc.psum_tensor("PA", [128, 2048], F32))
        PB = stack.enter_context(nc.psum_tensor("PB", [128, 2048], F32))

        def peek_out(tag, ap):
            if tag not in peek:
                return
            t = nc.dram_tensor("peek_" + tag, list(ap.shape), ap.dtype,
                               kind="ExternalOutput").ap()
            p.wait(sync, DV); p.wait(sync, AC); p.wait(sync, TE)
            p.inc(sync.dma_start(t, ap), LD, 16)

        GA = ((0, "A"), (1, "B"))

        def endphase(name):
            if last_phase == name:
                raise _Stop()


        def grows(grp, n=8):
            return slice(grp * 64, grp * 64 + n)

        for dst, src in ((idn16, ident16_h), (idn32, ident32_h), (ones_col, ones_col_h),
                         (ones_row, ones_row_h), (i8c, i8_h), (mstr, mstr_h),
                         (mincl, mincl_h), (idup, idup_h), (exp64, exp64_h),
                         (fcn, fconst_h)):
            p.inc(sync.dma_start(dst[:], src), LD, 16)
        for (c0, c1), src in (((0, 64), qcw_h), ((64, 80), qcb_h), ((80, 144), kcw_h),
                              ((144, 160), kcb_h), ((160, 288), vcw_h), ((288, 320), vcb_h)):
            p.inc(sync.dma_start(convw[:, c0:c1], src), LD, 16)
        p.inc(sync.dma_start(stgf[0][:, 0:256], wa_h), LD, 16)
        p.inc(sync.dma_start(stgf[0][:, 256:512], wb_h), LD, 16)
        p.inc(sync.dma_start(stgf[1][:, 0:128], gw_h), LD, 16)
        p.inc(sync.dma_start(stgf[1][:, 128:144], se_h), LD, 16)
        p.wait(dv, LD)
        p.inc(dv.tensor_copy(wab[:], stgf[0][:, 0:512]), DV)
        p.inc(dv.tensor_copy(gwc[:], stgf[1][:, 0:128]), DV)
        p.inc(dv.tensor_copy(sec[:], stgf[1][:, 128:144]), DV)

        qcw = convw[:, 0:64].rearrange("p (m j) -> p m j", j=4)
        qcb = convw[:, 64:80]
        kcw = convw[:, 80:144].rearrange("p (m j) -> p m j", j=4)
        kcb = convw[:, 144:160]
        vcw = convw[:, 160:288].rearrange("p (m j) -> p m j", j=4)
        vcb = convw[:, 288:320]

        strip = WK3
        PTR = PB[:, 0:512].bitcast(BF16)
        p.wait(sync, DV)
        ld_mark = p.inc(sync.dma_start(xin[0][:], x_h[0:128, :]), LD, 16)
        for tt in range(NT):
            buf = xin[tt % 2]
            my_ld = ld_mark
            if tt + 1 < NT:
                if tt >= 1:
                    p.wait(sync, DV)
                ld_mark = p.inc(
                    sync.dma_start(xin[(tt + 1) % 2][:], x_h[(tt + 1) * 128:(tt + 2) * 128, :]),
                    LD, 16)
            p.wait(dv, LD, my_ld)
            p.inc(dv.reduce_sum(colt[:, 0:1], buf[:], axis=AX.X), DV)
            p.wait(dv, DV)
            p.inc(dv.tensor_scalar_mul(colt[:, 1:2], colt[:, 0:1], 1.0 / DM), DV)
            p.wait(dv, DV)
            p.wait(ac, LD, my_ld)
            p.inc(ac.activation(sqscr[:], buf[:], AF.Square, accum_out=colt[:, 2:3]), AC)
            p.wait(dv, AC)
            p.inc(dv.tensor_scalar_mul(colt[:, 3:4], colt[:, 2:3], 1.0 / DM), DV)
            p.inc(dv.tensor_tensor(colt[:, 4:5], colt[:, 1:2], colt[:, 1:2], op=ALU.mult), DV)
            p.wait(dv, DV)
            p.inc(dv.tensor_tensor(colt[:, 5:6], colt[:, 3:4], colt[:, 4:5], op=ALU.subtract), DV)
            p.wait(ac, DV)
            p.inc(ac.activation(colt[:, 6:7], colt[:, 5:6], AF.Sqrt, bias=fcn[:, 0:1]), AC)
            p.wait(dv, AC)
            p.inc(dv.reciprocal(colt[:, 7:8], colt[:, 6:7]), DV)
            p.wait(dv, DV)
            p.inc(dv.tensor_scalar(strip[:], buf[:], colt[:, 1:2], colt[:, 7:8],
                                   op0=ALU.subtract, op1=ALU.mult), DV)
            p.wait(te, DV)
            for half in range(2):
                for k8 in range(8):
                    kd = half * 8 + k8
                    ins = te.transpose(PTR[:, k8 * 128:(k8 + 1) * 128],
                                       strip[:, kd * 128:(kd + 1) * 128], idn16[:])
                p.inc(ins, TE)
                p.wait(dv, TE)
                dst = XNT.rearrange("p (kd t) -> p kd t", t=T)[
                    :, half * 8:(half + 1) * 8, tt * 128:(tt + 1) * 128]
                p.inc(dv.tensor_copy(dst, PTR.rearrange("p (k c) -> p k c", c=128)), DV)
                if half == 0:
                    p.wait(te, DV)
        peek_out("xnt", XNT[:, 0:4096])

        endphase("P1")
        PAB = PA[0:8, :]
        PABB = PA[64:72, :]
        for which in ("a", "b"):
            for grp, gn in GA:
                pab = PAB if grp == 0 else PABB
                for kd in range(NKD):
                    for ng in range(4):
                        off = (0 if which == "a" else 256) + kd * 16 + grp * 8
                        ins = te.matmul(pab[:, ng * 512:(ng + 1) * 512],
                                        wab[:, off:off + 8],
                                        XNT[:, kd * T + ng * 512:kd * T + (ng + 1) * 512],
                                        start=(kd == 0), stop=(kd == NKD - 1))
                    p.inc(ins, TE)
                p.wait(ac, TE)
                gr = grows(grp)
                if which == "a":
                    p.inc(ac.activation(TMP2[gr, :], pab[:], AF.Sigmoid), AC)
                    p.wait(ac, AC)
                    p.inc(ac.activation(LTv[gr, :], TMP2[gr, :], AF.Ln), AC)
                else:
                    p.inc(ac.activation(bTv[gr, :], pab[:], AF.Sigmoid), AC)
                p.wait(te, AC)
        p.wait(dv, AC)
        for grp, gn in GA:
            lt = LTv[grows(grp), :]
            for c in range(NCH):
                p.inc(dv.tensor_tensor_scan(
                    lt[:, c * CH:(c + 1) * CH], lt[:, c * CH:(c + 1) * CH],
                    lt[:, c * CH:(c + 1) * CH], 0.0, op0=ALU.add, op1=ALU.bypass), DV)
        peek_out("LA", LTv[0:8, 0:T])
        peek_out("bA", bTv[0:8, 0:T])

        endphase("P2")
        PSC = PB[:, 0:16]
        PTRN = PB[0:8, 512:1024]
        for tt in range(NT):
            for kd in range(NKD):
                ins = te.matmul(PSC[:, 0:8], XNT[:, kd * T + tt * 128:kd * T + (tt + 1) * 128],
                                gwc[:, kd * 8:(kd + 1) * 8], start=(kd == 0), stop=(kd == NKD - 1))
            p.inc(ins, TE)
            for kd in range(NKD):
                ins = te.matmul(PSC[:, 8:9], XNT[:, kd * T + tt * 128:kd * T + (tt + 1) * 128],
                                sec[:, kd:kd + 1], start=(kd == 0), stop=(kd == NKD - 1))
            p.inc(ins, TE)
            p.wait(dv, TE)
            c_ = colt
            p.inc(dv.tensor_copy(c_[:, 8:16], PSC[:, 0:8]), DV)
            p.wait(dv, DV)
            p.inc(dv.reduce_max(c_[:, 16:17], c_[:, 8:16], axis=AX.X), DV)
            p.wait(dv, DV)
            p.inc(dv.tensor_scalar(c_[:, 17:25], c_[:, 8:16], c_[:, 16:17], None,
                                   op0=ALU.is_equal), DV)
            p.wait(dv, DV)
            p.inc(dv.scalar_tensor_tensor(c_[:, 25:33], c_[:, 17:25], -1e30, c_[:, 8:16],
                                          op0=ALU.mult, op1=ALU.add), DV)
            p.wait(dv, DV)
            p.inc(dv.reduce_max(c_[:, 33:34], c_[:, 25:33], axis=AX.X), DV)
            p.wait(dv, DV)
            p.inc(dv.tensor_scalar(c_[:, 25:33], c_[:, 25:33], c_[:, 33:34], None,
                                   op0=ALU.is_equal), DV)
            p.inc(dv.tensor_tensor(c_[:, 34:35], c_[:, 16:17], c_[:, 33:34], op=ALU.subtract), DV)
            p.wait(dv, DV)
            p.wait(ac, DV)
            p.inc(ac.activation(c_[:, 35:36], c_[:, 34:35], AF.Sigmoid), AC)
            p.inc(ac.activation(c_[:, 36:37], PSC[:, 8:9], AF.Sigmoid), AC)
            p.wait(dv, AC)
            p.inc(dv.tensor_scalar(c_[:, 37:38], c_[:, 35:36], -1.0, 1.0,
                                   op0=ALU.mult, op1=ALU.add), DV)
            p.inc(dv.tensor_scalar_mul(c_[:, 17:25], c_[:, 17:25], c_[:, 35:36]), DV)
            p.wait(dv, DV)
            p.inc(dv.scalar_tensor_tensor(c_[:, 17:25], c_[:, 25:33], c_[:, 37:38],
                                          c_[:, 17:25], op0=ALU.mult, op1=ALU.add), DV)
            p.wait(te, DV)
            p.inc(te.transpose(PTRN[0:8, 0:128], c_[:, 17:25], idn32[:]), TE)
            p.inc(te.transpose(PTRN[0:1, 128:256], c_[:, 36:37], idn32[:]), TE)
            p.wait(dv, TE)
            p.inc(dv.tensor_copy(combT[:, tt * 128:(tt + 1) * 128], PTRN[0:8, 0:128]), DV)
            p.inc(dv.tensor_copy(sgrow[0:1, tt * 128:(tt + 1) * 128], PTRN[0:1, 128:256]), DV)
            p.wait(te, DV)
        p.inc(dv.tensor_copy(rowsg[0:1, :], sgrow[0:1, :]), DV)
        p.wait(te, DV)
        for ng in range(4):
            ins = te.matmul(PA[:, ng * 512:(ng + 1) * 512], ones_row[0:1, :],
                            rowsg[0:1, ng * 512:(ng + 1) * 512], start=True, stop=True)
        p.inc(ins, TE)
        p.wait(ac, TE)
        p.inc(ac.activation(sgbc[:], PA[:], AF.Copy), AC)
        p.wait(te, AC)
        peek_out("comb", combT[:, 0:T])

        endphase("P3")
        spill_mark = [0]
        t0b = WK1
        sq16 = WK2
        strip = WK3
        pvt = s1b_alias = None
        pvtb = SB("pvtb", [128, DM], BF16)
        stg16v = [SB(f"stg16v{i}", [128, DM], BF16) for i in range(2)]

        cast_mark = [0, 0]
        mm_mark = [0, 0]
        dma_mark = [0, 0]
        evac_mark = [0]

        def w_load(i, w_h, m):
            p.wait(sync, DV, cast_mark[i])
            dma_mark[i] = p.inc(sync.dma_start(stgf[i][:, :], w_h[m]), LD, 16)

        def w_cast(i):
            p.wait(dv, LD, dma_mark[i])
            p.wait(dv, TE, mm_mark[i])
            cast_mark[i] = p.inc(dv.tensor_copy(stg16[i][:, :], stgf[i][:, :]), DV)

        def proj_mms(i, dst):
            p.wait(te, DV, max(cast_mark[i], evac_mark[0]))
            for kd in range(NKD):
                for ng in range(4):
                    ins = te.matmul(dst[:, ng * 512:(ng + 1) * 512],
                                    stg16[i][:, kd * 128:(kd + 1) * 128],
                                    XNT[:, kd * T + ng * 512:kd * T + (ng + 1) * 512],
                                    start=(kd == 0), stop=(kd == NKD - 1))
            mm_mark[i] = p.inc(ins, TE)

        def conv_silu(m, wcol, bcol, sq_dst):
            p.wait(dv, TE, mm_mark[m % 2])
            evac_mark[0] = p.inc(dv.tensor_copy(pvtb[:, :], PA[:, :]), DV)
            p.wait(dv, AC)
            p.inc(dv.tensor_scalar(t0b[:, :], pvtb[:, :], wcol[:, m, 3:4], bcol[:, m:m + 1],
                                   op0=ALU.mult, op1=ALU.add), DV)
            for j, off in ((2, 1), (1, 2), (0, 3)):
                p.inc(dv.scalar_tensor_tensor(t0b[:, off:T], pvtb[:, 0:T - off],
                                              wcol[:, m, j:j + 1], t0b[:, off:T],
                                              op0=ALU.mult, op1=ALU.add), DV)
            p.wait(ac, DV)
            p.wait(ac, SP, spill_mark[0])
            p.inc(ac.activation(strip[:, :], t0b[:, :], AF.Silu), AC)
            if sq_dst is not None:
                p.wait(dv, AC)
                p.inc(dv.tensor_tensor(sq16[:, :], strip[:, :], strip[:, :], op=ALU.mult), DV)
                p.wait(te, DV)
                for tt in range(NT):
                    p.inc(te.matmul(sq_dst(tt), sq16[:, tt * 128:(tt + 1) * 128],
                                    ones_col[:], start=True, stop=True), TE)

        PSS = PB[:, 512:1024]

        p.wait(sync, DV); p.wait(sync, AC); p.wait(sync, TE)
        w_load(0, wq_h, 0)
        for m in range(HK):
            w_cast(m % 2)
            if m + 1 < HK:
                w_load((m + 1) % 2, wq_h, m + 1)
            proj_mms(m % 2, PA)
            conv_silu(m, qcw, qcb,
                      lambda tt, m=m: PSS[:, tt * 16 + m:tt * 16 + m + 1])
            p.wait(ac, AC)
            spill_mark[0] = p.inc(ac.dma_start(qT_sp[m * 128:(m + 1) * 128, :], strip[:, :]),
                                  SP, 16)
        peek_out("q15", strip[:, 0:T])

        endphase("P4")
        w_load(0, wk_h, 0)
        for m in range(HK):
            w_cast(m % 2)
            if m + 1 < HK:
                w_load((m + 1) % 2, wk_h, m + 1)
            proj_mms(m % 2, PA)
            conv_silu(m, kcw, kcb,
                      lambda tt, m=m: PSS[:, 256 + tt * 16 + m:256 + tt * 16 + m + 1])
            p.wait(ac, AC)
            spill_mark[0] = p.inc(ac.dma_start(kT_sp[m * 128:(m + 1) * 128, :], strip[:, :]),
                                  SP, 16)
        endphase("P5")
        ts0 = stgf[0][:, 0:512]
        ts1 = stgf[0][:, 512:1024]
        p.wait(dv, TE)
        p.wait(dv, SP, spill_mark[0])
        p.inc(dv.tensor_scalar_max(ts0, PSS[:], 1e-24), DV)
        p.wait(ac, DV)
        p.inc(ac.activation(ts1, ts0, AF.Ln), AC)
        p.wait(ac, AC)
        p.inc(ac.activation(ts0, ts1, AF.Exp, scale=fcn[:, 1:2]), AC)
        p.wait(dv, AC)
        p.inc(dv.tensor_copy(rq_tok, ts0[:, 0:256]), DV)
        p.inc(dv.tensor_copy(rk_tok, ts0[:, 256:512]), DV)
        PT8 = PB[:, 0:160]
        p.wait(te, DV)
        for tt in range(NT):
            for grp, gn in GA:
                gr = grows(grp)
                p.inc(te.transpose(PT8[0:8, 32:160], rk_tok[:, tt * 16 + grp * 8:
                                                            tt * 16 + grp * 8 + 8],
                                   idn32[:]), TE)
                p.wait(dv, TE)
                p.inc(dv.tensor_copy(rkTv[gr, tt * 128:(tt + 1) * 128], PT8[0:8, 32:160]), DV)
                p.wait(te, DV)
                p.inc(te.transpose(PT8[0:8, 32:160], rq_tok[:, tt * 16 + grp * 8:
                                                            tt * 16 + grp * 8 + 8],
                                   idn32[:]), TE)
                p.wait(dv, TE)
                p.inc(dv.tensor_copy(rqTv[gr, tt * 128:(tt + 1) * 128], PT8[0:8, 32:160]), DV)
                p.wait(te, DV)
        for tt in range(NT):
            for grp, gn in GA:
                gr = grows(grp)
                cs = tt * 16 + grp * 8
                p.inc(te.transpose(PT8[0:128, 32:40], bTv[gr, tt * 128:(tt + 1) * 128],
                                   idn32[gr, grp * 64:grp * 64 + 8]), TE)
                p.wait(dv, TE)
                p.inc(dv.tensor_copy(b_tok[:, cs:cs + 8], PT8[0:128, 32:40]), DV)
                p.wait(te, DV)
        for grp, gn in GA:
            gr = grows(grp)
            lt = LTv[gr, :]
            p.wait(ac, DV); p.wait(ac, TE)
            for c in range(NCH):
                mid = c * CH + CH // 2 - 1
                p.inc(ac.activation(TMP5[gr, c * CH:(c + 1) * CH], lt[:, c * CH:(c + 1) * CH],
                                    AF.Exp, scale=fcn[gr, 2:3], bias=lt[:, mid:mid + 1]), AC)
            p.wait(dv, AC)
            p.inc(dv.tensor_tensor(TMP5[gr, :], TMP5[gr, :], rkTv[gr, :], op=ALU.mult), DV)
            p.wait(dv, DV)
            p.inc(dv.tensor_copy(md16lrk[gr, :], TMP5[gr, :]), DV)
            p.wait(te, DV)
            for tt in range(NT):
                cs = tt * 16 + grp * 8
                p.inc(te.transpose(PT8[0:128, 32:40], TMP5[gr, tt * 128:(tt + 1) * 128],
                                   idn32[gr, grp * 64:grp * 64 + 8]), TE)
                p.wait(dv, TE)
                p.inc(dv.tensor_copy(lrk_tok[:, cs:cs + 8], PT8[0:128, 32:40]), DV)
                p.wait(te, DV)
            for c in range(NCH):
                mid = c * CH + CH // 2 - 1
                p.wait(dv, AC)
                p.inc(dv.tensor_scalar_mul(colt[gr, 38:39], lt[:, mid:mid + 1], -1.0), DV)
                p.wait(dv, DV)
                p.wait(ac, DV)
                p.inc(ac.activation(TMP2[gr, c * CH:(c + 1) * CH], lt[:, c * CH:(c + 1) * CH],
                                    AF.Exp, bias=colt[gr, 38:39]), AC)
            p.wait(dv, AC)
            p.inc(dv.tensor_tensor(TMP5[gr, :], TMP2[gr, :], bTv[gr, :], op=ALU.mult), DV)
            p.wait(dv, DV)
            p.inc(dv.tensor_tensor(TMP5[gr, :], TMP5[gr, :], rkTv[gr, :], op=ALU.mult), DV)
            p.wait(dv, DV)
            p.inc(dv.tensor_copy(md16bt[gr, :], TMP5[gr, :]), DV)
            p.wait(te, DV)
            for tt in range(NT):
                cs = tt * 16 + grp * 8
                p.inc(te.transpose(PT8[0:128, 32:40], TMP5[gr, tt * 128:(tt + 1) * 128],
                                   idn32[gr, grp * 64:grp * 64 + 8]), TE)
                p.wait(dv, TE)
                p.inc(dv.tensor_scalar_mul(nbt_tok[:, cs:cs + 8], PT8[0:128, 32:40], -1.0), DV)
                p.wait(te, DV)
            p.inc(dv.tensor_tensor(TMP2[gr, :], TMP2[gr, :], rqTv[gr, :], op=ALU.mult), DV)
            p.wait(dv, DV)
            p.wait(te, DV)
            for c in range(NCH):
                p.inc(te.transpose(PT8[0:64, 32:40],
                                   TMP2[gr, c * CH:(c + 1) * CH],
                                   idn32[gr, grp * 64:grp * 64 + 8]), TE)
                p.wait(dv, TE)
                dst = lq_dup[0:64, :].rearrange("p (h c) -> p h c", c=32)[
                    :, grp * 8:grp * 8 + 8, c]
                p.inc(dv.tensor_copy(dst, PT8[0:64, 32:40]), DV)
                p.wait(te, DV)
        for grp, gn in GA:
            gr = grows(grp)
            lt = LTv[gr, :]
            lc = lt.rearrange("p (c j) -> p c j", j=CH)[:, :, CH - 1]
            lm_ = lt.rearrange("p (c j) -> p c j", j=CH)[:, :, CH // 2 - 1]
            p.inc(dv.tensor_tensor(colt[gr, 0:32], lc, lm_, op=ALU.subtract), DV)
            p.inc(dv.memset(colt[gr, 32:36], 0.0), DV)
            nxt = lt[:, CH + CH // 2 - 1:T:CH]
            p.inc(dv.tensor_tensor(colt[gr, 0:31], colt[gr, 0:31], nxt, op=ALU.add), DV)
            p.wait(ac, DV)
            p.inc(ac.activation(colt[gr, 0:32], colt[gr, 0:32], AF.Exp), AC)
            p.wait(dv, AC)
            p.inc(dv.tensor_copy(sq16[gr, 0:32], colt[gr, 0:32]), DV)
            p.wait(te, DV)
            for h in range(8):
                p.inc(te.matmul(PB[0:1, grp * 256 + h * 32:grp * 256 + (h + 1) * 32],
                                i8c[gr, h:h + 1], sq16[gr, 0:32], start=True, stop=True), TE)
        p.wait(dv, TE)
        p.inc(dv.tensor_copy(rowb[0:1, 0:512], PB[0:1, 0:512]), DV)
        p.wait(te, DV)
        for ng in range(4):
            ins = te.matmul(PB[:, 512 + ng * 128:512 + (ng + 1) * 128], ones_row[0:1, :],
                            rowb[0:1, ng * 128:(ng + 1) * 128], start=True, stop=True)
        p.inc(ins, TE)
        p.wait(dv, TE)
        p.inc(dv.tensor_copy(scal[:, :], PB[:, 512:1024]), DV)
        peek_out("scal", scal[:, 0:512])
        peek_out("lrkA", md16lrk[0:8, 0:T])
        peek_out("btA", md16bt[0:8, 0:T])
        peek_out("lqdup", lq_dup[:, 0:512])
        peek_out("btok", tokmd[:, 0:1280])

        endphase("P6")
        PTRK = PA[:, 0:1024].bitcast(BF16)
        p.wait(dv, DV)
        for m in range(HK):
            p.wait(sync, DV); p.wait(sync, AC); p.wait(sync, SP)
            wld = p.inc(sync.dma_start(strip[:], kT_sp[m * 128:(m + 1) * 128, :]), LD, 16)
            p.wait(te, LD, wld)
            p.wait(te, DV); p.wait(te, AC)
            for tt in range(NT):
                ins = te.transpose(PTRK[:, tt * 128:(tt + 1) * 128],
                                   strip[:, tt * 128:(tt + 1) * 128], idn16[:])
            p.inc(ins, TE)
            p.wait(dv, TE); p.wait(ac, TE)
            for tt in range(NT):
                src = PTRK[:, tt * 128:(tt + 1) * 128]
                cK = lrk_tok[:, tt * 16 + m:tt * 16 + m + 1]
                cB = nbt_tok[:, tt * 16 + m:tt * 16 + m + 1]
                if tt < 8:
                    p.inc(dv.tensor_scalar_mul(stg16[0][:, tt * 128:(tt + 1) * 128], src, cK), DV)
                    p.inc(dv.tensor_scalar_mul(stg16[1][:, tt * 128:(tt + 1) * 128], src, cB), DV)
                else:
                    p.inc(ac.activation(stg16[0][:, tt * 128:(tt + 1) * 128], src,
                                        AF.Copy, scale=cK), AC)
                    p.inc(ac.activation(stg16[1][:, tt * 128:(tt + 1) * 128], src,
                                        AF.Copy, scale=cB), AC)
            p.wait(ac, DV); p.wait(ac, AC)
            km_dst = km_sp[m * T:(m + 1) * T, :].rearrange("(tt p) k -> p tt k", p=128)
            nbk_dst = nbk_sp[m * T:(m + 1) * T, :].rearrange("(tt p) k -> p tt k", p=128)
            p.wait(ac, AC)
            p.inc(ac.dma_start(km_dst, stg16[0][:].rearrange("p (tt k) -> p tt k", k=128)),
                  SP, 16)
            spill_mark[0] = p.inc(
                ac.dma_start(nbk_dst, stg16[1][:].rearrange("p (tt k) -> p tt k", k=128)),
                SP, 16)
            p.wait(dv, SP); p.wait(ac, SP)

        endphase("P7")
        PTRK8 = PB[:, 0:1024].bitcast(BF16)
        tr_mark = [0]
        p.wait(sync, DV); p.wait(sync, AC); p.wait(sync, TE)
        w_load(0, wv_h, 0)
        for m in range(NVT):
            w_cast(m % 2)
            if m + 1 < NVT:
                w_load((m + 1) % 2, wv_h, m + 1)
            proj_mms(m % 2, PA)
            h, vh = m // 2, m % 2
            p.wait(dv, TE, mm_mark[m % 2])
            evac_mark[0] = p.inc(dv.tensor_copy(pvtb[:, :], PA[:, :]), DV)
            p.wait(dv, AC)
            p.inc(dv.tensor_scalar(t0b[:, :], pvtb[:, :], vcw[:, m, 3:4], vcb[:, m:m + 1],
                                   op0=ALU.mult, op1=ALU.add), DV)
            for j, off in ((2, 1), (1, 2), (0, 3)):
                p.inc(dv.scalar_tensor_tensor(t0b[:, off:T], pvtb[:, 0:T - off],
                                              vcw[:, m, j:j + 1], t0b[:, off:T],
                                              op0=ALU.mult, op1=ALU.add), DV)
            p.wait(ac, DV)
            p.wait(ac, TE, tr_mark[0])
            p.inc(ac.activation(strip[:, :], t0b[:, :], AF.Silu), AC)
            p.wait(te, AC)
            p.wait(te, DV)
            for tt in range(NT):
                ins = te.transpose(PTRK8[:, tt * 128:(tt + 1) * 128],
                                   strip[:, tt * 128:(tt + 1) * 128], idn16[:])
            tr_mark[0] = p.inc(ins, TE)
            p.wait(dv, TE); p.wait(ac, TE)
            if vh == 0:
                p.wait(dv, SP, spill_mark[0]); p.wait(ac, SP, spill_mark[0])
            for tt in range(NT):
                src_ = PTRK8[:, tt * 128:(tt + 1) * 128]
                cB = b_tok[:, tt * 16 + h:tt * 16 + h + 1]
                dst = stg16v[vh][:, tt * 128:(tt + 1) * 128]
                if tt < 8:
                    p.inc(dv.tensor_scalar_mul(dst, src_, cB), DV)
                else:
                    p.inc(ac.activation(dst, src_, AF.Copy, scale=cB), AC)
            if vh == 1:
                p.wait(ac, DV); p.wait(ac, AC)
                dst0 = bv_sp[h * T:(h + 1) * T, 0:128].rearrange("(tt p) k -> p tt k", p=128)
                dst1 = bv_sp[h * T:(h + 1) * T, 128:256].rearrange("(tt p) k -> p tt k", p=128)
                p.inc(ac.dma_start(dst0, stg16v[0][:, :].rearrange("p (tt k) -> p tt k", k=128)),
                      SP, 16)
                spill_mark[0] = p.inc(
                    ac.dma_start(dst1, stg16v[1][:, :].rearrange("p (tt k) -> p tt k", k=128)),
                    SP, 16)
        p.wait(dv, SP); p.wait(ac, SP)

        dwsrcs = ([(wd_h, wd_bf, i) for i in range(E * 8)]
                  + [(shd_h, shd_bf, i) for i in range(16)]
                  + [(wo_h, wo_bf, i) for i in range(32)])
        dw_dma = [0, 0]
        dw_cast = [0, 0]
        dw_spill = [0, 0]

        def dw_stg(par):
            return (rkTv if par == 0 else rqTv)

        def dw_stg16(par):
            return AR1[:, 53248 + par * 2048:53248 + (par + 1) * 2048]

        def dw_load(i):
            par = i % 2
            src_h, _, mi = dwsrcs[i]
            p.wait(sync, DV, dw_cast[par])
            dw_dma[par] = p.inc(sync.dma_start(dw_stg(par)[:, :], src_h[mi]), LD, 16)

        def dw_consume(i):
            par = i % 2
            _, dst_sp, mi = dwsrcs[i]
            p.wait(dv, LD, dw_dma[par])
            p.wait(dv, SP, dw_spill[par])
            dw_cast[par] = p.inc(dv.tensor_copy(dw_stg16(par)[:, :], dw_stg(par)[:, :]), DV)
            p.wait(ac, DV, dw_cast[par])
            dw_spill[par] = p.inc(
                ac.dma_start(dst_sp[mi * 128:(mi + 1) * 128, :], dw_stg16(par)[:, :]), SP, 16)

        endphase("P8")
        s1b, x3b, htb = WK1, WK2, WK3
        x1_mark = [0]
        x3_mark = [0]

        p.wait(sync, DV); p.wait(sync, AC); p.wait(sync, TE)
        w_load(0, wg_h.rearrange("e m p d -> (e m) p d"), 0)
        wgf = wg_h.rearrange("e m p d -> (e m) p d")
        wuf = wu_h.rearrange("e m p d -> (e m) p d")
        for e in range(E):
            p.wait(te, DV)
            p.wait(te, AC)
            for ng in range(4):
                ins = te.matmul(PB[0:1, ng * 512:(ng + 1) * 512], i8c[0:8, e:e + 1],
                                combT[0:8, ng * 512:(ng + 1) * 512], start=True, stop=True)
            p.inc(ins, TE)
            p.wait(dv, TE)
            p.inc(dv.tensor_copy(rowsg[0:1, :], PB[0:1, 0:2048]), DV)
            p.wait(dv, DV)
            p.wait(te, DV)
            for ng in range(4):
                ins = te.matmul(PA[:, ng * 512:(ng + 1) * 512], ones_row[0:1, :],
                                rowsg[0:1, ng * 512:(ng + 1) * 512], start=True, stop=True)
            p.inc(ins, TE)
            p.wait(dv, TE)
            evac_mark[0] = p.inc(dv.tensor_copy(combbc[:, :], PA[:, :]), DV)
            for m in range(8):
                gi = e * 8 + m
                w_cast(0)
                if gi + 1 < 64:
                    p.wait(sync, DV, cast_mark[0])
                    dma_mark[0] = p.inc(sync.dma_start(stgf[0][:, :], wgf[gi + 1]), LD, 16)
                p.wait(te, DV, max(cast_mark[0], evac_mark[0]))
                p.wait(te, AC, x1_mark[0])
                for kd in range(NKD):
                    for ng in range(4):
                        ins = te.matmul(PA[:, ng * 512:(ng + 1) * 512],
                                        stg16[0][:, kd * 128:(kd + 1) * 128],
                                        XNT[:, kd * T + ng * 512:kd * T + (ng + 1) * 512],
                                        start=(kd == 0), stop=(kd == NKD - 1))
                mma = mm_mark[0] = p.inc(ins, TE)
                if gi == 0:
                    p.wait(sync, DV, cast_mark[1])
                    dma_mark[1] = p.inc(sync.dma_start(stgf[1][:, :], wuf[gi]), LD, 16)
                w_cast(1)
                if gi + 1 < 64:
                    p.wait(sync, DV, cast_mark[1])
                    dma_mark[1] = p.inc(sync.dma_start(stgf[1][:, :], wuf[gi + 1]), LD, 16)
                p.wait(te, DV, max(cast_mark[1], x3_mark[0]))
                for kd in range(NKD):
                    for ng in range(4):
                        ins = te.matmul(PB[:, ng * 512:(ng + 1) * 512],
                                        stg16[1][:, kd * 128:(kd + 1) * 128],
                                        XNT[:, kd * T + ng * 512:kd * T + (ng + 1) * 512],
                                        start=(kd == 0), stop=(kd == NKD - 1))
                mmb = mm_mark[1] = p.inc(ins, TE)
                p.wait(ac, TE, mma)
                p.wait(ac, SP, spill_mark[0])
                x1_mark[0] = p.inc(ac.activation(s1b[:, :], PA[:, :], AF.Silu), AC)
                p.wait(dv, TE, mmb)
                p.wait(dv, SP, spill_mark[0])
                x3_mark[0] = p.inc(dv.tensor_tensor(x3b[:, :], PB[:, :], combbc[:, :],
                                                    op=ALU.mult), DV)
                p.wait(dv, AC, x1_mark[0])
                p.inc(dv.tensor_tensor(htb[:, :], s1b[:, :], x3b[:, :], op=ALU.mult), DV)
                p.wait(ac, DV)
                p.wait(ac, AC)
                spill_mark[0] = p.inc(
                    ac.dma_start(ht_sp[gi * 128:(gi + 1) * 128, :], htb[:, :]), SP, 16)
        peek_out("ht63", htb[:, 0:T])

        endphase("P9")
        p.wait(sync, DV, cast_mark[0])
        dma_mark[0] = p.inc(sync.dma_start(stgf[0][:, :], shg_h[0]), LD, 16)
        for m in range(16):
            w_cast(0)
            if m + 1 < 16:
                p.wait(sync, DV, cast_mark[0])
                dma_mark[0] = p.inc(sync.dma_start(stgf[0][:, :], shg_h[m + 1]), LD, 16)
            p.wait(te, DV, max(cast_mark[0], evac_mark[0]))
            p.wait(te, AC, x1_mark[0])
            for kd in range(NKD):
                for ng in range(4):
                    ins = te.matmul(PA[:, ng * 512:(ng + 1) * 512],
                                    stg16[0][:, kd * 128:(kd + 1) * 128],
                                    XNT[:, kd * T + ng * 512:kd * T + (ng + 1) * 512],
                                    start=(kd == 0), stop=(kd == NKD - 1))
            mma = mm_mark[0] = p.inc(ins, TE)
            if m == 0:
                p.wait(sync, DV, cast_mark[1])
                dma_mark[1] = p.inc(sync.dma_start(stgf[1][:, :], shu_h[0]), LD, 16)
            w_cast(1)
            if m + 1 < 16:
                p.wait(sync, DV, cast_mark[1])
                dma_mark[1] = p.inc(sync.dma_start(stgf[1][:, :], shu_h[m + 1]), LD, 16)
            p.wait(te, DV, max(cast_mark[1], x3_mark[0]))
            for kd in range(NKD):
                for ng in range(4):
                    ins = te.matmul(PB[:, ng * 512:(ng + 1) * 512],
                                    stg16[1][:, kd * 128:(kd + 1) * 128],
                                    XNT[:, kd * T + ng * 512:kd * T + (ng + 1) * 512],
                                    start=(kd == 0), stop=(kd == NKD - 1))
            mmb = mm_mark[1] = p.inc(ins, TE)
            p.wait(ac, TE, mma)
            p.wait(ac, SP, spill_mark[0])
            x1_mark[0] = p.inc(ac.activation(s1b[:, :], PA[:, :], AF.Silu), AC)
            p.wait(dv, TE, mmb)
            p.wait(dv, SP, spill_mark[0])
            x3_mark[0] = p.inc(dv.tensor_tensor(x3b[:, :], PB[:, :], sgbc[:, :],
                                                op=ALU.mult), DV)
            p.wait(dv, AC, x1_mark[0])
            p.inc(dv.tensor_tensor(htb[:, :], s1b[:, :], x3b[:, :], op=ALU.mult), DV)
            p.wait(ac, DV)
            p.wait(ac, AC)
            spill_mark[0] = p.inc(
                ac.dma_start(s_sp[m * 128:(m + 1) * 128, :], htb[:, :]), SP, 16)

        endphase("P10")
        dw_load(0)
        for i in range(len(dwsrcs)):
            if i + 1 < len(dwsrcs):
                dw_load(i + 1)
            dw_consume(i)
        p.wait(dv, SP); p.wait(ac, SP)
        endphase("P105")
        p.wait(dv, DV)
        p.inc(dv.memset(SR[:, :], 0.0), DV)
        p.inc(dv.memset(SR16[:, :], 0.0), DV)
        PD = PA[0:64, 0:1024]
        PKQ = PA[0:64, 1024:2048]
        PLAM = PB[0:64, 0:1024]
        PROW = PB[0:1, 1024:2048]
        PBB = PB[0:64, 0:1024]
        PTR2 = PB[0:64, 0:512].bitcast(BF16)
        PSQ = PB[0:64, 0:2048]
        PCB = PB[0:64, 0:1024]
        PG = PA[:, 0:1024]
        PUW = [PA[0:64, 0:2048], PB[0:64, 0:2048]]
        POW = [PA[0:64, 0:2048], PB[0:64, 0:2048]]
        PDLT = PA[:, 0:2048]
        POTR = PB[:, 0:512].bitcast(BF16)

        qk_mark = [0, 0]
        ch_mark = [0] * NCH

        def pref_quad(qd_):
            bq = qd_ % 2
            src_q = qT_sp.rearrange("(h p) t -> h p t", p=128)[:, :, qd_ * 256:(qd_ + 1) * 256]
            src_k = kT_sp.rearrange("(h p) t -> h p t", p=128)[:, :, qd_ * 256:(qd_ + 1) * 256]
            p.inc(sync.dma_start(qbuf[bq].rearrange("p (h t) -> p h t", t=256),
                                 src_q.rearrange("h p t -> p h t")), LD, 16)
            qk_mark[bq] = p.inc(
                sync.dma_start(kbuf[bq].rearrange("p (h t) -> p h t", t=256),
                               src_k.rearrange("h p t -> p h t")), LD, 16)

        def pref_chunk(c_):
            bcn = c_ % 2
            src = bv_sp.rearrange("(h t) k -> h t k", t=T)[:, c_ * CH:(c_ + 1) * CH, :]
            p.inc(sync.dma_start(
                bvb[bcn].rearrange("p (h k) -> p h k", k=VE),
                src.rearrange("h t k -> t h k")), LD, 16)
            src = km_sp.rearrange("(h t) k -> h t k", t=T)[:, c_ * CH:(c_ + 1) * CH, :]
            p.inc(sync.dma_start(
                kmb[bcn].rearrange("p (h k) -> p h k", k=DK),
                src.rearrange("h t k -> t h k")), LD, 16)
            src = nbk_sp.rearrange("(h t) k -> h t k", t=T)[:, c_ * CH:(c_ + 1) * CH, :]
            ch_mark[c_] = p.inc(sync.dma_start(
                nbkb[bcn].rearrange("p (h k) -> p h k", k=DK),
                src.rearrange("h t k -> t h k")), LD, 16)

        pref_quad(0)
        pref_chunk(0)
        pref_chunk(1)
        id64 = idn16[0:64, 0:64]

        for c in range(NCH):
            qd_, cq = c // 4, c % 4
            bcn = c % 2
            qb, kb = qbuf[qd_ % 2], kbuf[qd_ % 2]

            def ksl(h, cq=cq, kb=kb):
                return kb[:, h * 256 + cq * 64:h * 256 + cq * 64 + 64]

            def qsl(h, cq=cq, qb=qb):
                return qb[:, h * 256 + cq * 64:h * 256 + cq * 64 + 64]

            if cq == 3 and qd_ + 1 < 8:
                p.wait(sync, TE)
                pref_quad(qd_ + 1)
            p.wait(te, LD, max(qk_mark[qd_ % 2], ch_mark[c]))

            for grp, gn in GA:
                gr = grows(grp)
                p.inc(te.matmul(PLAM[:, grp * 512:(grp + 1) * 512],
                                md16lrk[gr, c * CH:(c + 1) * CH],
                                exp64[gr, :], start=True, stop=True), TE)
                for h in range(8):
                    p.inc(te.matmul(PROW[0:1, grp * 512 + h * 64:grp * 512 + (h + 1) * 64],
                                    i8c[gr, h:h + 1], md16bt[gr, c * CH:(c + 1) * CH],
                                    start=True, stop=True), TE)
            p.wait(dv, TE)
            p.inc(dv.tensor_tensor(lstr[:, :], PLAM, mstr[:], op=ALU.mult), DV)
            p.inc(dv.tensor_tensor(lincl[:, :], PLAM, mincl[:], op=ALU.mult), DV)
            p.inc(dv.tensor_copy(rowb[0:1, 0:1024], PROW), DV)
            p.wait(dv, DV)
            p.wait(te, DV)
            for half in range(2):
                p.inc(te.matmul(PBB[:, half * 512:(half + 1) * 512], ones_row[0:1, 0:64],
                                rowb[0:1, half * 512:(half + 1) * 512],
                                start=True, stop=True), TE)
            p.wait(ac, TE)
            p.inc(ac.activation(bbs[:, :], PBB, AF.Copy), AC)
            p.wait(te, AC)
            for h in range(HK):
                p.inc(te.matmul(PD[:, h * 64:(h + 1) * 64], ksl(h), ksl(h),
                                start=True, stop=True), TE)
                p.inc(te.matmul(PKQ[:, h * 64:(h + 1) * 64], ksl(h), qsl(h),
                                start=True, stop=True), TE)
            p.wait(dv, TE)
            p.inc(dv.tensor_tensor(Pcb[:, :], PD, lstr[:, :], op=ALU.mult), DV)
            p.wait(dv, DV)
            p.inc(dv.tensor_tensor(Pcb[:, :], Pcb[:, :], bbs[:, :], op=ALU.mult), DV)
            p.inc(dv.tensor_tensor(ATs[:, :], PKQ, lincl[:, :], op=ALU.mult), DV)
            if c == 0:
                peek_out("negMT0", Pcb[:, 0:1024])
            p.wait(te, DV)
            for h in range(HK):
                ins = te.transpose(PTR2[:, h * 64:(h + 1) * 64],
                                   Pcb[:, h * 64:(h + 1) * 64], id64)
            p.inc(ins, TE)
            p.wait(dv, TE)
            p.inc(dv.tensor_copy(PTcb[:, :], PTR2), DV)
            p.inc(dv.tensor_tensor(Tcb[:, :], Pcb[:, :], idup[:], op=ALU.add), DV)
            for lvl in range(5):
                p.wait(te, DV)
                for h in range(HK):
                    sl = slice(h * 64, (h + 1) * 64)
                    p.inc(te.matmul(PSQ[:, h * 128:h * 128 + 64], PTcb[:, sl],
                                    Pcb[:, sl], start=True, stop=True), TE)
                    p.inc(te.matmul(PSQ[:, h * 128 + 64:h * 128 + 128], Pcb[:, sl],
                                    PTcb[:, sl], start=True, stop=True), TE)
                p.wait(dv, TE)
                psq = PSQ.rearrange("p (h two j) -> p h two j", two=2, j=64)
                p.inc(dv.tensor_copy(
                    Pcb.rearrange("p (h j) -> p h j", j=64), psq[:, :, 0, :]), DV)
                p.inc(dv.tensor_copy(
                    PTcb.rearrange("p (h j) -> p h j", j=64), psq[:, :, 1, :]), DV)
                p.wait(te, DV)
                for h in range(HK):
                    sl = slice(h * 64, (h + 1) * 64)
                    p.inc(te.matmul(PCB[:, sl], PTcb[:, sl], Tcb[:, sl],
                                    start=True, stop=True), TE)
                p.wait(dv, TE)
                p.wait(dv, DV)
                p.inc(dv.tensor_tensor(Tcb[:, :], Tcb[:, :], PCB, op=ALU.add), DV)
            p.wait(te, DV)
            for h in range(HK):
                p.inc(te.matmul(PG[:, h * 64:(h + 1) * 64],
                                nbkb[bcn][:, h * 128:(h + 1) * 128],
                                Tcb[:, h * 64:(h + 1) * 64], start=True, stop=True), TE)
            p.wait(ac, TE)
            p.inc(ac.activation(Gts[:, :], PG, AF.Copy), AC)
            p.wait(te, AC)
            for wv in range(2):
                for hh in range(8):
                    h = wv * 8 + hh
                    p.inc(te.matmul(PUW[wv][:, hh * 256:(hh + 1) * 256],
                                    Tcb[:, h * 64:(h + 1) * 64],
                                    bvb[bcn][:, h * 256:(h + 1) * 256],
                                    start=True, stop=False), TE)
                    p.inc(te.matmul(PUW[wv][:, hh * 256:(hh + 1) * 256],
                                    Gts[:, h * 64:(h + 1) * 64],
                                    SR16[:, h * 256:(h + 1) * 256],
                                    start=False, stop=True), TE)
                p.wait(dv, TE)
                p.inc(dv.tensor_copy(Us[:, wv * 2048:(wv + 1) * 2048], PUW[wv]), DV)
            for wv in range(2):
                p.wait(te, DV)
                for hh in range(8):
                    h = wv * 8 + hh
                    p.inc(te.matmul(POW[wv][:, hh * 256:(hh + 1) * 256], qsl(h),
                                    SR16[:, h * 256:(h + 1) * 256], start=True, stop=False), TE)
                    p.inc(te.matmul(POW[wv][:, hh * 256:(hh + 1) * 256],
                                    ATs[:, h * 64:(h + 1) * 64],
                                    Us[:, h * 256:(h + 1) * 256],
                                    start=False, stop=True), TE)
                p.wait(dv, TE); p.wait(ac, TE)
                for hh in range(8):
                    h = wv * 8 + hh
                    colap = lq_dup[0:64, h * 32 + c:h * 32 + c + 1]
                    dst = obuf[:, h * 256:(h + 1) * 256]
                    if hh < 4:
                        p.inc(dv.tensor_scalar_mul(dst, POW[wv][:, hh * 256:(hh + 1) * 256],
                                                   colap), DV)
                    else:
                        p.inc(ac.activation(dst, POW[wv][:, hh * 256:(hh + 1) * 256],
                                            AF.Copy, scale=colap), AC)
            tq0 = (c % 4) * 64
            p.wait(te, DV); p.wait(te, AC)
            for rnd in range(2):
                for s in range(16):
                    vt = rnd * 16 + s
                    h, vh = vt // 2, vt % 2
                    ins = te.transpose(POTR[:, s * 64:(s + 1) * 64],
                                       obuf[:, h * 256 + vh * 128:h * 256 + (vh + 1) * 128],
                                       id64)
                p.inc(ins, TE)
                p.wait(dv, TE)
                dst = OTacc.rearrange("p (vt q) -> p vt q", q=OT_QT)[
                    :, rnd * 16:(rnd + 1) * 16, tq0:tq0 + 64]
                p.inc(dv.tensor_copy(
                    dst, POTR.rearrange("p (vt j) -> p vt j", j=64)[:, 0:16, :]), DV)
                p.wait(te, DV)
            for wv in range(2):
                p.wait(te, DV)
                for hh in range(8):
                    h = wv * 8 + hh
                    p.inc(te.matmul(PDLT[:, hh * 256:(hh + 1) * 256],
                                    kmb[bcn][:, h * 128:(h + 1) * 128],
                                    Us[:, h * 256:(h + 1) * 256],
                                    start=True, stop=True), TE)
                p.wait(dv, TE)
                p.inc(dv.tensor_tensor(SR[:, wv * 2048:(wv + 1) * 2048],
                                       SR[:, wv * 2048:(wv + 1) * 2048], PDLT,
                                       op=ALU.add), DV)
            p.wait(dv, DV)
            for h in range(HK):
                scol = scal[:, h * 32 + c:h * 32 + c + 1]
                p.inc(dv.tensor_scalar_mul(SR[:, h * 256:(h + 1) * 256],
                                           SR[:, h * 256:(h + 1) * 256], scol), DV)
            p.wait(dv, DV)
            p.wait(ac, DV)
            p.inc(ac.activation(SR16[:, :], SR[:, :], AF.Copy), AC)
            if c + 2 < NCH:
                p.wait(sync, TE)
                pref_chunk(c + 2)
            if c % 4 == 3:
                qd4 = c // 4
                p.wait(ac, DV)
                p.wait(ac, AC)
                spill_mark[0] = p.inc(ac.dma_start(
                    ot_sp.rearrange("(b vt p) q -> b p vt q", vt=32, p=128)[qd4],
                    OTacc.rearrange("p (vt q) -> p vt q", q=OT_QT)), SP, 16)
                if c + 1 < NCH:
                    p.wait(dv, SP)
            if c == 0:
                peek_out("TT0", Tcb[:, 0:1024])
                peek_out("U0", Us[:, 0:4096])
                peek_out("O0", obuf[:, 0:4096])
                peek_out("AT0", ATs[:, 0:1024])
        p.wait(dv, AC)
        peek_out("SR", SR[:, 0:4096])

        endphase("P11")
        nsrc = 112
        srcs = ([(wo_bf, vt) for vt in range(32)] + [(wd_bf, em) for em in range(64)]
                + [(shd_bf, m) for m in range(16)])
        NWB = 4
        wmark = [0] * NWB

        def wload(gi):
            dq_, si_ = gi // nsrc, gi % nsrc
            src, idx = srcs[si_]
            wmark[gi % NWB] = p.inc(sync.dma_start(
                wdn[gi % NWB][:, :], src[idx * 128:(idx + 1) * 128, dq_ * QT:(dq_ + 1) * QT]),
                LD, 16)

        for qd_ in range(QD):
            p.wait(sync, TE); p.wait(sync, SP)
            for hb in range(2):
                p.inc(sync.dma_start(
                    ACTS.rearrange("p (f t) -> p f t", t=QT)[:, 0:32, hb * 256:(hb + 1) * 256],
                    ot_sp.rearrange("(b vt p) q -> b p vt q", vt=32, p=128)[qd_ * 2 + hb]),
                    LD, 16)
            for h8 in range(4):
                p.inc(sync.dma_start(
                    ACTS.rearrange("p (f t) -> p f t", t=QT)[:, 32 + h8 * 16:32 + (h8 + 1) * 16, :],
                    ht_sp.rearrange("(f p) t -> p f t", p=128)[
                        :, h8 * 16:(h8 + 1) * 16, qd_ * QT:(qd_ + 1) * QT]), LD, 16)
            acts_ld = p.inc(sync.dma_start(
                ACTS.rearrange("p (f t) -> p f t", t=QT)[:, 96:112, :],
                s_sp.rearrange("(f p) t -> p f t", p=128)[:, :, qd_ * QT:(qd_ + 1) * QT]),
                LD, 16)
            for gi in range(NWB - 1):
                wload(gi)
            for dq in range(4):
                PQ = (PA if dq % 2 == 0 else PB)
                p.wait(te, DV)
                for si in range(nsrc):
                    gi = dq * nsrc + si
                    if gi + NWB - 1 < 4 * nsrc:
                        p.wait(sync, TE)
                        wload(gi + NWB - 1)
                    p.wait(te, LD, max(wmark[gi % NWB], acts_ld))
                    for sub in range(4):
                        p.inc(te.matmul(
                            PQ[:, sub * 512:(sub + 1) * 512],
                            ACTS[:, si * QT + sub * 128:si * QT + sub * 128 + 128],
                            wdn[gi % NWB][:, :], start=(si == 0), stop=(si == nsrc - 1)), TE)
                for sub in range(4):
                    tglob = qd_ * QT + sub * 128
                    p.wait(sync, DV)
                    wld = p.inc(sync.dma_start(
                        xres[:, :], x_h[tglob:tglob + 128, dq * QT:(dq + 1) * QT]), LD, 16)
                    p.wait(dv, LD, wld)
                    p.wait(dv, TE)
                    p.inc(dv.tensor_tensor(oute[:, :], PQ[:, sub * 512:(sub + 1) * 512],
                                           xres[:, :], op=ALU.add), DV)
                    p.wait(sync, DV)
                    p.inc(sync.dma_start(
                        out_h[tglob:tglob + 128, dq * QT:(dq + 1) * QT], oute[:, :]), LD, 16)

        p.wait(sync, LD)
        p.wait(sync, SP)
    except _Stop:
        sync.wait_ge(p.sem[LD], p.cnt[LD])
        sync.wait_ge(p.sem[SP], p.cnt[SP])
        stack.close()

    return nc


def kernel(**inputs):
    g = host_prep(inputs)
    x = np.asarray(inputs["L0_attn_x"], np.float32)
    nc = build()
    in_maps = []
    for core in range(8):
        m = dict(g)
        m["x"] = np.ascontiguousarray(x[core])
        in_maps.append(m)
    res = run_bass_kernel_spmd(nc, in_maps, core_ids=list(range(8)))
    out = np.stack([res.results[i]["out"] for i in range(8)])
    return out.astype(np.float32)


